# version 15
# baseline (speedup 1.0000x reference)
from contextlib import ExitStack

import numpy as np
import ml_dtypes

import concourse.bass as bass
import concourse.mybir as mybir
from concourse.bass_utils import run_bass_kernel_spmd

F32 = mybir.dt.float32
BF16 = mybir.dt.bfloat16
I8 = mybir.dt.int8
AF = mybir.ActivationFunctionType
ALU = mybir.AluOpType
BF = ml_dtypes.bfloat16

NLA_CLIP = 23.025851
CH = 64
TB = 128

FULL = dict(D=2048, FFN=8192, SEG=1024, NSEG=4, B=2)
MINI = dict(D=512, FFN=2048, SEG=256, NSEG=2, B=2)

DMA_STREAMS = {"dC": 1, "dx": 2, "wa": 2, "wi": 2, "wv": 2, "dwo": 16,
               "wg": 2, "wu": 2, "dn": 4, "dxa": 1, "dx1s": 1, "dcc": 1,
               "dy": 2}


class Prog:

    ENGINES = ("sync", "scalar", "vector", "tensor", "gpsimd")

    def __init__(self, nc):
        self.nc = nc
        self.ops = {e: [] for e in self.ENGINES}
        self.cnt = {}
        self.sems = {}
        self.duse = {}
        self._waited = {e: {} for e in self.ENGINES}

    def make_sems(self, ctx, names):
        for n in names:
            self.sems[n] = ctx.enter_context(self.nc.semaphore("sem_" + n))
            self.cnt[n] = 0

    def make_dma_streams(self, ctx, streams):
        for s, n in streams.items():
            for i in range(n):
                name = s + str(i)
                self.sems[name] = ctx.enter_context(
                    self.nc.semaphore("sem_" + name))
                self.cnt[name] = 0
            self.duse[s] = 0

    def _dedup(self, engine, waits):
        wl = []
        for w in waits:
            if w is None:
                continue
            sname, val = w
            if val is None or val <= 0:
                continue
            if self._waited[engine].get(sname, 0) >= val:
                continue
            self._waited[engine][sname] = val
            wl.append((self.sems[sname], val))
        return wl

    def op(self, engine, method, *args, wait=(), inc=None, inc_k=1, **kwargs):
        wl = self._dedup(engine, list(wait))
        ret = None
        sem = None
        if inc is not None:
            self.cnt[inc] += inc_k
            ret = self.cnt[inc]
            sem = self.sems[inc]

        def go(eng, wl=wl, method=method, args=args, kwargs=kwargs, sem=sem,
               inc_k=inc_k):
            for (s, val) in wl:
                eng.wait_ge(s, val)
            inst = getattr(eng, method)(*args, **kwargs)
            if sem is not None:
                inst.then_inc(sem, inc_k)

        self.ops[engine].append(go)
        return ret

    def wait(self, engine, *waits):
        wl = self._dedup(engine, list(waits))
        if wl:
            def go(eng, wl=wl):
                for (s, val) in wl:
                    eng.wait_ge(s, val)
            self.ops[engine].append(go)

    def vop(self, method, *args, wait=(), **kwargs):
        return self.op("vector", method, *args,
                       wait=[("v", self.cnt["v"])] + list(wait), inc="v", **kwargs)

    def sop(self, method, *args, wait=(), **kwargs):
        return self.op("scalar", method, *args,
                       wait=[("s", self.cnt["s"])] + list(wait), inc="s", **kwargs)

    def gop(self, method, *args, wait=(), **kwargs):
        return self.op("gpsimd", method, *args,
                       wait=[("g", self.cnt["g"])] + list(wait), inc="g", **kwargs)

    def top(self, method, *args, wait=(), inc=False, **kwargs):
        return self.op("tensor", method, *args, wait=wait,
                       inc=("t" if inc else None), **kwargs)

    def dma(self, queue, stream, dst, src, wait=()):
        n = DMA_STREAMS[stream]
        use = self.duse[stream]
        self.duse[stream] = use + 1
        semname = stream + str(use % n)
        self.op(queue, "dma_start", dst, src, wait=wait, inc=semname, inc_k=16)
        return (semname, self.cnt[semname])

    def finalize(self, block):
        for e in self.ENGINES:
            fns = self.ops[e]
            if not fns:
                continue

            def emit(eng, fns=fns):
                for f in fns:
                    f(eng)

            getattr(block, e)(emit)


def build_nc(cfg):
    D, FFN, SEG = cfg["D"], cfg["FFN"], cfg["SEG"]
    KB = D // 128
    NTB = SEG // TB
    NCH = SEG // CH
    OBF = FFN // 128
    NB = D // 512
    NCORES = cfg["B"] * cfg["NSEG"]
    NQ = 4 if OBF % 8 == 0 else 2
    OHQ = OBF // NQ
    KHQ = OHQ
    MMW = min(SEG, 512)
    NMM = SEG // MMW
    eps_ln, eps_rms = 1e-5, 1e-6

    nc = bass.Bass("TRN2", target_bir_lowering=False)

    def din(name, shape, dt=F32):
        return nc.dram_tensor(name, list(shape), dt, kind="ExternalInput").ap()

    x_in = din("x", (SEG, D))
    w_aiv_l = {w: din("w_%s_l" % w, (128, KB, KB * 128)) for w in ("a", "i", "v")}
    w_o_r = din("w_o_r", (D, D))
    w_gu_l = din("w_gu_l", (128, 2 * OBF, KB * 128))
    w_dn_r = din("w_dn_r", (FFN, D))
    wg8_l = din("wg8_l", (128, OBF, KB * 128), I8)
    wu8_l = din("wu8_l", (128, OBF, KB * 128), I8)
    wd8_r = din("wd8_r", (NB, FFN, 512), I8)
    biasT = din("biasT", (128, KB))
    wpreT = din("wpreT", (128, KB))
    wffT = din("wffT", (128, KB))
    sgT = din("sgT", (128, OBF))
    suT = din("suT", (128, OBF))
    sd_row = din("sd_row", (1, D))
    scanmask = din("scanmask", (128, SEG))
    identity = din("identity", (128, 128), BF16)
    ones_row = din("ones_row", (1, 128), BF16)
    rankmask = din("rankmask", (128, NCORES))
    y_out = nc.dram_tensor("y", [SEG, D], F32, kind="ExternalOutput").ap()

    KSPL = 12
    cc_in_a = nc.dram_tensor("cc_in_a", [128, 2 * KSPL], F32)
    cc_out_a = nc.dram_tensor("cc_out_a", [NCORES * 128, 2 * KSPL], F32)
    cc_in_b = nc.dram_tensor("cc_in_b", [128, 2 * (KB - KSPL)], F32)
    cc_out_b = nc.dram_tensor("cc_out_b", [NCORES * 128, 2 * (KB - KSPL)], F32)
    xa_sp = nc.dram_tensor("xa_sp", [SEG, D], F32)
    x1_sp = nc.dram_tensor("x1_sp", [SEG, D], F32)

    with ExitStack() as ctx:
        E = ctx.enter_context
        p = Prog(nc)
        p.make_sems(ctx, ["v", "s", "g", "t", "cc"])
        p.make_dma_streams(ctx, DMA_STREAMS)

        hT = E(nc.sbuf_tensor("hT", [128, KB * SEG], BF16))
        states = E(nc.sbuf_tensor("states", [128, KB * SEG], BF16))
        arena = E(nc.sbuf_tensor("arena", [128, 20480], F32))
        arenaF = arena[:]
        arenaB = arena[:].bitcast(BF16)
        c_bias = E(nc.sbuf_tensor("c_bias", [128, KB], F32))
        c_nbias = E(nc.sbuf_tensor("c_nbias", [128, KB], F32))
        c_wpre = E(nc.sbuf_tensor("c_wpre", [128, KB], F32))
        c_wff = E(nc.sbuf_tensor("c_wff", [128, KB], F32))
        c_sg = E(nc.sbuf_tensor("c_sg", [128, OBF], F32))
        c_su = E(nc.sbuf_tensor("c_su", [128, OBF], F32))
        c_mask = E(nc.sbuf_tensor("c_mask", [128, SEG], F32))
        c_id = E(nc.sbuf_tensor("c_id", [128, 128], BF16))
        c_ones = E(nc.sbuf_tensor("c_ones", [1, 128], BF16))
        c_rmask = E(nc.sbuf_tensor("c_rmask", [128, NCORES], F32))
        sdb = E(nc.sbuf_tensor("sdb", [128, D], F32))
        LP = E(nc.sbuf_tensor("LP", [128, 2 * KB], F32))
        gath = E(nc.sbuf_tensor("gath", [128, NCORES, 2 * KB], F32))
        carry = E(nc.sbuf_tensor("carry", [128, KB], F32))
        cr_t1 = E(nc.sbuf_tensor("cr_t1", [128, NCORES, KB], F32))
        cr_t2 = E(nc.sbuf_tensor("cr_t2", [128, NCORES, KB], F32))
        cr_tmp = E(nc.sbuf_tensor("cr_tmp", [128, KB], F32))
        fix_tmp = E(nc.sbuf_tensor("fix_tmp", [128, SEG], BF16))
        scan_x = E(nc.sbuf_tensor("scan_x", [128, SEG], F32))
        t16b = E(nc.sbuf_tensor("t16b", [128, 6 * (SEG // CH)], F32))
        xa_t = E(nc.sbuf_tensor("xa_t", [128, D], F32))
        stat4b = E(nc.sbuf_tensor("stat4b", [128, 8], F32))
        x0_t = [E(nc.sbuf_tensor("x0_t%d" % i, [128, D], F32)) for i in range(2)]
        stat = E(nc.sbuf_tensor("stat", [128, 8], F32))
        scr_big = E(nc.sbuf_tensor("scr_big", [128, D], F32))
        hb = E(nc.sbuf_tensor("hb", [128, D], BF16))
        t16 = [E(nc.sbuf_tensor("t16_%d" % i, [128, NCH], F32)) for i in range(6)]
        z16 = E(nc.sbuf_tensor("z16", [128, NCH], F32))

        p.dma("sync", "dC", c_bias[:], biasT)
        p.dma("sync", "dC", c_wpre[:], wpreT)
        p.dma("sync", "dC", c_wff[:], wffT)
        p.dma("sync", "dC", c_sg[:], sgT)
        p.dma("sync", "dC", c_su[:], suT)
        p.dma("sync", "dC", c_mask[:], scanmask)
        p.dma("sync", "dC", c_id[:], identity)
        p.dma("sync", "dC", c_ones[:], ones_row)
        p.dma("sync", "dC", c_rmask[:], rankmask)
        ld_sdb = p.dma("sync", "dC", sdb[:],
                       sd_row.broadcast_to([128, D]))
        ldC = ld_sdb
        p.vop("memset", z16[:], 0.0)
        p.vop("tensor_scalar_mul", c_nbias[:], c_bias[:], -1.0, wait=[ldC])

        def stats_full(src_ap, dump_ap, eps, mean_too, wait=(), st=None):
            if st is None:
                st = stat[:]
            if mean_too:
                p.sop("activation", dump_ap, src_ap, AF.Identity,
                      accum_out=st[:, 0:1], wait=list(wait))
                wait = ()
            p.sop("activation", dump_ap, src_ap, AF.Square,
                  accum_out=st[:, 1:2], wait=list(wait))
            sw = [("s", p.cnt["s"])]
            if mean_too:
                p.vop("tensor_scalar_mul", st[:, 2:3], st[:, 0:1], 1.0 / D,
                      wait=sw)
                p.vop("tensor_mul", st[:, 3:4], st[:, 2:3], st[:, 2:3])
                p.vop("tensor_scalar", st[:, 6:7], st[:, 1:2], 1.0 / D,
                      st[:, 3:4], ALU.mult, ALU.subtract)
                vv = p.vop("tensor_scalar_add", st[:, 6:7], st[:, 6:7], eps)
            else:
                vv = p.vop("tensor_scalar", st[:, 6:7], st[:, 1:2], 1.0 / D,
                           eps, ALU.mult, ALU.add, wait=sw)
            p.sop("activation", st[:, 5:6], st[:, 6:7], AF.Ln,
                  wait=[("v", vv)])
            return p.sop("activation", st[:, 4:5], st[:, 5:6], AF.Exp,
                         scale=-0.5)

        with ExitStack() as ph1:
            ps_tr = [ph1.enter_context(nc.psum_tensor("ps_tr%d" % i, [128, 512],
                                                      BF16)) for i in range(2)]
            statesP1 = states[:].bitcast(F32)
            scr1 = [scr_big[:], statesP1[:, 0:2048]]
            stat1 = [stat[:], statesP1[:, 2048:2056]]
            hb1 = [hb[:], states[:][:, 4112:6160]]
            st1 = {}

            def P1_0(m):
                par = m % 2
                prev = st1.get(m - 2)
                w = []
                if prev:
                    w = [("s", prev["s_sq"]), ("v", prev["v_ln1"])]
                st1[m] = dict(ldx=p.dma("sync", "dx", x0_t[par][:],
                                        x_in[m * TB:(m + 1) * TB, :], wait=w))

            def P1_1(m):
                par = m % 2
                prev = st1.get(m - 2)
                w = [st1[m]["ldx"]]
                if prev:
                    w.append(("v", prev["v_hb"]))
                st1[m]["s_r"] = stats_full(x0_t[par][:], scr1[par], eps_ln,
                                           True, wait=w, st=stat1[par])

            def P1_2(m):
                par = m % 2
                st1[m]["v_ln1"] = p.vop("tensor_scalar", scr1[par],
                                        x0_t[par][:], stat1[par][:, 2:3],
                                        stat1[par][:, 4:5], ALU.subtract,
                                        ALU.mult, wait=[("s", st1[m]["s_r"])])

            def P1_3(m):
                par = m % 2
                st1[m]["s_sq"] = p.sop("activation", x0_t[par][:], scr1[par],
                                       AF.Square, accum_out=stat1[par][:, 1:2],
                                       wait=[("v", st1[m]["v_ln1"])])
                vv = p.vop("tensor_scalar", stat1[par][:, 6:7],
                           stat1[par][:, 1:2], 1.0 / D, eps_rms, ALU.mult,
                           ALU.add, wait=[("s", p.cnt["s"])])
                p.sop("activation", stat1[par][:, 5:6], stat1[par][:, 6:7],
                      AF.Ln, wait=[("v", vv)])
                st1[m]["s_r2"] = p.sop("activation", stat1[par][:, 4:5],
                                       stat1[par][:, 5:6], AF.Exp, scale=-0.5)

            def P1_4(m):
                par = m % 2
                prev = st1.get(m - 2)
                w = [("s", st1[m]["s_r2"])]
                if prev:
                    w.append(("t", prev["t_tr"]))
                st1[m]["v_hb"] = p.vop("tensor_scalar_mul", hb1[par],
                                       scr1[par], stat1[par][:, 4:5], wait=w)

            def P1_5(m):
                par = m % 2
                copy_cnt = {}
                for g in range(KB // 4):
                    pp = ps_tr[g % 2]
                    for j in range(4):
                        k = g * 4 + j
                        w = []
                        if j == 0:
                            if g < 2:
                                w = [("v", st1[m]["v_hb"])]
                                if m > 0:
                                    w.append(("v", st1[m - 1]["tr_copies"]))
                            else:
                                w = [("v", copy_cnt[g - 2])]
                        p.top("transpose", pp[:, j * 128:(j + 1) * 128],
                              hb1[par][:, k * 128:(k + 1) * 128], c_id[:],
                              wait=w, inc=(j == 3))
                    wsl = c_wpre[:, g * 4:g * 4 + 4].unsqueeze(2)                         .broadcast_to([128, 4, TB])
                    copy_cnt[g] = p.vop(
                        "tensor_tensor",
                        hT[:].rearrange("p (k t) -> p k t", t=SEG)
                        [:, g * 4:g * 4 + 4, m * TB:(m + 1) * TB],
                        pp[:].rearrange("p (k t) -> p k t", t=TB),
                        wsl, ALU.mult, wait=[("t", p.cnt["t"])])
                st1[m]["tr_copies"] = copy_cnt[KB // 4 - 1]
                st1[m]["t_tr"] = p.cnt["t"]

            stages1 = [P1_0, P1_1, P1_2, P1_3, P1_4, P1_5]
            for m0 in range(0, NTB, 2):
                for sfn in stages1:
                    sfn(m0)
                    sfn(m0 + 1)
            s_ph1_end = p.cnt["s"]
            t_ph1_end = p.cnt["t"]

        with ExitStack() as ph2:
            E2 = ph2.enter_context
            Fb = arenaB[:, 0:KB * SEG]
            _wqo = {"a": 0, "i": 1, "v": 2}
            Wq = {w: [arenaB[:, 16384 + 2048 * (2 * _wqo[w] + i):
                             16384 + 2048 * (2 * _wqo[w] + i) + 2048]
                      for i in range(2)]
                  for w in ("a", "i", "v")}
            slA = [arenaF[:, 14336:15360], arenaF[:, 15360:16384]]
            slC = [arenaF[:, 16384:17408], arenaF[:, 17408:18432]]
            slE = [scr_big[:, 0:SEG], scr_big[:, SEG:2 * SEG]]
            slB = [hb[:].bitcast(F32)[:, 0:SEG], arenaF[:, 18432:19456]]
            slD = [arenaF[:, 19456:20480], scan_x[:, 0:SEG]]
            slF = [xa_t[:, 0:SEG], xa_t[:, SEG:2 * SEG]]
            slcd = [x0_t[0][:, 0:SEG], x0_t[0][:, SEG:2 * SEG]]
            slcs = [x0_t[1][:, 0:SEG], x0_t[1][:, SEG:2 * SEG]]
            t16s = [[t16[i][:] for i in range(6)],
                    [t16b[:, i * NCH:(i + 1) * NCH] for i in range(6)]]
            ps_a = E2(nc.psum_tensor("ps_a", [128, SEG], F32))
            ps_i = E2(nc.psum_tensor("ps_i", [128, SEG], F32))
            ps_v = E2(nc.psum_tensor("ps_v", [128, SEG], F32))

            v_hT_done = p.cnt["v"]
            t_w_end = {}
            wload = {}

            def load_aiv(o):
                for w in ("a", "i", "v"):
                    tw = [("t", t_w_end.get((w, o - 2), 0))]
                    wload[(w, o)] = p.dma("gpsimd", "w" + w, Wq[w][o % 2],
                                          w_aiv_l[w][:, o, :], wait=tw)

            load_aiv(0)
            psum_free = {"a": ("s", 0), "i": ("s", 0), "v": ("v", 0)}
            v_lp_at = {}
            v_state_w = {}
            v_fb_w = {}
            chain_v_end = {}
            for o in range(KB):
                if o + 1 < KB:
                    load_aiv(o + 1)
                for (w, ps) in (("a", ps_a), ("i", ps_i), ("v", ps_v)):
                    for k in range(KB):
                        for n in range(NMM):
                            waits = []
                            if n == 0 and k == 0:
                                waits = [wload[(w, o)], psum_free[w]]
                                if o == 0 and w == "a":
                                    waits.append(("v", v_hT_done))
                            p.top("matmul", ps[:, n * MMW:(n + 1) * MMW],
                                  Wq[w][o % 2][:, k * 128:(k + 1) * 128],
                                  hT[:, k * SEG + n * MMW:k * SEG + (n + 1) * MMW],
                                  start=(k == 0), stop=(k == KB - 1),
                                  wait=waits, inc=(k == KB - 1 and n == NMM - 1))
                    t_w_end[(w, o)] = p.cnt["t"]
                par = o % 2
                A, B, C = slA[par], slB[par], slC[par]
                Dg, Eg, F = slD[par], slE[par], slF[par]
                tcd, tcs = slcd[par], slcs[par]
                t6 = t16s[par]
                wv_prev = [("v", chain_v_end.get(o - 2, 0))]
                if o < 2:
                    wv_prev += [("v", v_hT_done), ("t", t_ph1_end),
                                ("s", s_ph1_end)]
                s1 = p.sop("activation", A, ps_a[:], AF.Exp, scale=-1.0,
                           bias=c_nbias[:, o:o + 1],
                           wait=[("t", t_w_end[("a", o)])] + wv_prev)
                psum_free["a"] = ("s", s1)
                s2 = p.sop("activation", F, ps_i[:], AF.Exp, scale=-1.0,
                           wait=[("t", t_w_end[("i", o)])])
                psum_free["i"] = ("s", s2)
                p.sop("activation", B, A, AF.Ln, bias=1.0)
                s4 = p.sop("activation", C, B, AF.Exp, scale=-2.0)
                p.sop("activation", A, F, AF.Ln, bias=1.0)
                s6 = p.sop("activation", Dg, A, AF.Exp, scale=-1.0)
                p.vop("tensor_scalar", Eg, C, -1.0, 1.0, ALU.mult,
                      ALU.add, wait=[("s", s4)])
                v_y = p.vop("tensor_scalar_max", Eg, Eg, 1e-8)
                p.sop("activation", C, Eg, AF.Ln, wait=[("v", v_y)])
                s_sq = p.sop("activation", A, C, AF.Exp, scale=0.5)
                p.vop("tensor_mul", Eg, A, Dg,
                      wait=[("s", max(s_sq, s6))])
                v_v = p.vop("tensor_mul", Eg, Eg, ps_v[:],
                            wait=[("t", t_w_end[("v", o)])])
                if o == KB - 1:
                    v_ps_free, s_ps_free = v_v, s2
                psum_free["v"] = ("v", v_v)
                p.vop("tensor_scalar_min", F, B, NLA_CLIP)
                v_scan = p.vop("tensor_tensor_scan", C, c_mask[:],
                               F, 0.0, ALU.mult, ALU.add,
                               wait=[("s", s_sq)])
                s_cd = p.sop("activation", tcd, C, AF.Exp,
                             scale=-1.0, wait=[("v", v_scan)])
                p.vop("tensor_scalar_min", F, C, NLA_CLIP)
                s_rinv = p.sop("activation", A, F, AF.Exp,
                               wait=[("v", p.cnt["v"])])
                p.vop("tensor_mul", Eg, Eg, A,
                      wait=[("s", s_rinv)])
                p.vop("tensor_tensor_scan", tcs, c_mask[:],
                      Eg, 0.0, ALU.mult, ALU.add)
                p.vop("tensor_copy", t6[0], tcd[:, CH - 1:SEG:CH],
                      wait=[("s", s_cd)])
                p.vop("tensor_copy", t6[1], tcs[:, CH - 1:SEG:CH])
                p.vop("tensor_mul", t6[2], t6[0], t6[1])
                p.vop("tensor_tensor_scan", t6[3], t6[0], t6[2], 0.0,
                      ALU.mult, ALU.add)
                p.vop("tensor_tensor_scan", t6[4], t6[0], z16[:], 1.0,
                      ALU.mult, ALU.add)
                p.vop("tensor_copy", LP[:, o:o + 1], t6[3][:, NCH - 1:NCH])
                v_lp_at[o] = p.vop("tensor_copy", LP[:, KB + o:KB + o + 1],
                                   t6[4][:, NCH - 1:NCH])
                p.vop("memset", t6[5][:, 0:1], 0.0)
                if NCH > 1:
                    p.vop("tensor_copy", t6[5][:, 1:NCH], t6[3][:, 0:NCH - 1])
                sin_b = t6[5].unsqueeze(2).broadcast_to([128, NCH, CH])
                p.vop("tensor_tensor",
                      Eg.rearrange("p (c k) -> p c k", k=CH),
                      tcs.rearrange("p (c k) -> p c k", k=CH), sin_b,
                      ALU.add)
                v_state_w[o] = p.vop(
                    "tensor_mul", states[:, o * SEG:(o + 1) * SEG], Eg,
                    tcd,
                    wait=[("s", s_ph1_end), ("t", t_ph1_end)])
                p.vop("memset", t6[5][:, 0:1], 1.0)
                if NCH > 1:
                    p.vop("tensor_copy", t6[5][:, 1:NCH], t6[4][:, 0:NCH - 1])
                pin_b = t6[5].unsqueeze(2).broadcast_to([128, NCH, CH])
                v_fb_w[o] = p.vop(
                    "tensor_tensor",
                    Fb[:, o * SEG:(o + 1) * SEG].rearrange(
                        "p (c k) -> p c k", k=CH),
                    tcd.rearrange("p (c k) -> p c k", k=CH), pin_b,
                    ALU.mult)
                chain_v_end[o] = p.cnt["v"]
            v_scan_end = p.cnt["v"]
            s_scan_end = p.cnt["s"]
            LPg = LP[:].rearrange("p (h o) -> p h o", h=2)
            wo_chunk = []
            for k in range(KB):
                if k < 6:
                    off = 16384 + 2048 * k
                elif k < 10:
                    off = 28672 + 2048 * (k - 6)
                elif k < 12:
                    off = 8192 + 2048 * (k - 10)
                else:
                    off = 2048 * (k - 12)
                wo_chunk.append(arenaB[:, off:off + 2048])
            ldcc_a = p.dma("gpsimd", "dcc", cc_in_a.ap(), LPg[:, :, 0:KSPL],
                           wait=[("v", v_lp_at[KSPL - 1])])
            p.wait("gpsimd", ldcc_a)
            p.op("gpsimd", "collective_compute", "AllGather", ALU.bypass,
                 [[r for r in range(NCORES)]],
                 ins=[cc_in_a.ap().opt()], outs=[cc_out_a.ap().opt()],
                 inc="cc", inc_k=1)
            ldg_a = p.dma("gpsimd", "dcc", gath[:, :, 0:2 * KSPL],
                          cc_out_a.ap().rearrange("(r p) f -> p r f",
                                                  r=NCORES), wait=[("cc", 1)])
            fix_g = {}

            def carry_group(o0, o1, g_states, g_prods, wld):
                nn = o1 - o0
                rb = c_rmask[:].unsqueeze(2).broadcast_to([128, NCORES, nn])
                p.gop("tensor_tensor", cr_t1[:, :, o0:o1], g_states, rb,
                      ALU.mult, wait=[wld])
                p.gop("tensor_scalar_add", cr_t2[:, :, o0:o1], g_prods, -1.0)
                p.gop("tensor_tensor", cr_t2[:, :, o0:o1], cr_t2[:, :, o0:o1],
                      rb, ALU.mult)
                p.gop("tensor_scalar_add", cr_t2[:, :, o0:o1],
                      cr_t2[:, :, o0:o1], 1.0)
                p.gop("memset", carry[:, o0:o1], 0.0)
                for r in range(NCORES):
                    p.gop("tensor_mul", cr_tmp[:, o0:o1], cr_t2[:, r, o0:o1],
                          carry[:, o0:o1])
                    p.gop("tensor_add", carry[:, o0:o1], cr_tmp[:, o0:o1],
                          cr_t1[:, r, o0:o1])
                for o in range(o0, o1):
                    cb = carry[:, o:o + 1].broadcast_to([128, SEG])
                    p.gop("tensor_tensor", fix_tmp[:],
                          Fb[:, o * SEG:(o + 1) * SEG], cb, ALU.mult,
                          wait=[("v", v_fb_w[o])])
                    fix_g[o] = p.gop(
                        "tensor_add", states[:, o * SEG:(o + 1) * SEG],
                        states[:, o * SEG:(o + 1) * SEG], fix_tmp[:],
                        wait=[("v", v_state_w[o])])

            carry_group(0, KSPL, gath[:, :, 0:KSPL], gath[:, :, KSPL:2 * KSPL],
                        ldg_a)
            ldwo = {}
            for k in range(6):
                ldwo[k] = p.dma("gpsimd", "dwo", wo_chunk[k],
                                w_o_r[k * 128:(k + 1) * 128, :],
                                wait=[("t", t_w_end[("v", KB - 1)])])
            for k in range(6, 10):
                ldwo[k] = p.dma("gpsimd", "dwo", wo_chunk[k],
                                w_o_r[k * 128:(k + 1) * 128, :],
                                wait=[("v", v_scan_end), ("s", s_scan_end)])
            for k in range(10, KB):
                ldwo[k] = p.dma("gpsimd", "dwo", wo_chunk[k],
                                w_o_r[k * 128:(k + 1) * 128, :],
                                wait=[("g", fix_g[KSPL - 1])])
            ldcc_b = p.dma("gpsimd", "dcc", cc_in_b.ap(), LPg[:, :, KSPL:KB],
                           wait=[("v", v_lp_at[KB - 1])])
            p.wait("gpsimd", ldcc_b)
            p.op("gpsimd", "collective_compute", "AllGather", ALU.bypass,
                 [[r for r in range(NCORES)]],
                 ins=[cc_in_b.ap().opt()], outs=[cc_out_b.ap().opt()],
                 inc="cc", inc_k=1)
            ldg_b = p.dma("gpsimd", "dcc", gath[:, :, 2 * KSPL:2 * KB],
                          cc_out_b.ap().rearrange("(r p) f -> p r f",
                                                  r=NCORES), wait=[("cc", 2)])
            carry_group(KSPL, KB, gath[:, :, 2 * KSPL:KSPL + KB],
                        gath[:, :, KSPL + KB:2 * KB], ldg_b)
            g_fix_end = p.cnt["g"]

        h2T = hT
        with ExitStack() as ph4:
            E4 = ph4.enter_context
            ps_o = [E4(nc.psum_tensor("ps_o%d" % i, [128, 512], F32))
                    for i in range(NB)]
            ps_tr = [E4(nc.psum_tensor("ps_tr4%d" % i, [128, 512], BF16))
                     for i in range(2)]
            scr4 = [scr_big[:], arenaF[:, 18432:20480]]
            xa4 = [xa_t[:], arenaF[:, 6144:8192]]
            hb4 = [hb[:], scan_x[:].bitcast(BF16)]
            stat4 = [stat[:], stat4b[:]]
            t_scan_mm = t_w_end[("v", KB - 1)]
            st4 = {}

            for m in range(NTB):
                par = m % 2
                prev = st4.get(m - 2)
                tmm = {}
                for k in range(KB):
                    for n in range(NB):
                        waits = []
                        if n == 0 and m == 0:
                            waits = [ldwo[k], ("g", fix_g[k])]
                            if k == 0:
                                waits += [("s", s_ps_free), ("v", v_ps_free)]
                        elif k == 0 and m > 0:
                            waits = [("v", st4[m - 1]["evn"][n])]
                        p.top("matmul", ps_o[n][:],
                              states[:, k * SEG + m * TB:k * SEG + (m + 1) * TB],
                              wo_chunk[k][:, n * 512:(n + 1) * 512],
                              start=(k == 0), stop=(k == KB - 1),
                              wait=waits,
                              inc=(k == KB - 1))
                        if k == KB - 1:
                            tmm[n] = p.cnt["t"]
                w_ldx = [("v", v_scan_end), ("s", s_scan_end)]
                if prev:
                    w_ldx = [("v", prev["v_ln1"]), ("s", prev["s_sq"])]
                ldx = p.dma("sync", "dx", x0_t[par][:],
                            x_in[m * TB:(m + 1) * TB, :], wait=w_ldx)
                w_st = [ldx]
                if prev:
                    w_st += [("v", prev["v_hb"]), ("t", prev["t_tr"])]
                s_r = stats_full(x0_t[par][:], hb4[par], eps_ln, True,
                                 wait=w_st, st=stat4[par])
                v_ln1 = p.vop("tensor_scalar", scr4[par], x0_t[par][:],
                              stat4[par][:, 2:3], stat4[par][:, 4:5],
                              ALU.subtract, ALU.mult, wait=[("s", s_r)])
                st4[m] = dict(v_ln1=v_ln1)
                evn = {}
                for n in range(NB):
                    w_ev = [("t", tmm[n])]
                    if m == 0:
                        w_ev.append(("s", s_scan_end))
                    if m == 1:
                        w_ev.append(("g", g_fix_end))
                    if prev:
                        w_ev += [prev["sp_xa"], ("s", prev["s_sq"])]
                    p.vop("tensor_copy", xa4[par][:, n * 512:(n + 1) * 512],
                          ps_o[n][:], wait=w_ev)
                    evn[n] = p.cnt["v"]
                st4[m]["evn"] = evn
                v_xa = p.vop("tensor_add", xa4[par], xa4[par], scr4[par])
                s_sq = p.sop("activation", x0_t[par][:], xa4[par], AF.Square,
                             accum_out=stat4[par][:, 1:2], wait=[("v", v_xa)])
                st4[m]["s_sq"] = s_sq
                vv = p.vop("tensor_scalar", stat4[par][:, 6:7],
                           stat4[par][:, 1:2], 1.0 / D, eps_rms, ALU.mult,
                           ALU.add, wait=[("s", s_sq)])
                p.sop("activation", stat4[par][:, 5:6], stat4[par][:, 6:7],
                      AF.Ln, wait=[("v", vv)])
                s_r2 = p.sop("activation", stat4[par][:, 4:5],
                             stat4[par][:, 5:6], AF.Exp, scale=-0.5)
                v_hb = p.vop("tensor_scalar_mul", hb4[par], xa4[par],
                             stat4[par][:, 4:5], wait=[("s", s_r2)])
                st4[m]["v_hb"] = v_hb
                copy_cnt = {}
                for g in range(KB // 4):
                    pp = ps_tr[g % 2]
                    for j in range(4):
                        k = g * 4 + j
                        w = []
                        if j == 0:
                            if g < 2:
                                w = [("v", v_hb)]
                                if m > 0:
                                    w.append(("v", st4[m - 1]["tr_copies"]))
                            else:
                                w = [("v", copy_cnt[g - 2])]
                        p.top("transpose", pp[:, j * 128:(j + 1) * 128],
                              hb4[par][:, k * 128:(k + 1) * 128], c_id[:],
                              wait=w, inc=(j == 3))
                    wsl = c_wff[:, g * 4:g * 4 + 4].unsqueeze(2)                         .broadcast_to([128, 4, TB])
                    copy_cnt[g] = p.vop(
                        "tensor_tensor",
                        h2T[:].rearrange("p (k t) -> p k t", t=SEG)
                        [:, g * 4:g * 4 + 4, m * TB:(m + 1) * TB],
                        pp[:].rearrange("p (k t) -> p k t", t=TB),
                        wsl, ALU.mult,
                        wait=[("t", p.cnt["t"]), ("t", t_scan_mm)])
                st4[m]["tr_copies"] = copy_cnt[KB // 4 - 1]
                st4[m]["t_tr"] = p.cnt["t"]
                st4[m]["sp_xa"] = p.dma("gpsimd", "dxa",
                                        xa_sp.ap()[m * TB:(m + 1) * TB, :],
                                        xa4[par], wait=[("v", v_hb)])

        def gu_down_phase(tag, srcT, acc_ap, gate_src, up_src, dn_rhs_dma,
                          fuse_buf, dn_ring, t_silu, wg_t, wu_t,
                          sg_col=None, su_col=None):
            v_src_done = p.cnt["v"]
            t_entry = p.cnt["t"]
            s_entry = p.cnt["s"]
            dx1_entry = p.cnt["dx1s0"]
            ev_bank = {m: 0 for m in range(NTB)}
            ev_all = 0
            evac_final = [0] * NTB
            t_gu_last = t_entry
            lastpar = {0: t_entry, 1: t_entry}
            lastring = {}
            for q in range(NQ):
                with ExitStack() as pq:
                    ps_g = [pq.enter_context(
                        nc.psum_tensor("ps_g%s_%d_%d" % (tag, q, i),
                                       [128, SEG], F32)) for i in range(2)]
                    ps_u = [pq.enter_context(
                        nc.psum_tensor("ps_u%s_%d_%d" % (tag, q, i),
                                       [128, SEG], F32)) for i in range(2)]
                    sgv = {}
                    fu = {}
                    wl = {}

                    def load_pair(oi, q=q, wl=wl):
                        o = q * OHQ + oi
                        par = oi % 2
                        tw = [("t", lastpar[par])]
                        if q == 0 and oi == 0:
                            tw = [("t", t_entry), ("v", v_src_done),
                                  ("s", s_entry)]
                        a_ = p.dma("gpsimd", "wg", wg_t[par],
                                   gate_src[:, o, :], wait=tw)
                        b_ = p.dma("gpsimd", "wu", wu_t[par], up_src[:, o, :])
                        wl[oi] = (a_, b_)

                    load_pair(0)
                    for oi in range(OHQ):
                        o = q * OHQ + oi
                        par = oi % 2
                        if oi + 1 < OHQ:
                            load_pair(oi + 1)
                        for (wt, ps, wld) in ((wg_t[par], ps_g[par], wl[oi][0]),
                                              (wu_t[par], ps_u[par], wl[oi][1])):
                            for k in range(KB):
                                for n in range(NMM):
                                    waits = []
                                    if n == 0 and k == 0:
                                        waits = [wld, ("v", v_src_done),
                                                 ("s", sgv.get(par, 0)),
                                                 ("v", max(fu.get(par, 0),
                                                           ev_all))]
                                    p.top("matmul",
                                          ps[:, n * MMW:(n + 1) * MMW],
                                          wt[:, k * 128:(k + 1) * 128],
                                          srcT[:, k * SEG + n * MMW:
                                               k * SEG + (n + 1) * MMW],
                                          start=(k == 0), stop=(k == KB - 1),
                                          wait=waits,
                                          inc=(k == KB - 1 and n == NMM - 1))
                        wl[("t", oi)] = p.cnt["t"]
                        lastpar[par] = p.cnt["t"]
                        silu_sl = t_silu[:, par * SEG:(par + 1) * SEG]
                        guard = ([("v", v_src_done), ("dx1s0", dx1_entry)]
                                 if (q == 0 and oi == 0) else [])
                        if sg_col is None:
                            s_g = p.sop("activation", silu_sl, ps_g[par][:],
                                        AF.Silu, wait=[("t", wl[("t", oi)]),
                                                       ("v", fu.get(par, 0))]
                                        + guard)
                            f_ = p.vop("tensor_mul",
                                       fuse_buf[:, oi * SEG:(oi + 1) * SEG],
                                       silu_sl, ps_u[par][:],
                                       wait=[("s", s_g)])
                        else:
                            s_g = p.sop("activation", silu_sl, ps_g[par][:],
                                        AF.Silu, scale=sg_col[:, o:o + 1],
                                        wait=[("t", wl[("t", oi)]),
                                              ("v", fu.get(par, 0))] + guard)
                            f_ = p.vop("scalar_tensor_tensor",
                                       fuse_buf[:, oi * SEG:(oi + 1) * SEG],
                                       ps_u[par][:], su_col[:, o:o + 1],
                                       silu_sl, ALU.mult, ALU.mult,
                                       wait=[("s", s_g)])
                        sgv[par] = s_g
                        fu[par] = f_
                    v_fuse_done = p.cnt["v"]
                    s_last = p.cnt["s"]
                    if q == NQ - 1:
                        t_gu_last = p.cnt["t"]
                with ExitStack() as pd:
                    ps_d = [pd.enter_context(
                        nc.psum_tensor("ps_d%s_%d_%d" % (tag, q, i),
                                       [128, 512], F32))
                        for i in range(min(8, NTB))]
                    for n in range(NB):
                        tm = {}
                        for k in range(KHQ):
                            r = k % len(dn_ring)
                            ring = dn_ring[r]
                            tw = [("t", lastring.get(r, t_entry))]
                            ldd = dn_rhs_dma(q, n, k, ring, tw)
                            for m in range(NTB):
                                waits = []
                                if m == 0:
                                    waits = [ldd]
                                if k == 0:
                                    waits = list(waits) + [("v", ev_bank[m])]
                                    if n == 0:
                                        waits += [("v", v_fuse_done),
                                                  ("s", s_last)]
                                p.top("matmul", ps_d[m % 8][:],
                                      fuse_buf[:, k * SEG + m * TB:
                                               k * SEG + (m + 1) * TB],
                                      ring,
                                      start=(k == 0), stop=(k == KHQ - 1),
                                      wait=waits,
                                      inc=(m == NTB - 1 or k == KHQ - 1))
                                if k == KHQ - 1:
                                    tm[m] = p.cnt["t"]
                            lastring[r] = p.cnt["t"]
                        for m in range(NTB):
                            dst = acc_ap[:, m * D + n * 512:m * D + (n + 1) * 512]
                            if q == 0:
                                p.vop("tensor_copy", dst, ps_d[m % 8][:],
                                      wait=[("t", tm[m]),
                                            ("dxa0", 16 * NTB)])
                            else:
                                p.vop("tensor_add", dst, dst, ps_d[m % 8][:],
                                      wait=[("t", tm[m])])
                            ev_bank[m] = p.cnt["v"]
                            if q == NQ - 1 and n == NB - 1:
                                evac_final[m] = p.cnt["v"]
                    ev_all = max(ev_bank.values())
            return dict(ev_all=ev_all, evac_final=evac_final,
                        t_gu_last=t_gu_last)

        with ExitStack() as ph5:
            E5 = ph5.enter_context
            assert OHQ * SEG == KB * SEG
            fuse5 = states[:, 0:OHQ * SEG]
            fuse7 = hT[:, 0:OHQ * SEG]
            acc = arenaF[:, 0:NTB * D]
            t_silu = scr_big[:]
            KW = KB * 128
            x0b = [x0_t[i][:].bitcast(BF16) for i in range(2)]
            wg_t = [x0b[0][:, 0:KW], x0b[0][:, KW:2 * KW]]
            wu_t = [x0b[1][:, 0:KW], x0b[1][:, KW:2 * KW]]
            nring = max(1, min(4, D // 512))
            dn_ring = [hb[:, r * 512:(r + 1) * 512] for r in range(nring)]

            def dn_dma(qq, n, k, ring, tw):
                row0 = qq * (FFN // NQ) + k * 128
                return p.dma("gpsimd", "dn", ring,
                             w_dn_r[row0:row0 + 128, n * 512:(n + 1) * 512],
                             wait=tw)

            info5 = gu_down_phase('5', h2T, acc, w_gu_l[:, 0:OBF, :],
                                  w_gu_l[:, OBF:2 * OBF, :], dn_dma,
                                  fuse5, dn_ring, t_silu, wg_t, wu_t)

            ln2T = states
            t_p5_end = p.cnt["t"]
            s_p5_end = p.cnt["s"]
            t_gu5 = info5["t_gu_last"]
            ev5 = info5["evac_final"]
            ev5_all = info5["ev_all"]
            hTf = hT[:].bitcast(F32)
            scrb16 = scr_big[:].bitcast(BF16)
            scr6 = [hTf[:, 0:2048], hTf[:, 2048:4096]]
            xat6 = [hTf[:, 4096:6144], xa_t[:]]
            stat6 = [stat[:], hTf[:, 6144:6152]]
            hb6 = [scrb16[:, 0:2048], scrb16[:, 2048:4096]]
            with ExitStack() as ph6:
                ps_tr6 = [ph6.enter_context(
                    nc.psum_tensor("ps_tr6%d" % i, [128, 512], BF16))
                    for i in range(2)]
                st6 = {}

                def S6_0(m):
                    par = m % 2
                    buf = x0_t[par]
                    prev = st6.get(m - 2)
                    w = [("t", t_gu5), ("dxa0", 16 * (m + 1))]
                    if prev:
                        w += [("v", prev["v_x1"])]
                    ldxa = p.dma("sync", "dx", buf[:],
                                 xa_sp.ap()[m * TB:(m + 1) * TB, :], wait=w)
                    w2 = [ldxa, ("v", ev5[m]), ("t", t_gu5)]
                    if prev:
                        w2 += [("s", prev["s_end"]), prev["sp_x1"]]
                    v_ao = p.vop("tensor_add", scr6[par], buf[:],
                                 acc[:, m * D:(m + 1) * D], wait=w2)
                    st6[m] = dict(v_ao=v_ao)

                def S6_1(m):
                    par = m % 2
                    prev = st6.get(m - 2)
                    w = [("v", st6[m]["v_ao"])]
                    if prev:
                        w.append(("v", prev["v_hb"]))
                    st6[m]["s_r"] = stats_full(scr6[par], x0_t[par][:],
                                               eps_ln, True, wait=w,
                                               st=stat6[par])

                def S6_2(m):
                    par = m % 2
                    s_r = st6[m]["s_r"]
                    p.vop("tensor_scalar", xat6[par], scr6[par],
                          stat6[par][:, 2:3], stat6[par][:, 4:5],
                          ALU.subtract, ALU.mult,
                          wait=[("s", s_r), ("dxa0", 16 * NTB)])
                    st6[m]["ldx0"] = p.dma("sync", "dx", x0_t[par][:],
                                           x_in[m * TB:(m + 1) * TB, :],
                                           wait=[("s", s_r)])

                def S6_3(m):
                    par = m % 2
                    v_x1 = p.vop("tensor_add", scr6[par], xat6[par],
                                 x0_t[par][:], wait=[st6[m]["ldx0"]])
                    st6[m]["v_x1"] = v_x1
                    st6[m]["sp_x1"] = p.dma("gpsimd", "dx1s",
                                            x1_sp.ap()[m * TB:(m + 1) * TB, :],
                                            scr6[par], wait=[("v", v_x1)])
                    st6[m]["s_r2"] = stats_full(scr6[par], xat6[par], eps_ln,
                                                True, wait=[("v", v_x1)],
                                                st=stat6[par])
                    st6[m]["s_end"] = p.cnt["s"]

                def S6_4(m):
                    par = m % 2
                    prev = st6.get(m - 2)
                    w = [("s", st6[m]["s_r2"]), ("s", s_p5_end)]
                    if prev:
                        w.append(("t", prev["t_tr"]))
                    st6[m]["v_hb"] = p.vop(
                        "tensor_scalar", hb6[par], scr6[par],
                        stat6[par][:, 2:3], stat6[par][:, 4:5],
                        ALU.subtract, ALU.mult, wait=w)

                def S6_5(m):
                    par = m % 2
                    copy_cnt = {}
                    for g in range(KB // 4):
                        pp = ps_tr6[g % 2]
                        for j in range(4):
                            k = g * 4 + j
                            w = []
                            if j == 0:
                                if g < 2:
                                    w = [("v", st6[m]["v_hb"])]
                                    if m == 0:
                                        w.append(("v", ev5_all))
                                    else:
                                        w.append(("v",
                                                  st6[m - 1]["tr_copies"]))
                                else:
                                    w = [("v", copy_cnt[g - 2])]
                            p.top("transpose", pp[:, j * 128:(j + 1) * 128],
                                  hb6[par][:, k * 128:(k + 1) * 128],
                                  c_id[:], wait=w, inc=(j == 3))
                        copy_cnt[g] = p.vop(
                            "tensor_copy",
                            ln2T[:].rearrange("p (k t) -> p k t", t=SEG)
                            [:, g * 4:g * 4 + 4, m * TB:(m + 1) * TB],
                            pp[:].rearrange("p (k t) -> p k t", t=TB),
                            wait=[("t", p.cnt["t"]), ("t", t_p5_end)])
                    st6[m]["tr_copies"] = copy_cnt[KB // 4 - 1]
                    st6[m]["t_tr"] = p.cnt["t"]

                stages6 = [S6_0, S6_1, S6_2, S6_3, S6_4, S6_5]
                for m0 in range(0, NTB, 2):
                    for sfn in stages6:
                        sfn(m0)
                        sfn(m0 + 1)

            def dn8_dma(qq, n, k, ring, tw):
                row0 = qq * (FFN // NQ) + k * 128
                return p.dma("gpsimd", "dn", ring,
                             wd8_r[n, row0:row0 + 128, :], wait=tw)

            info7 = gu_down_phase('7', ln2T, acc, wg8_l, wu8_l, dn8_dma,
                                  fuse7, dn_ring, t_silu, wg_t, wu_t,
                                  sg_col=c_sg[:], su_col=c_su[:])

            t_p7_end = p.cnt["t"]
            t_gu7 = info7["t_gu_last"]
            ev7 = info7["evac_final"]
            statesf = states[:].bitcast(F32)
            scr8 = [hTf[:, 0:2048], hTf[:, 2048:4096]]
            xat8 = [hTf[:, 4096:6144], xa_t[:]]
            stat8 = [stat[:], hTf[:, 6144:6152]]
            dump8 = [statesf[:, 0:2048], statesf[:, 2048:4096]]
            st8 = {}

            def S8_0(m):
                par = m % 2
                prev = st8.get(m - 2)
                w = [("t", t_gu7), ("dx1s0", 16 * (m + 1))]
                if prev:
                    w.append(("v", prev["v_o"]))
                ldx1 = p.dma("sync", "dx", x0_t[par][:],
                             x1_sp.ap()[m * TB:(m + 1) * TB, :], wait=w)
                w2 = [("v", ev7[m]), ld_sdb, ("t", t_p7_end)]
                if prev:
                    w2 += [("s", prev["s_end"]), prev["dy"]]
                v_y3 = p.vop("tensor_mul", scr8[par],
                             acc[:, m * D:(m + 1) * D], sdb[:], wait=w2)
                st8[m] = dict(ldx1=ldx1, v_y3=v_y3)

            def S8_1(m):
                par = m % 2
                st8[m]["s_r"] = stats_full(scr8[par], dump8[par], eps_ln,
                                           True, wait=[("v", st8[m]["v_y3"])],
                                           st=stat8[par])
                st8[m]["s_end"] = p.cnt["s"]

            def S8_2(m):
                par = m % 2
                p.vop("tensor_scalar", xat8[par], scr8[par],
                      stat8[par][:, 2:3], stat8[par][:, 4:5],
                      ALU.subtract, ALU.mult, wait=[("s", st8[m]["s_r"])])

            def S8_3(m):
                par = m % 2
                v_o = p.vop("tensor_add", scr8[par], xat8[par], x0_t[par][:],
                            wait=[st8[m]["ldx1"]])
                st8[m]["v_o"] = v_o
                st8[m]["dy"] = p.dma("sync", "dy",
                                     y_out[m * TB:(m + 1) * TB, :],
                                     scr8[par], wait=[("v", v_o)])

            stages8 = [S8_0, S8_1, S8_2, S8_3]
            for m0 in range(0, NTB, 2):
                for sfn in stages8:
                    sfn(m0)
                    sfn(m0 + 1)
            p.wait("sync", st8[NTB - 2]["dy"], st8[NTB - 1]["dy"])

        with nc.Block() as block:
            p.finalize(block)

    return nc


def _lhsT_layout(W, dtype):
    Din, Dout = W.shape
    KBl, OBl = Din // 128, Dout // 128
    return np.ascontiguousarray(
        W.reshape(KBl, 128, OBl, 128).transpose(1, 2, 0, 3)
        .reshape(128, OBl, KBl * 128)).astype(dtype, copy=False)


def _colT(vec):
    Dl = vec.shape[0]
    return np.ascontiguousarray(vec.reshape(Dl // 128, 128).T)


def make_inputs(cfg, fi):
    D, FFN, SEG, NSEG, Bb = (cfg[k] for k in ("D", "FFN", "SEG", "NSEG", "B"))
    NCORES = Bb * NSEG
    NB = D // 512
    x = np.asarray(fi["L0_attn_x"], np.float32)

    shared = {
        "w_a_l": _lhsT_layout(np.asarray(fi["L0_attn_w_a"], np.float32), np.float32),
        "w_i_l": _lhsT_layout(np.asarray(fi["L0_attn_w_i"], np.float32), np.float32),
        "w_v_l": _lhsT_layout(np.asarray(fi["L0_attn_w_v"], np.float32), np.float32),
        "w_o_r": np.ascontiguousarray(np.asarray(fi["L0_attn_out_proj_w"], np.float32)),
        "w_gu_l": _lhsT_layout(np.asarray(fi["L0_attn_w_gate_up"], np.float32),
                               np.float32),
        "w_dn_r": np.ascontiguousarray(np.asarray(fi["L0_attn_w_down"], np.float32)),
        "wg8_l": _lhsT_layout(np.asarray(fi["L0_ffn_w_gate"], np.int8), np.int8),
        "wu8_l": _lhsT_layout(np.asarray(fi["L0_ffn_w_up"], np.int8), np.int8),
        "wd8_r": np.ascontiguousarray(
            np.asarray(fi["L0_ffn_w_down"], np.int8).reshape(FFN, NB, 512)
            .transpose(1, 0, 2)),
        "biasT": _colT(np.asarray(fi["L0_attn_decay_bias"], np.float32)),
        "wpreT": _colT(np.asarray(fi["L0_attn_pre_norm_w"], np.float32)),
        "wffT": _colT(np.asarray(fi["L0_attn_ffn_norm_w"], np.float32)),
        "sgT": _colT(np.asarray(fi["L0_ffn_s_gate"], np.float32)),
        "suT": _colT(np.asarray(fi["L0_ffn_s_up"], np.float32)),
        "sd_row": np.ascontiguousarray(
            np.asarray(fi["L0_ffn_s_down"], np.float32)[None, :]),
        "identity": np.eye(128, dtype=BF),
        "ones_row": np.ones((1, 128), BF),
    }
    scanmask = np.ones((128, SEG), np.float32)
    scanmask[:, ::CH] = 0.0
    shared["scanmask"] = scanmask

    in_maps = []
    for c in range(NCORES):
        b, s0 = divmod(c, NSEG)
        rmask = np.zeros((128, NCORES), np.float32)
        for r in range(NCORES):
            if r // NSEG == b and (r % NSEG) < s0:
                rmask[:, r] = 1.0
        m = dict(shared)
        m["x"] = np.ascontiguousarray(x[b, s0 * SEG:(s0 + 1) * SEG, :])
        m["rankmask"] = rmask
        in_maps.append(m)
    return in_maps


def kernel(**inputs):
    cfg = FULL
    NCORES = cfg["B"] * cfg["NSEG"]
    nc = build_nc(cfg)
    in_maps = make_inputs(cfg, inputs)
    res = run_bass_kernel_spmd(nc, in_maps, core_ids=list(range(NCORES)))
    SEG, D = cfg["SEG"], cfg["D"]
    out = np.empty((cfg["B"], cfg["NSEG"] * SEG, D), np.float32)
    for c in range(NCORES):
        b, s0 = divmod(c, cfg["NSEG"])
        out[b, s0 * SEG:(s0 + 1) * SEG, :] = res.results[c]["y"]
    return out


# version 16
# speedup vs baseline: 1.0583x; 1.0583x over previous
from contextlib import ExitStack

import numpy as np
import ml_dtypes

import concourse.bass as bass
import concourse.mybir as mybir
from concourse.bass_utils import run_bass_kernel_spmd

F32 = mybir.dt.float32
BF16 = mybir.dt.bfloat16
I8 = mybir.dt.int8
AF = mybir.ActivationFunctionType
ALU = mybir.AluOpType
BF = ml_dtypes.bfloat16

NLA_CLIP = 23.025851
CH = 64
TB = 128

FULL = dict(D=2048, FFN=8192, SEG=1024, NSEG=4, B=2)
MINI = dict(D=512, FFN=2048, SEG=256, NSEG=2, B=2)

DMA_STREAMS = {"dC": 1, "dx": 2, "wa": 2, "wi": 2, "wv": 2, "dwo": 16,
               "wg": 2, "wu": 2, "dn": 4, "dxa": 1, "dx1s": 1, "dcc": 1,
               "dy": 2}


class Prog:

    ENGINES = ("sync", "scalar", "vector", "tensor", "gpsimd")

    def __init__(self, nc):
        self.nc = nc
        self.ops = {e: [] for e in self.ENGINES}
        self.cnt = {}
        self.sems = {}
        self.duse = {}
        self._waited = {e: {} for e in self.ENGINES}

    def make_sems(self, ctx, names):
        for n in names:
            self.sems[n] = ctx.enter_context(self.nc.semaphore("sem_" + n))
            self.cnt[n] = 0

    def make_dma_streams(self, ctx, streams):
        for s, n in streams.items():
            for i in range(n):
                name = s + str(i)
                self.sems[name] = ctx.enter_context(
                    self.nc.semaphore("sem_" + name))
                self.cnt[name] = 0
            self.duse[s] = 0

    def _dedup(self, engine, waits):
        wl = []
        for w in waits:
            if w is None:
                continue
            sname, val = w
            if val is None or val <= 0:
                continue
            if self._waited[engine].get(sname, 0) >= val:
                continue
            self._waited[engine][sname] = val
            wl.append((self.sems[sname], val))
        return wl

    def op(self, engine, method, *args, wait=(), inc=None, inc_k=1, **kwargs):
        wl = self._dedup(engine, list(wait))
        ret = None
        sem = None
        if inc is not None:
            self.cnt[inc] += inc_k
            ret = self.cnt[inc]
            sem = self.sems[inc]

        def go(eng, wl=wl, method=method, args=args, kwargs=kwargs, sem=sem,
               inc_k=inc_k):
            for (s, val) in wl:
                eng.wait_ge(s, val)
            inst = getattr(eng, method)(*args, **kwargs)
            if sem is not None:
                inst.then_inc(sem, inc_k)

        self.ops[engine].append(go)
        return ret

    def wait(self, engine, *waits):
        wl = self._dedup(engine, list(waits))
        if wl:
            def go(eng, wl=wl):
                for (s, val) in wl:
                    eng.wait_ge(s, val)
            self.ops[engine].append(go)

    def vop(self, method, *args, wait=(), **kwargs):
        return self.op("vector", method, *args,
                       wait=[("v", self.cnt["v"])] + list(wait), inc="v", **kwargs)

    def sop(self, method, *args, wait=(), **kwargs):
        return self.op("scalar", method, *args,
                       wait=[("s", self.cnt["s"])] + list(wait), inc="s", **kwargs)

    def gop(self, method, *args, wait=(), **kwargs):
        return self.op("gpsimd", method, *args,
                       wait=[("g", self.cnt["g"])] + list(wait), inc="g", **kwargs)

    def top(self, method, *args, wait=(), inc=False, **kwargs):
        return self.op("tensor", method, *args, wait=wait,
                       inc=("t" if inc else None), **kwargs)

    def dma(self, queue, stream, dst, src, wait=()):
        n = DMA_STREAMS[stream]
        use = self.duse[stream]
        self.duse[stream] = use + 1
        semname = stream + str(use % n)
        self.op(queue, "dma_start", dst, src, wait=wait, inc=semname, inc_k=16)
        return (semname, self.cnt[semname])

    def finalize(self, block):
        for e in self.ENGINES:
            fns = self.ops[e]
            if not fns:
                continue

            def emit(eng, fns=fns):
                for f in fns:
                    f(eng)

            getattr(block, e)(emit)


def build_nc(cfg):
    D, FFN, SEG = cfg["D"], cfg["FFN"], cfg["SEG"]
    KB = D // 128
    NTB = SEG // TB
    NCH = SEG // CH
    OBF = FFN // 128
    NB = D // 512
    NCORES = cfg["B"] * cfg["NSEG"]
    NQ = 4 if OBF % 8 == 0 else 2
    OHQ = OBF // NQ
    KHQ = OHQ
    MMW = min(SEG, 512)
    NMM = SEG // MMW
    eps_ln, eps_rms = 1e-5, 1e-6

    nc = bass.Bass("TRN2", target_bir_lowering=False)

    def din(name, shape, dt=F32):
        return nc.dram_tensor(name, list(shape), dt, kind="ExternalInput").ap()

    x_in = din("x", (SEG, D))
    w_aiv_l = {w: din("w_%s_l" % w, (128, KB, KB * 128)) for w in ("a", "i", "v")}
    w_o_r = din("w_o_r", (D, D))
    w_gu_l = din("w_gu_l", (128, 2 * OBF, KB * 128))
    w_dn_r = din("w_dn_r", (FFN, D))
    wg8_l = din("wg8_l", (128, OBF, KB * 128), I8)
    wu8_l = din("wu8_l", (128, OBF, KB * 128), I8)
    wd8_r = din("wd8_r", (NB, FFN, 512), I8)
    biasT = din("biasT", (128, KB))
    wpreT = din("wpreT", (128, KB))
    wffT = din("wffT", (128, KB))
    sgT = din("sgT", (128, OBF))
    suT = din("suT", (128, OBF))
    sd_row = din("sd_row", (1, D))
    scanmask = din("scanmask", (128, SEG))
    identity = din("identity", (128, 128), BF16)
    ones_row = din("ones_row", (1, 128), BF16)
    rankmask = din("rankmask", (128, NCORES))
    y_out = nc.dram_tensor("y", [SEG, D], F32, kind="ExternalOutput").ap()

    KSPL = 12
    cc_in_a = nc.dram_tensor("cc_in_a", [128, 2 * KSPL], F32)
    cc_out_a = nc.dram_tensor("cc_out_a", [NCORES * 128, 2 * KSPL], F32)
    cc_in_b = nc.dram_tensor("cc_in_b", [128, 2 * (KB - KSPL)], F32)
    cc_out_b = nc.dram_tensor("cc_out_b", [NCORES * 128, 2 * (KB - KSPL)], F32)
    xa_sp = nc.dram_tensor("xa_sp", [SEG, D], F32)
    x1_sp = nc.dram_tensor("x1_sp", [SEG, D], F32)

    with ExitStack() as ctx:
        E = ctx.enter_context
        p = Prog(nc)
        p.make_sems(ctx, ["v", "s", "g", "t", "cc"])
        p.make_dma_streams(ctx, DMA_STREAMS)

        hT = E(nc.sbuf_tensor("hT", [128, KB * SEG], BF16))
        states = E(nc.sbuf_tensor("states", [128, KB * SEG], BF16))
        arena = E(nc.sbuf_tensor("arena", [128, 20480], F32))
        arenaF = arena[:]
        arenaB = arena[:].bitcast(BF16)
        c_bias = E(nc.sbuf_tensor("c_bias", [128, KB], F32))
        c_nbias = E(nc.sbuf_tensor("c_nbias", [128, KB], F32))
        c_wpre = E(nc.sbuf_tensor("c_wpre", [128, KB], F32))
        c_wff = E(nc.sbuf_tensor("c_wff", [128, KB], F32))
        c_sg = E(nc.sbuf_tensor("c_sg", [128, OBF], F32))
        c_su = E(nc.sbuf_tensor("c_su", [128, OBF], F32))
        c_mask = E(nc.sbuf_tensor("c_mask", [128, SEG], F32))
        c_id = E(nc.sbuf_tensor("c_id", [128, 128], BF16))
        c_ones = E(nc.sbuf_tensor("c_ones", [1, 128], BF16))
        c_rmask = E(nc.sbuf_tensor("c_rmask", [128, NCORES], F32))
        sdb = E(nc.sbuf_tensor("sdb", [128, D], F32))
        LP = E(nc.sbuf_tensor("LP", [128, 2 * KB], F32))
        gath = E(nc.sbuf_tensor("gath", [128, NCORES, 2 * KB], F32))
        carry = E(nc.sbuf_tensor("carry", [128, KB], F32))
        cr_t1 = E(nc.sbuf_tensor("cr_t1", [128, NCORES, KB], F32))
        cr_t2 = E(nc.sbuf_tensor("cr_t2", [128, NCORES, KB], F32))
        cr_tmp = E(nc.sbuf_tensor("cr_tmp", [128, KB], F32))
        fix_tmp = E(nc.sbuf_tensor("fix_tmp", [128, SEG], BF16))
        scan_x = E(nc.sbuf_tensor("scan_x", [128, SEG], F32))
        t16b = E(nc.sbuf_tensor("t16b", [128, 6 * (SEG // CH)], F32))
        xa_t = E(nc.sbuf_tensor("xa_t", [128, D], F32))
        stat4b = E(nc.sbuf_tensor("stat4b", [128, 8], F32))
        x0_t = [E(nc.sbuf_tensor("x0_t%d" % i, [128, D], F32)) for i in range(2)]
        stat = E(nc.sbuf_tensor("stat", [128, 8], F32))
        scr_big = E(nc.sbuf_tensor("scr_big", [128, D], F32))
        hb = E(nc.sbuf_tensor("hb", [128, D], BF16))
        t16 = [E(nc.sbuf_tensor("t16_%d" % i, [128, NCH], F32)) for i in range(6)]
        z16 = E(nc.sbuf_tensor("z16", [128, NCH], F32))

        p.dma("sync", "dC", c_bias[:], biasT)
        p.dma("sync", "dC", c_wpre[:], wpreT)
        p.dma("sync", "dC", c_wff[:], wffT)
        p.dma("sync", "dC", c_sg[:], sgT)
        p.dma("sync", "dC", c_su[:], suT)
        p.dma("sync", "dC", c_mask[:], scanmask)
        p.dma("sync", "dC", c_id[:], identity)
        p.dma("sync", "dC", c_ones[:], ones_row)
        p.dma("sync", "dC", c_rmask[:], rankmask)
        ld_sdb = p.dma("sync", "dC", sdb[:],
                       sd_row.broadcast_to([128, D]))
        ldC = ld_sdb
        p.vop("memset", z16[:], 0.0)
        p.vop("tensor_scalar_mul", c_nbias[:], c_bias[:], -1.0, wait=[ldC])

        def stats_full(src_ap, dump_ap, eps, mean_too, wait=(), st=None):
            if st is None:
                st = stat[:]
            if mean_too:
                p.sop("activation", dump_ap, src_ap, AF.Identity,
                      accum_out=st[:, 0:1], wait=list(wait))
                wait = ()
            p.sop("activation", dump_ap, src_ap, AF.Square,
                  accum_out=st[:, 1:2], wait=list(wait))
            sw = [("s", p.cnt["s"])]
            if mean_too:
                p.vop("tensor_scalar_mul", st[:, 2:3], st[:, 0:1], 1.0 / D,
                      wait=sw)
                p.vop("tensor_mul", st[:, 3:4], st[:, 2:3], st[:, 2:3])
                p.vop("tensor_scalar", st[:, 6:7], st[:, 1:2], 1.0 / D,
                      st[:, 3:4], ALU.mult, ALU.subtract)
                vv = p.vop("tensor_scalar_add", st[:, 6:7], st[:, 6:7], eps)
            else:
                vv = p.vop("tensor_scalar", st[:, 6:7], st[:, 1:2], 1.0 / D,
                           eps, ALU.mult, ALU.add, wait=sw)
            p.sop("activation", st[:, 5:6], st[:, 6:7], AF.Ln,
                  wait=[("v", vv)])
            return p.sop("activation", st[:, 4:5], st[:, 5:6], AF.Exp,
                         scale=-0.5)

        with ExitStack() as ph1:
            ps_tr = [ph1.enter_context(nc.psum_tensor("ps_tr%d" % i, [128, 512],
                                                      BF16)) for i in range(2)]
            statesP1 = states[:].bitcast(F32)
            scr1 = [scr_big[:], statesP1[:, 0:2048]]
            stat1 = [stat[:], statesP1[:, 2048:2056]]
            hb1 = [hb[:], states[:][:, 4112:6160]]
            st1 = {}

            def P1_0(m):
                par = m % 2
                prev = st1.get(m - 2)
                w = []
                if prev:
                    w = [("s", prev["s_sq"]), ("v", prev["v_ln1"])]
                st1[m] = dict(ldx=p.dma("sync", "dx", x0_t[par][:],
                                        x_in[m * TB:(m + 1) * TB, :], wait=w))

            def P1_1(m):
                par = m % 2
                prev = st1.get(m - 2)
                w = [st1[m]["ldx"]]
                if prev:
                    w.append(("v", prev["v_hb"]))
                st1[m]["s_r"] = stats_full(x0_t[par][:], scr1[par], eps_ln,
                                           True, wait=w, st=stat1[par])

            def P1_2(m):
                par = m % 2
                st1[m]["v_ln1"] = p.vop("tensor_scalar", scr1[par],
                                        x0_t[par][:], stat1[par][:, 2:3],
                                        stat1[par][:, 4:5], ALU.subtract,
                                        ALU.mult, wait=[("s", st1[m]["s_r"])])

            def P1_3(m):
                par = m % 2
                st1[m]["s_sq"] = p.sop("activation", x0_t[par][:], scr1[par],
                                       AF.Square, accum_out=stat1[par][:, 1:2],
                                       wait=[("v", st1[m]["v_ln1"])])
                vv = p.vop("tensor_scalar", stat1[par][:, 6:7],
                           stat1[par][:, 1:2], 1.0 / D, eps_rms, ALU.mult,
                           ALU.add, wait=[("s", p.cnt["s"])])
                p.sop("activation", stat1[par][:, 5:6], stat1[par][:, 6:7],
                      AF.Ln, wait=[("v", vv)])
                st1[m]["s_r2"] = p.sop("activation", stat1[par][:, 4:5],
                                       stat1[par][:, 5:6], AF.Exp, scale=-0.5)

            def P1_4(m):
                par = m % 2
                prev = st1.get(m - 2)
                w = [("s", st1[m]["s_r2"])]
                if prev:
                    w.append(("t", prev["t_tr"]))
                st1[m]["v_hb"] = p.vop("tensor_scalar_mul", hb1[par],
                                       scr1[par], stat1[par][:, 4:5], wait=w)

            def P1_5(m):
                par = m % 2
                copy_cnt = {}
                for g in range(KB // 4):
                    pp = ps_tr[g % 2]
                    for j in range(4):
                        k = g * 4 + j
                        w = []
                        if j == 0:
                            if g < 2:
                                w = [("v", st1[m]["v_hb"])]
                                if m > 0:
                                    w.append(("v", st1[m - 1]["tr_copies"]))
                            else:
                                w = [("v", copy_cnt[g - 2])]
                        p.top("transpose", pp[:, j * 128:(j + 1) * 128],
                              hb1[par][:, k * 128:(k + 1) * 128], c_id[:],
                              wait=w, inc=(j == 3))
                    wsl = c_wpre[:, g * 4:g * 4 + 4].unsqueeze(2)                         .broadcast_to([128, 4, TB])
                    copy_cnt[g] = p.vop(
                        "tensor_tensor",
                        hT[:].rearrange("p (k t) -> p k t", t=SEG)
                        [:, g * 4:g * 4 + 4, m * TB:(m + 1) * TB],
                        pp[:].rearrange("p (k t) -> p k t", t=TB),
                        wsl, ALU.mult, wait=[("t", p.cnt["t"])])
                st1[m]["tr_copies"] = copy_cnt[KB // 4 - 1]
                st1[m]["t_tr"] = p.cnt["t"]

            stages1 = [P1_0, P1_1, P1_2, P1_3, P1_4, P1_5]
            for m0 in range(0, NTB, 2):
                for sfn in stages1:
                    sfn(m0)
                    sfn(m0 + 1)
            s_ph1_end = p.cnt["s"]
            t_ph1_end = p.cnt["t"]

        with ExitStack() as ph2:
            E2 = ph2.enter_context
            Fb = arenaB[:, 0:KB * SEG]
            _wqo = {"a": 0, "i": 1, "v": 2}
            Wq = {w: [arenaB[:, 16384 + 2048 * (2 * _wqo[w] + i):
                             16384 + 2048 * (2 * _wqo[w] + i) + 2048]
                      for i in range(2)]
                  for w in ("a", "i", "v")}
            slA = [arenaF[:, 14336:15360], arenaF[:, 15360:16384]]
            slC = [arenaF[:, 16384:17408], arenaF[:, 17408:18432]]
            slE = [scr_big[:, 0:SEG], scr_big[:, SEG:2 * SEG]]
            slB = [hb[:].bitcast(F32)[:, 0:SEG], arenaF[:, 18432:19456]]
            slD = [arenaF[:, 19456:20480], scan_x[:, 0:SEG]]
            slF = [xa_t[:, 0:SEG], xa_t[:, SEG:2 * SEG]]
            slcd = [x0_t[0][:, 0:SEG], x0_t[0][:, SEG:2 * SEG]]
            slcs = [x0_t[1][:, 0:SEG], x0_t[1][:, SEG:2 * SEG]]
            t16s = [[t16[i][:] for i in range(6)],
                    [t16b[:, i * NCH:(i + 1) * NCH] for i in range(6)]]
            ps_a = E2(nc.psum_tensor("ps_a", [128, SEG], F32))
            ps_i = E2(nc.psum_tensor("ps_i", [128, SEG], F32))
            ps_v = E2(nc.psum_tensor("ps_v", [128, SEG], F32))

            v_hT_done = p.cnt["v"]
            t_w_end = {}
            wload = {}

            def load_aiv(o):
                for w in ("a", "i", "v"):
                    tw = [("t", t_w_end.get((w, o - 2), 0))]
                    wload[(w, o)] = p.dma("gpsimd", "w" + w, Wq[w][o % 2],
                                          w_aiv_l[w][:, o, :], wait=tw)

            load_aiv(0)
            psum_free = {"a": ("s", 0), "i": ("s", 0), "v": ("v", 0)}
            v_lp_at = {}
            v_state_w = {}
            v_fb_w = {}
            chain_v_end = {}
            for o in range(KB):
                if o + 1 < KB:
                    load_aiv(o + 1)
                for (w, ps) in (("a", ps_a), ("i", ps_i), ("v", ps_v)):
                    for k in range(KB):
                        for n in range(NMM):
                            waits = []
                            if n == 0 and k == 0:
                                waits = [wload[(w, o)], psum_free[w]]
                                if o == 0 and w == "a":
                                    waits.append(("v", v_hT_done))
                            p.top("matmul", ps[:, n * MMW:(n + 1) * MMW],
                                  Wq[w][o % 2][:, k * 128:(k + 1) * 128],
                                  hT[:, k * SEG + n * MMW:k * SEG + (n + 1) * MMW],
                                  start=(k == 0), stop=(k == KB - 1),
                                  wait=waits, inc=(k == KB - 1 and n == NMM - 1))
                    t_w_end[(w, o)] = p.cnt["t"]
                par = o % 2
                A, B, C = slA[par], slB[par], slC[par]
                Dg, Eg, F = slD[par], slE[par], slF[par]
                tcd, tcs = slcd[par], slcs[par]
                t6 = t16s[par]
                wv_prev = [("v", chain_v_end.get(o - 2, 0))]
                if o < 2:
                    wv_prev += [("v", v_hT_done), ("t", t_ph1_end),
                                ("s", s_ph1_end)]
                s1 = p.sop("activation", A, ps_a[:], AF.Exp, scale=-1.0,
                           bias=c_nbias[:, o:o + 1],
                           wait=[("t", t_w_end[("a", o)])] + wv_prev)
                psum_free["a"] = ("s", s1)
                s2 = p.sop("activation", F, ps_i[:], AF.Exp, scale=-1.0,
                           wait=[("t", t_w_end[("i", o)])])
                psum_free["i"] = ("s", s2)
                p.sop("activation", B, A, AF.Ln, bias=1.0)
                s4 = p.sop("activation", C, B, AF.Exp, scale=-2.0)
                p.sop("activation", A, F, AF.Ln, bias=1.0)
                s6 = p.sop("activation", Dg, A, AF.Exp, scale=-1.0)
                p.vop("tensor_scalar", Eg, C, -1.0, 1.0, ALU.mult,
                      ALU.add, wait=[("s", s4)])
                v_y = p.vop("tensor_scalar_max", Eg, Eg, 1e-8)
                p.sop("activation", C, Eg, AF.Ln, wait=[("v", v_y)])
                s_sq = p.sop("activation", A, C, AF.Exp, scale=0.5)
                p.vop("tensor_mul", Eg, A, Dg,
                      wait=[("s", max(s_sq, s6))])
                v_v = p.vop("tensor_mul", Eg, Eg, ps_v[:],
                            wait=[("t", t_w_end[("v", o)])])
                if o == KB - 1:
                    v_ps_free, s_ps_free = v_v, s2
                psum_free["v"] = ("v", v_v)
                p.vop("tensor_scalar_min", F, B, NLA_CLIP)
                v_scan = p.vop("tensor_tensor_scan", C, c_mask[:],
                               F, 0.0, ALU.mult, ALU.add,
                               wait=[("s", s_sq)])
                s_cd = p.sop("activation", tcd, C, AF.Exp,
                             scale=-1.0, wait=[("v", v_scan)])
                p.vop("tensor_scalar_min", F, C, NLA_CLIP)
                s_rinv = p.sop("activation", A, F, AF.Exp,
                               wait=[("v", p.cnt["v"])])
                p.vop("tensor_mul", Eg, Eg, A,
                      wait=[("s", s_rinv)])
                p.vop("tensor_tensor_scan", tcs, c_mask[:],
                      Eg, 0.0, ALU.mult, ALU.add)
                p.vop("tensor_copy", t6[0], tcd[:, CH - 1:SEG:CH],
                      wait=[("s", s_cd)])
                p.vop("tensor_copy", t6[1], tcs[:, CH - 1:SEG:CH])
                p.vop("tensor_mul", t6[2], t6[0], t6[1])
                p.vop("tensor_tensor_scan", t6[3], t6[0], t6[2], 0.0,
                      ALU.mult, ALU.add)
                p.vop("tensor_tensor_scan", t6[4], t6[0], z16[:], 1.0,
                      ALU.mult, ALU.add)
                p.vop("tensor_copy", LP[:, o:o + 1], t6[3][:, NCH - 1:NCH])
                v_lp_at[o] = p.vop("tensor_copy", LP[:, KB + o:KB + o + 1],
                                   t6[4][:, NCH - 1:NCH])
                p.vop("memset", t6[5][:, 0:1], 0.0)
                if NCH > 1:
                    p.vop("tensor_copy", t6[5][:, 1:NCH], t6[3][:, 0:NCH - 1])
                sin_b = t6[5].unsqueeze(2).broadcast_to([128, NCH, CH])
                p.vop("tensor_tensor",
                      Eg.rearrange("p (c k) -> p c k", k=CH),
                      tcs.rearrange("p (c k) -> p c k", k=CH), sin_b,
                      ALU.add)
                v_state_w[o] = p.vop(
                    "tensor_mul", states[:, o * SEG:(o + 1) * SEG], Eg,
                    tcd,
                    wait=[("s", s_ph1_end), ("t", t_ph1_end)])
                p.vop("memset", t6[5][:, 0:1], 1.0)
                if NCH > 1:
                    p.vop("tensor_copy", t6[5][:, 1:NCH], t6[4][:, 0:NCH - 1])
                pin_b = t6[5].unsqueeze(2).broadcast_to([128, NCH, CH])
                v_fb_w[o] = p.vop(
                    "tensor_tensor",
                    Fb[:, o * SEG:(o + 1) * SEG].rearrange(
                        "p (c k) -> p c k", k=CH),
                    tcd.rearrange("p (c k) -> p c k", k=CH), pin_b,
                    ALU.mult)
                chain_v_end[o] = p.cnt["v"]
            v_scan_end = p.cnt["v"]
            s_scan_end = p.cnt["s"]
            LPg = LP[:].rearrange("p (h o) -> p h o", h=2)
            wo_chunk = []
            for k in range(KB):
                if k < 6:
                    off = 16384 + 2048 * k
                elif k < 10:
                    off = 28672 + 2048 * (k - 6)
                elif k < 12:
                    off = 8192 + 2048 * (k - 10)
                else:
                    off = 2048 * (k - 12)
                wo_chunk.append(arenaB[:, off:off + 2048])
            ldcc_a = p.dma("gpsimd", "dcc", cc_in_a.ap(), LPg[:, :, 0:KSPL],
                           wait=[("v", v_lp_at[KSPL - 1])])
            p.wait("gpsimd", ldcc_a)
            p.op("gpsimd", "collective_compute", "AllGather", ALU.bypass,
                 [[r for r in range(NCORES)]],
                 ins=[cc_in_a.ap().opt()], outs=[cc_out_a.ap().opt()],
                 inc="cc", inc_k=1)
            ldwo = {}
            for k in range(6):
                ldwo[k] = p.dma("gpsimd", "dwo", wo_chunk[k],
                                w_o_r[k * 128:(k + 1) * 128, :],
                                wait=[("t", t_w_end[("v", KB - 1)])])
            ldg_a = p.dma("gpsimd", "dcc", gath[:, :, 0:2 * KSPL],
                          cc_out_a.ap().rearrange("(r p) f -> p r f",
                                                  r=NCORES), wait=[("cc", 1)])
            fix_w = {}

            def carry_chain(o0, o1, g_states, g_prods, wld):
                nn = o1 - o0
                rb = c_rmask[:].unsqueeze(2).broadcast_to([128, NCORES, nn])
                p.gop("tensor_tensor", cr_t1[:, :, o0:o1], g_states, rb,
                      ALU.mult, wait=[wld])
                p.gop("tensor_scalar_add", cr_t2[:, :, o0:o1], g_prods, -1.0)
                p.gop("tensor_tensor", cr_t2[:, :, o0:o1], cr_t2[:, :, o0:o1],
                      rb, ALU.mult)
                p.gop("tensor_scalar_add", cr_t2[:, :, o0:o1],
                      cr_t2[:, :, o0:o1], 1.0)
                p.gop("memset", carry[:, o0:o1], 0.0)
                for r in range(NCORES):
                    p.gop("tensor_mul", cr_tmp[:, o0:o1], cr_t2[:, r, o0:o1],
                          carry[:, o0:o1])
                    p.gop("tensor_add", carry[:, o0:o1], cr_tmp[:, o0:o1],
                          cr_t1[:, r, o0:o1])
                return p.cnt["g"]

            def fix_gp(o):
                cb = carry[:, o:o + 1].broadcast_to([128, SEG])
                p.gop("tensor_tensor", fix_tmp[:],
                      Fb[:, o * SEG:(o + 1) * SEG], cb, ALU.mult,
                      wait=[("v", v_fb_w[o])])
                fix_w[o] = ("g", p.gop(
                    "tensor_add", states[:, o * SEG:(o + 1) * SEG],
                    states[:, o * SEG:(o + 1) * SEG], fix_tmp[:],
                    wait=[("v", v_state_w[o])]))

            def fix_vec(o, g_carry):
                fix_w[o] = ("v", p.vop(
                    "scalar_tensor_tensor",
                    states[:, o * SEG:(o + 1) * SEG],
                    Fb[:, o * SEG:(o + 1) * SEG], carry[:, o:o + 1],
                    states[:, o * SEG:(o + 1) * SEG], ALU.mult, ALU.add,
                    wait=[("g", g_carry)]))

            g_carry_a = carry_chain(0, KSPL, gath[:, :, 0:KSPL],
                                    gath[:, :, KSPL:2 * KSPL], ldg_a)
            for o in range(6):
                fix_gp(o)
            for o in range(6, KSPL):
                fix_vec(o, g_carry_a)
            for k in range(6, 10):
                ldwo[k] = p.dma("gpsimd", "dwo", wo_chunk[k],
                                w_o_r[k * 128:(k + 1) * 128, :],
                                wait=[("v", v_scan_end), ("s", s_scan_end)])
            for k in range(10, 12):
                ldwo[k] = p.dma("gpsimd", "dwo", wo_chunk[k],
                                w_o_r[k * 128:(k + 1) * 128, :],
                                wait=[fix_w[8 + 2 * (k - 10)],
                                      fix_w[9 + 2 * (k - 10)]])
            for k in range(12, KB):
                ldwo[k] = p.dma("gpsimd", "dwo", wo_chunk[k],
                                w_o_r[k * 128:(k + 1) * 128, :],
                                wait=[fix_w[2 * (k - 12)],
                                      fix_w[2 * (k - 12) + 1]])
            ldcc_b = p.dma("gpsimd", "dcc", cc_in_b.ap(), LPg[:, :, KSPL:KB],
                           wait=[("v", v_lp_at[KB - 1])])
            p.wait("gpsimd", ldcc_b)
            p.op("gpsimd", "collective_compute", "AllGather", ALU.bypass,
                 [[r for r in range(NCORES)]],
                 ins=[cc_in_b.ap().opt()], outs=[cc_out_b.ap().opt()],
                 inc="cc", inc_k=1)
            ldg_b = p.dma("gpsimd", "dcc", gath[:, :, 2 * KSPL:2 * KB],
                          cc_out_b.ap().rearrange("(r p) f -> p r f",
                                                  r=NCORES), wait=[("cc", 2)])
            g_carry_b = carry_chain(KSPL, KB, gath[:, :, 2 * KSPL:KSPL + KB],
                                    gath[:, :, KSPL + KB:2 * KB], ldg_b)
            for o in range(KSPL, KB):
                fix_vec(o, g_carry_b)
            g_fix_end = p.cnt["g"]
            v_fix_end = p.cnt["v"]

        h2T = hT
        with ExitStack() as ph4:
            E4 = ph4.enter_context
            ps_o = [E4(nc.psum_tensor("ps_o%d" % i, [128, 512], F32))
                    for i in range(NB)]
            ps_tr = [E4(nc.psum_tensor("ps_tr4%d" % i, [128, 512], BF16))
                     for i in range(2)]
            scr4 = [scr_big[:], arenaF[:, 18432:20480]]
            xa4 = [xa_t[:], arenaF[:, 6144:8192]]
            hb4 = [hb[:], scan_x[:].bitcast(BF16)]
            stat4 = [stat[:], stat4b[:]]
            t_scan_mm = t_w_end[("v", KB - 1)]
            st4 = {}

            def mm4(m):
                par = m % 2
                tmm = {}
                for k in range(KB):
                    for n in range(NB):
                        waits = []
                        if n == 0 and m == 0:
                            waits = [ldwo[k], fix_w[k]]
                            if k == 0:
                                waits += [("s", s_ps_free), ("v", v_ps_free)]
                        elif k == 0 and m > 0:
                            waits = [("v", st4[m - 1]["evn"][n])]
                        p.top("matmul", ps_o[n][:],
                              states[:, k * SEG + m * TB:k * SEG + (m + 1) * TB],
                              wo_chunk[k][:, n * 512:(n + 1) * 512],
                              start=(k == 0), stop=(k == KB - 1),
                              wait=waits,
                              inc=(k == KB - 1))
                        if k == KB - 1:
                            tmm[n] = p.cnt["t"]
                st4.setdefault(m, {})["tmm"] = tmm

            def lh4(m):
                par = m % 2
                prev = st4.get(m - 2)
                w_ldx = [("v", v_scan_end), ("s", s_scan_end)]
                if prev:
                    w_ldx = [("v", prev["v_ln1"]), ("s", prev["s_sq"])]
                ldx = p.dma("sync", "dx", x0_t[par][:],
                            x_in[m * TB:(m + 1) * TB, :], wait=w_ldx)
                w_st = [ldx]
                if prev:
                    w_st += [("v", prev["v_hb"]), ("t", prev["t_tr"])]
                s_r = stats_full(x0_t[par][:], hb4[par], eps_ln, True,
                                 wait=w_st, st=stat4[par])
                st4.setdefault(m, {})["v_ln1"] = p.vop(
                    "tensor_scalar", scr4[par], x0_t[par][:],
                    stat4[par][:, 2:3], stat4[par][:, 4:5],
                    ALU.subtract, ALU.mult, wait=[("s", s_r)])

            def ev4(m):
                par = m % 2
                prev = st4.get(m - 2)
                evn = {}
                for n in range(NB):
                    w_ev = [("t", st4[m]["tmm"][n])]
                    if m == 0:
                        w_ev.append(("s", s_scan_end))
                    if m == 1:
                        w_ev += [("g", g_fix_end), ("v", v_fix_end)]
                    if prev:
                        w_ev += [prev["sp_xa"], ("s", prev["s_sq"])]
                    p.vop("tensor_copy", xa4[par][:, n * 512:(n + 1) * 512],
                          ps_o[n][:], wait=w_ev)
                    evn[n] = p.cnt["v"]
                st4[m]["evn"] = evn
                p.vop("tensor_add", xa4[par], xa4[par], scr4[par])

            def tail4(m):
                par = m % 2
                s_sq = p.sop("activation", x0_t[par][:], xa4[par], AF.Square,
                             accum_out=stat4[par][:, 1:2],
                             wait=[("v", p.cnt["v"])])
                st4[m]["s_sq"] = s_sq
                vv = p.vop("tensor_scalar", stat4[par][:, 6:7],
                           stat4[par][:, 1:2], 1.0 / D, eps_rms, ALU.mult,
                           ALU.add, wait=[("s", s_sq)])
                p.sop("activation", stat4[par][:, 5:6], stat4[par][:, 6:7],
                      AF.Ln, wait=[("v", vv)])
                s_r2 = p.sop("activation", stat4[par][:, 4:5],
                             stat4[par][:, 5:6], AF.Exp, scale=-0.5)
                v_hb = p.vop("tensor_scalar_mul", hb4[par], xa4[par],
                             stat4[par][:, 4:5], wait=[("s", s_r2)])
                st4[m]["v_hb"] = v_hb
                copy_cnt = {}
                for g in range(KB // 4):
                    pp = ps_tr[g % 2]
                    for j in range(4):
                        k = g * 4 + j
                        w = []
                        if j == 0:
                            if g < 2:
                                w = [("v", v_hb)]
                                if m > 0:
                                    w.append(("v", st4[m - 1]["tr_copies"]))
                            else:
                                w = [("v", copy_cnt[g - 2])]
                        p.top("transpose", pp[:, j * 128:(j + 1) * 128],
                              hb4[par][:, k * 128:(k + 1) * 128], c_id[:],
                              wait=w, inc=(j == 3))
                    wsl = c_wff[:, g * 4:g * 4 + 4].unsqueeze(2)                         .broadcast_to([128, 4, TB])
                    copy_cnt[g] = p.vop(
                        "tensor_tensor",
                        h2T[:].rearrange("p (k t) -> p k t", t=SEG)
                        [:, g * 4:g * 4 + 4, m * TB:(m + 1) * TB],
                        pp[:].rearrange("p (k t) -> p k t", t=TB),
                        wsl, ALU.mult,
                        wait=[("t", p.cnt["t"]), ("t", t_scan_mm)])
                st4[m]["tr_copies"] = copy_cnt[KB // 4 - 1]
                st4[m]["t_tr"] = p.cnt["t"]
                st4[m]["sp_xa"] = p.dma("gpsimd", "dxa",
                                        xa_sp.ap()[m * TB:(m + 1) * TB, :],
                                        xa4[par], wait=[("v", v_hb)])

            mm4(0)
            lh4(0)
            for m in range(NTB):
                ev4(m)
                if m + 1 < NTB:
                    lh4(m + 1)
                    mm4(m + 1)
                tail4(m)

        def gu_down_phase(tag, srcT, acc_ap, gate_src, up_src, dn_rhs_dma,
                          fuse_buf, dn_ring, t_silu, wg_t, wu_t,
                          sg_col=None, su_col=None):
            v_src_done = p.cnt["v"]
            t_entry = p.cnt["t"]
            s_entry = p.cnt["s"]
            dx1_entry = p.cnt["dx1s0"]
            ev_bank = {m: 0 for m in range(NTB)}
            ev_all = 0
            evac_final = [0] * NTB
            t_gu_last = t_entry
            lastpar = {0: t_entry, 1: t_entry}
            lastring = {}
            for q in range(NQ):
                with ExitStack() as pq:
                    ps_g = [pq.enter_context(
                        nc.psum_tensor("ps_g%s_%d_%d" % (tag, q, i),
                                       [128, SEG], F32)) for i in range(2)]
                    ps_u = [pq.enter_context(
                        nc.psum_tensor("ps_u%s_%d_%d" % (tag, q, i),
                                       [128, SEG], F32)) for i in range(2)]
                    sgv = {}
                    fu = {}
                    wl = {}

                    def load_pair(oi, q=q, wl=wl):
                        o = q * OHQ + oi
                        par = oi % 2
                        tw = [("t", lastpar[par])]
                        if q == 0 and oi == 0:
                            tw = [("t", t_entry), ("v", v_src_done),
                                  ("s", s_entry)]
                        a_ = p.dma("gpsimd", "wg", wg_t[par],
                                   gate_src[:, o, :], wait=tw)
                        b_ = p.dma("gpsimd", "wu", wu_t[par], up_src[:, o, :])
                        wl[oi] = (a_, b_)

                    load_pair(0)
                    for oi in range(OHQ):
                        o = q * OHQ + oi
                        par = oi % 2
                        if oi + 1 < OHQ:
                            load_pair(oi + 1)
                        for (wt, ps, wld) in ((wg_t[par], ps_g[par], wl[oi][0]),
                                              (wu_t[par], ps_u[par], wl[oi][1])):
                            for k in range(KB):
                                for n in range(NMM):
                                    waits = []
                                    if n == 0 and k == 0:
                                        waits = [wld, ("v", v_src_done),
                                                 ("s", sgv.get(par, 0)),
                                                 ("v", max(fu.get(par, 0),
                                                           ev_all))]
                                    p.top("matmul",
                                          ps[:, n * MMW:(n + 1) * MMW],
                                          wt[:, k * 128:(k + 1) * 128],
                                          srcT[:, k * SEG + n * MMW:
                                               k * SEG + (n + 1) * MMW],
                                          start=(k == 0), stop=(k == KB - 1),
                                          wait=waits,
                                          inc=(k == KB - 1 and n == NMM - 1))
                        wl[("t", oi)] = p.cnt["t"]
                        lastpar[par] = p.cnt["t"]
                        silu_sl = t_silu[:, par * SEG:(par + 1) * SEG]
                        guard = ([("v", v_src_done), ("dx1s0", dx1_entry)]
                                 if (q == 0 and oi == 0) else [])
                        if sg_col is None:
                            s_g = p.sop("activation", silu_sl, ps_g[par][:],
                                        AF.Silu, wait=[("t", wl[("t", oi)]),
                                                       ("v", fu.get(par, 0))]
                                        + guard)
                            f_ = p.vop("tensor_mul",
                                       fuse_buf[:, oi * SEG:(oi + 1) * SEG],
                                       silu_sl, ps_u[par][:],
                                       wait=[("s", s_g)])
                        else:
                            s_g = p.sop("activation", silu_sl, ps_g[par][:],
                                        AF.Silu, scale=sg_col[:, o:o + 1],
                                        wait=[("t", wl[("t", oi)]),
                                              ("v", fu.get(par, 0))] + guard)
                            f_ = p.vop("scalar_tensor_tensor",
                                       fuse_buf[:, oi * SEG:(oi + 1) * SEG],
                                       ps_u[par][:], su_col[:, o:o + 1],
                                       silu_sl, ALU.mult, ALU.mult,
                                       wait=[("s", s_g)])
                        sgv[par] = s_g
                        fu[par] = f_
                    v_fuse_done = p.cnt["v"]
                    s_last = p.cnt["s"]
                    if q == NQ - 1:
                        t_gu_last = p.cnt["t"]
                with ExitStack() as pd:
                    ps_d = [pd.enter_context(
                        nc.psum_tensor("ps_d%s_%d_%d" % (tag, q, i),
                                       [128, 512], F32))
                        for i in range(min(8, NTB))]
                    for n in range(NB):
                        tm = {}
                        for k in range(KHQ):
                            r = k % len(dn_ring)
                            ring = dn_ring[r]
                            tw = [("t", lastring.get(r, t_entry))]
                            ldd = dn_rhs_dma(q, n, k, ring, tw)
                            for m in range(NTB):
                                waits = []
                                if m == 0:
                                    waits = [ldd]
                                if k == 0:
                                    waits = list(waits) + [("v", ev_bank[m])]
                                    if n == 0:
                                        waits += [("v", v_fuse_done),
                                                  ("s", s_last)]
                                p.top("matmul", ps_d[m % 8][:],
                                      fuse_buf[:, k * SEG + m * TB:
                                               k * SEG + (m + 1) * TB],
                                      ring,
                                      start=(k == 0), stop=(k == KHQ - 1),
                                      wait=waits,
                                      inc=(m == NTB - 1 or k == KHQ - 1))
                                if k == KHQ - 1:
                                    tm[m] = p.cnt["t"]
                            lastring[r] = p.cnt["t"]
                        for m in range(NTB):
                            dst = acc_ap[:, m * D + n * 512:m * D + (n + 1) * 512]
                            if q == 0:
                                p.vop("tensor_copy", dst, ps_d[m % 8][:],
                                      wait=[("t", tm[m]),
                                            ("dxa0", 16 * NTB)])
                            else:
                                p.vop("tensor_add", dst, dst, ps_d[m % 8][:],
                                      wait=[("t", tm[m])])
                            ev_bank[m] = p.cnt["v"]
                            if q == NQ - 1 and n == NB - 1:
                                evac_final[m] = p.cnt["v"]
                    ev_all = max(ev_bank.values())
            return dict(ev_all=ev_all, evac_final=evac_final,
                        t_gu_last=t_gu_last)

        with ExitStack() as ph5:
            E5 = ph5.enter_context
            assert OHQ * SEG == KB * SEG
            fuse5 = states[:, 0:OHQ * SEG]
            fuse7 = hT[:, 0:OHQ * SEG]
            acc = arenaF[:, 0:NTB * D]
            t_silu = scr_big[:]
            KW = KB * 128
            x0b = [x0_t[i][:].bitcast(BF16) for i in range(2)]
            wg_t = [x0b[0][:, 0:KW], x0b[0][:, KW:2 * KW]]
            wu_t = [x0b[1][:, 0:KW], x0b[1][:, KW:2 * KW]]
            nring = max(1, min(4, D // 512))
            dn_ring = [hb[:, r * 512:(r + 1) * 512] for r in range(nring)]

            def dn_dma(qq, n, k, ring, tw):
                row0 = qq * (FFN // NQ) + k * 128
                return p.dma("gpsimd", "dn", ring,
                             w_dn_r[row0:row0 + 128, n * 512:(n + 1) * 512],
                             wait=tw)

            info5 = gu_down_phase('5', h2T, acc, w_gu_l[:, 0:OBF, :],
                                  w_gu_l[:, OBF:2 * OBF, :], dn_dma,
                                  fuse5, dn_ring, t_silu, wg_t, wu_t)

            ln2T = states
            t_p5_end = p.cnt["t"]
            s_p5_end = p.cnt["s"]
            t_gu5 = info5["t_gu_last"]
            ev5 = info5["evac_final"]
            ev5_all = info5["ev_all"]
            hTf = hT[:].bitcast(F32)
            scrb16 = scr_big[:].bitcast(BF16)
            scr6 = [hTf[:, 0:2048], hTf[:, 2048:4096]]
            xat6 = [hTf[:, 4096:6144], xa_t[:]]
            stat6 = [stat[:], hTf[:, 6144:6152]]
            hb6 = [scrb16[:, 0:2048], scrb16[:, 2048:4096]]
            with ExitStack() as ph6:
                ps_tr6 = [ph6.enter_context(
                    nc.psum_tensor("ps_tr6%d" % i, [128, 512], BF16))
                    for i in range(2)]
                st6 = {}

                def S6_0(m):
                    par = m % 2
                    buf = x0_t[par]
                    prev = st6.get(m - 2)
                    w = [("t", t_gu5), ("dxa0", 16 * (m + 1))]
                    if prev:
                        w += [("v", prev["v_x1"])]
                    ldxa = p.dma("sync", "dx", buf[:],
                                 xa_sp.ap()[m * TB:(m + 1) * TB, :], wait=w)
                    w2 = [ldxa, ("v", ev5[m]), ("t", t_gu5)]
                    if prev:
                        w2 += [("s", prev["s_end"]), prev["sp_x1"]]
                    v_ao = p.vop("tensor_add", scr6[par], buf[:],
                                 acc[:, m * D:(m + 1) * D], wait=w2)
                    st6[m] = dict(v_ao=v_ao)

                def S6_1(m):
                    par = m % 2
                    prev = st6.get(m - 2)
                    w = [("v", st6[m]["v_ao"])]
                    if prev:
                        w.append(("v", prev["v_hb"]))
                    st6[m]["s_r"] = stats_full(scr6[par], x0_t[par][:],
                                               eps_ln, True, wait=w,
                                               st=stat6[par])

                def S6_2(m):
                    par = m % 2
                    s_r = st6[m]["s_r"]
                    p.vop("tensor_scalar", xat6[par], scr6[par],
                          stat6[par][:, 2:3], stat6[par][:, 4:5],
                          ALU.subtract, ALU.mult,
                          wait=[("s", s_r), ("dxa0", 16 * NTB)])
                    st6[m]["ldx0"] = p.dma("sync", "dx", x0_t[par][:],
                                           x_in[m * TB:(m + 1) * TB, :],
                                           wait=[("s", s_r)])

                def S6_3(m):
                    par = m % 2
                    v_x1 = p.vop("tensor_add", scr6[par], xat6[par],
                                 x0_t[par][:], wait=[st6[m]["ldx0"]])
                    st6[m]["v_x1"] = v_x1
                    st6[m]["sp_x1"] = p.dma("gpsimd", "dx1s",
                                            x1_sp.ap()[m * TB:(m + 1) * TB, :],
                                            scr6[par], wait=[("v", v_x1)])
                    st6[m]["s_r2"] = stats_full(scr6[par], xat6[par], eps_ln,
                                                True, wait=[("v", v_x1)],
                                                st=stat6[par])
                    st6[m]["s_end"] = p.cnt["s"]

                def S6_4(m):
                    par = m % 2
                    prev = st6.get(m - 2)
                    w = [("s", st6[m]["s_r2"]), ("s", s_p5_end)]
                    if prev:
                        w.append(("t", prev["t_tr"]))
                    st6[m]["v_hb"] = p.vop(
                        "tensor_scalar", hb6[par], scr6[par],
                        stat6[par][:, 2:3], stat6[par][:, 4:5],
                        ALU.subtract, ALU.mult, wait=w)

                def S6_5(m):
                    par = m % 2
                    copy_cnt = {}
                    for g in range(KB // 4):
                        pp = ps_tr6[g % 2]
                        for j in range(4):
                            k = g * 4 + j
                            w = []
                            if j == 0:
                                if g < 2:
                                    w = [("v", st6[m]["v_hb"])]
                                    if m == 0:
                                        w.append(("v", ev5_all))
                                    else:
                                        w.append(("v",
                                                  st6[m - 1]["tr_copies"]))
                                else:
                                    w = [("v", copy_cnt[g - 2])]
                            p.top("transpose", pp[:, j * 128:(j + 1) * 128],
                                  hb6[par][:, k * 128:(k + 1) * 128],
                                  c_id[:], wait=w, inc=(j == 3))
                        copy_cnt[g] = p.vop(
                            "tensor_copy",
                            ln2T[:].rearrange("p (k t) -> p k t", t=SEG)
                            [:, g * 4:g * 4 + 4, m * TB:(m + 1) * TB],
                            pp[:].rearrange("p (k t) -> p k t", t=TB),
                            wait=[("t", p.cnt["t"]), ("t", t_p5_end)])
                    st6[m]["tr_copies"] = copy_cnt[KB // 4 - 1]
                    st6[m]["t_tr"] = p.cnt["t"]

                stages6 = [S6_0, S6_1, S6_2, S6_3, S6_4, S6_5]
                for m0 in range(0, NTB, 2):
                    for sfn in stages6:
                        sfn(m0)
                        sfn(m0 + 1)

            def dn8_dma(qq, n, k, ring, tw):
                row0 = qq * (FFN // NQ) + k * 128
                return p.dma("gpsimd", "dn", ring,
                             wd8_r[n, row0:row0 + 128, :], wait=tw)

            info7 = gu_down_phase('7', ln2T, acc, wg8_l, wu8_l, dn8_dma,
                                  fuse7, dn_ring, t_silu, wg_t, wu_t,
                                  sg_col=c_sg[:], su_col=c_su[:])

            t_p7_end = p.cnt["t"]
            t_gu7 = info7["t_gu_last"]
            ev7 = info7["evac_final"]
            statesf = states[:].bitcast(F32)
            scr8 = [hTf[:, 0:2048], hTf[:, 2048:4096]]
            xat8 = [hTf[:, 4096:6144], xa_t[:]]
            stat8 = [stat[:], hTf[:, 6144:6152]]
            dump8 = [statesf[:, 0:2048], statesf[:, 2048:4096]]
            st8 = {}

            def S8_0(m):
                par = m % 2
                prev = st8.get(m - 2)
                w = [("t", t_gu7), ("dx1s0", 16 * (m + 1))]
                if prev:
                    w.append(("v", prev["v_o"]))
                ldx1 = p.dma("sync", "dx", x0_t[par][:],
                             x1_sp.ap()[m * TB:(m + 1) * TB, :], wait=w)
                w2 = [("v", ev7[m]), ld_sdb, ("t", t_p7_end)]
                if prev:
                    w2 += [("s", prev["s_end"]), prev["dy"]]
                v_y3 = p.vop("tensor_mul", scr8[par],
                             acc[:, m * D:(m + 1) * D], sdb[:], wait=w2)
                st8[m] = dict(ldx1=ldx1, v_y3=v_y3)

            def S8_1(m):
                par = m % 2
                st8[m]["s_r"] = stats_full(scr8[par], dump8[par], eps_ln,
                                           True, wait=[("v", st8[m]["v_y3"])],
                                           st=stat8[par])
                st8[m]["s_end"] = p.cnt["s"]

            def S8_2(m):
                par = m % 2
                p.vop("tensor_scalar", xat8[par], scr8[par],
                      stat8[par][:, 2:3], stat8[par][:, 4:5],
                      ALU.subtract, ALU.mult, wait=[("s", st8[m]["s_r"])])

            def S8_3(m):
                par = m % 2
                v_o = p.vop("tensor_add", scr8[par], xat8[par], x0_t[par][:],
                            wait=[st8[m]["ldx1"]])
                st8[m]["v_o"] = v_o
                st8[m]["dy"] = p.dma("sync", "dy",
                                     y_out[m * TB:(m + 1) * TB, :],
                                     scr8[par], wait=[("v", v_o)])

            stages8 = [S8_0, S8_1, S8_2, S8_3]
            for m0 in range(0, NTB, 2):
                for sfn in stages8:
                    sfn(m0)
                    sfn(m0 + 1)
            p.wait("sync", st8[NTB - 2]["dy"], st8[NTB - 1]["dy"])

        with nc.Block() as block:
            p.finalize(block)

    return nc


def _lhsT_layout(W, dtype):
    Din, Dout = W.shape
    KBl, OBl = Din // 128, Dout // 128
    return np.ascontiguousarray(
        W.reshape(KBl, 128, OBl, 128).transpose(1, 2, 0, 3)
        .reshape(128, OBl, KBl * 128)).astype(dtype, copy=False)


def _colT(vec):
    Dl = vec.shape[0]
    return np.ascontiguousarray(vec.reshape(Dl // 128, 128).T)


def make_inputs(cfg, fi):
    D, FFN, SEG, NSEG, Bb = (cfg[k] for k in ("D", "FFN", "SEG", "NSEG", "B"))
    NCORES = Bb * NSEG
    NB = D // 512
    x = np.asarray(fi["L0_attn_x"], np.float32)

    shared = {
        "w_a_l": _lhsT_layout(np.asarray(fi["L0_attn_w_a"], np.float32), np.float32),
        "w_i_l": _lhsT_layout(np.asarray(fi["L0_attn_w_i"], np.float32), np.float32),
        "w_v_l": _lhsT_layout(np.asarray(fi["L0_attn_w_v"], np.float32), np.float32),
        "w_o_r": np.ascontiguousarray(np.asarray(fi["L0_attn_out_proj_w"], np.float32)),
        "w_gu_l": _lhsT_layout(np.asarray(fi["L0_attn_w_gate_up"], np.float32),
                               np.float32),
        "w_dn_r": np.ascontiguousarray(np.asarray(fi["L0_attn_w_down"], np.float32)),
        "wg8_l": _lhsT_layout(np.asarray(fi["L0_ffn_w_gate"], np.int8), np.int8),
        "wu8_l": _lhsT_layout(np.asarray(fi["L0_ffn_w_up"], np.int8), np.int8),
        "wd8_r": np.ascontiguousarray(
            np.asarray(fi["L0_ffn_w_down"], np.int8).reshape(FFN, NB, 512)
            .transpose(1, 0, 2)),
        "biasT": _colT(np.asarray(fi["L0_attn_decay_bias"], np.float32)),
        "wpreT": _colT(np.asarray(fi["L0_attn_pre_norm_w"], np.float32)),
        "wffT": _colT(np.asarray(fi["L0_attn_ffn_norm_w"], np.float32)),
        "sgT": _colT(np.asarray(fi["L0_ffn_s_gate"], np.float32)),
        "suT": _colT(np.asarray(fi["L0_ffn_s_up"], np.float32)),
        "sd_row": np.ascontiguousarray(
            np.asarray(fi["L0_ffn_s_down"], np.float32)[None, :]),
        "identity": np.eye(128, dtype=BF),
        "ones_row": np.ones((1, 128), BF),
    }
    scanmask = np.ones((128, SEG), np.float32)
    scanmask[:, ::CH] = 0.0
    shared["scanmask"] = scanmask

    in_maps = []
    for c in range(NCORES):
        b, s0 = divmod(c, NSEG)
        rmask = np.zeros((128, NCORES), np.float32)
        for r in range(NCORES):
            if r // NSEG == b and (r % NSEG) < s0:
                rmask[:, r] = 1.0
        m = dict(shared)
        m["x"] = np.ascontiguousarray(x[b, s0 * SEG:(s0 + 1) * SEG, :])
        m["rankmask"] = rmask
        in_maps.append(m)
    return in_maps


def kernel(**inputs):
    cfg = FULL
    NCORES = cfg["B"] * cfg["NSEG"]
    nc = build_nc(cfg)
    in_maps = make_inputs(cfg, inputs)
    res = run_bass_kernel_spmd(nc, in_maps, core_ids=list(range(NCORES)))
    SEG, D = cfg["SEG"], cfg["D"]
    out = np.empty((cfg["B"], cfg["NSEG"] * SEG, D), np.float32)
    for c in range(NCORES):
        b, s0 = divmod(c, cfg["NSEG"])
        out[b, s0 * SEG:(s0 + 1) * SEG, :] = res.results[c]["y"]
    return out


# version 18 (final)
# speedup vs baseline: 1.0656x; 1.0069x over previous
from contextlib import ExitStack

import numpy as np
import ml_dtypes

import concourse.bass as bass
import concourse.mybir as mybir
from concourse.bass_utils import run_bass_kernel_spmd

F32 = mybir.dt.float32
BF16 = mybir.dt.bfloat16
I8 = mybir.dt.int8
AF = mybir.ActivationFunctionType
ALU = mybir.AluOpType
BF = ml_dtypes.bfloat16

NLA_CLIP = 23.025851
CH = 64
TB = 128

FULL = dict(D=2048, FFN=8192, SEG=1024, NSEG=4, B=2)
MINI = dict(D=512, FFN=2048, SEG=256, NSEG=2, B=2)

DMA_STREAMS = {"dC": 1, "dx": 4, "wa": 2, "wi": 2, "wv": 2, "dwo": 16,
               "wg": 2, "wu": 2, "dn": 4, "dxa": 2, "dx1s": 2, "dcc": 1,
               "dy": 2}


class Prog:

    ENGINES = ("sync", "scalar", "vector", "tensor", "gpsimd")

    def __init__(self, nc):
        self.nc = nc
        self.ops = {e: [] for e in self.ENGINES}
        self.cnt = {}
        self.sems = {}
        self.duse = {}
        self._waited = {e: {} for e in self.ENGINES}

    def make_sems(self, ctx, names):
        for n in names:
            self.sems[n] = ctx.enter_context(self.nc.semaphore("sem_" + n))
            self.cnt[n] = 0

    def make_dma_streams(self, ctx, streams):
        for s, n in streams.items():
            for i in range(n):
                name = s + str(i)
                self.sems[name] = ctx.enter_context(
                    self.nc.semaphore("sem_" + name))
                self.cnt[name] = 0
            self.duse[s] = 0

    def _dedup(self, engine, waits):
        wl = []
        for w in waits:
            if w is None:
                continue
            sname, val = w
            if val is None or val <= 0:
                continue
            if self._waited[engine].get(sname, 0) >= val:
                continue
            self._waited[engine][sname] = val
            wl.append((self.sems[sname], val))
        return wl

    def op(self, engine, method, *args, wait=(), inc=None, inc_k=1, **kwargs):
        wl = self._dedup(engine, list(wait))
        ret = None
        sem = None
        if inc is not None:
            self.cnt[inc] += inc_k
            ret = self.cnt[inc]
            sem = self.sems[inc]

        def go(eng, wl=wl, method=method, args=args, kwargs=kwargs, sem=sem,
               inc_k=inc_k):
            for (s, val) in wl:
                eng.wait_ge(s, val)
            inst = getattr(eng, method)(*args, **kwargs)
            if sem is not None:
                inst.then_inc(sem, inc_k)

        self.ops[engine].append(go)
        return ret

    def wait(self, engine, *waits):
        wl = self._dedup(engine, list(waits))
        if wl:
            def go(eng, wl=wl):
                for (s, val) in wl:
                    eng.wait_ge(s, val)
            self.ops[engine].append(go)

    def vop(self, method, *args, wait=(), **kwargs):
        return self.op("vector", method, *args,
                       wait=[("v", self.cnt["v"])] + list(wait), inc="v", **kwargs)

    def sop(self, method, *args, wait=(), **kwargs):
        return self.op("scalar", method, *args,
                       wait=[("s", self.cnt["s"])] + list(wait), inc="s", **kwargs)

    def gop(self, method, *args, wait=(), **kwargs):
        return self.op("gpsimd", method, *args,
                       wait=[("g", self.cnt["g"])] + list(wait), inc="g", **kwargs)

    def top(self, method, *args, wait=(), inc=False, **kwargs):
        return self.op("tensor", method, *args, wait=wait,
                       inc=("t" if inc else None), **kwargs)

    def dma(self, queue, stream, dst, src, wait=()):
        n = DMA_STREAMS[stream]
        use = self.duse[stream]
        self.duse[stream] = use + 1
        semname = stream + str(use % n)
        self.op(queue, "dma_start", dst, src, wait=wait, inc=semname, inc_k=16)
        return (semname, self.cnt[semname])

    def finalize(self, block):
        for e in self.ENGINES:
            fns = self.ops[e]
            if not fns:
                continue

            def emit(eng, fns=fns):
                for f in fns:
                    f(eng)

            getattr(block, e)(emit)


def build_nc(cfg):
    D, FFN, SEG = cfg["D"], cfg["FFN"], cfg["SEG"]
    KB = D // 128
    NTB = SEG // TB
    NCH = SEG // CH
    OBF = FFN // 128
    NB = D // 512
    NCORES = cfg["B"] * cfg["NSEG"]
    NQ = 4 if OBF % 8 == 0 else 2
    OHQ = OBF // NQ
    KHQ = OHQ
    MMW = min(SEG, 512)
    NMM = SEG // MMW
    eps_ln, eps_rms = 1e-5, 1e-6

    nc = bass.Bass("TRN2", target_bir_lowering=False)

    def din(name, shape, dt=F32):
        return nc.dram_tensor(name, list(shape), dt, kind="ExternalInput").ap()

    x_in = din("x", (SEG, D))
    w_aiv_l = {w: din("w_%s_l" % w, (128, KB, KB * 128)) for w in ("a", "i", "v")}
    w_o_r = din("w_o_r", (D, D))
    w_gu_l = din("w_gu_l", (128, 2 * OBF, KB * 128))
    w_dn_r = din("w_dn_r", (FFN, D))
    wg8_l = din("wg8_l", (128, OBF, KB * 128), I8)
    wu8_l = din("wu8_l", (128, OBF, KB * 128), I8)
    wd8_r = din("wd8_r", (NB, FFN, 512), I8)
    biasT = din("biasT", (128, KB))
    wpreT = din("wpreT", (128, KB))
    wffT = din("wffT", (128, KB))
    sgT = din("sgT", (128, OBF))
    suT = din("suT", (128, OBF))
    sd_row = din("sd_row", (1, D))
    scanmask = din("scanmask", (128, SEG))
    identity = din("identity", (128, 128), BF16)
    ones_row = din("ones_row", (1, 128), BF16)
    rankmask = din("rankmask", (128, NCORES))
    y_out = nc.dram_tensor("y", [SEG, D], F32, kind="ExternalOutput").ap()

    KSPL = 12
    cc_in_a = nc.dram_tensor("cc_in_a", [128, 2 * KSPL], F32)
    cc_out_a = nc.dram_tensor("cc_out_a", [NCORES * 128, 2 * KSPL], F32)
    cc_in_b = nc.dram_tensor("cc_in_b", [128, 2 * (KB - KSPL)], F32)
    cc_out_b = nc.dram_tensor("cc_out_b", [NCORES * 128, 2 * (KB - KSPL)], F32)
    xa_sp = nc.dram_tensor("xa_sp", [SEG, D], F32)
    x1_sp = nc.dram_tensor("x1_sp", [SEG, D], F32)

    with ExitStack() as ctx:
        E = ctx.enter_context
        p = Prog(nc)
        p.make_sems(ctx, ["v", "s", "g", "t", "cc"])
        p.make_dma_streams(ctx, DMA_STREAMS)

        hT = E(nc.sbuf_tensor("hT", [128, KB * SEG], BF16))
        states = E(nc.sbuf_tensor("states", [128, KB * SEG], BF16))
        arena = E(nc.sbuf_tensor("arena", [128, 20480], F32))
        arenaF = arena[:]
        arenaB = arena[:].bitcast(BF16)
        c_bias = E(nc.sbuf_tensor("c_bias", [128, KB], F32))
        c_nbias = E(nc.sbuf_tensor("c_nbias", [128, KB], F32))
        c_wpre = E(nc.sbuf_tensor("c_wpre", [128, KB], F32))
        c_wff = E(nc.sbuf_tensor("c_wff", [128, KB], F32))
        c_sg = E(nc.sbuf_tensor("c_sg", [128, OBF], F32))
        c_su = E(nc.sbuf_tensor("c_su", [128, OBF], F32))
        c_mask = E(nc.sbuf_tensor("c_mask", [128, SEG], F32))
        c_id = E(nc.sbuf_tensor("c_id", [128, 128], BF16))
        c_ones = E(nc.sbuf_tensor("c_ones", [1, 128], BF16))
        c_rmask = E(nc.sbuf_tensor("c_rmask", [128, NCORES], F32))
        sdb = E(nc.sbuf_tensor("sdb", [128, D], F32))
        LP = E(nc.sbuf_tensor("LP", [128, 2 * KB], F32))
        gath = E(nc.sbuf_tensor("gath", [128, NCORES, 2 * KB], F32))
        carry = E(nc.sbuf_tensor("carry", [128, KB], F32))
        cr_t1 = E(nc.sbuf_tensor("cr_t1", [128, NCORES, KB], F32))
        cr_t2 = E(nc.sbuf_tensor("cr_t2", [128, NCORES, KB], F32))
        cr_tmp = E(nc.sbuf_tensor("cr_tmp", [128, KB], F32))
        fix_tmp = E(nc.sbuf_tensor("fix_tmp", [128, SEG], BF16))
        scan_x = E(nc.sbuf_tensor("scan_x", [128, SEG], F32))
        t16b = E(nc.sbuf_tensor("t16b", [128, 6 * (SEG // CH)], F32))
        xa_t = E(nc.sbuf_tensor("xa_t", [128, D], F32))
        stat4b = E(nc.sbuf_tensor("stat4b", [128, 8], F32))
        x0_t = [E(nc.sbuf_tensor("x0_t%d" % i, [128, D], F32)) for i in range(2)]
        stat = E(nc.sbuf_tensor("stat", [128, 8], F32))
        scr_big = E(nc.sbuf_tensor("scr_big", [128, D], F32))
        hb = E(nc.sbuf_tensor("hb", [128, D], BF16))
        t16 = [E(nc.sbuf_tensor("t16_%d" % i, [128, NCH], F32)) for i in range(6)]
        z16 = E(nc.sbuf_tensor("z16", [128, NCH], F32))

        p.dma("sync", "dC", c_bias[:], biasT)
        p.dma("sync", "dC", c_wpre[:], wpreT)
        p.dma("sync", "dC", c_wff[:], wffT)
        p.dma("sync", "dC", c_sg[:], sgT)
        p.dma("sync", "dC", c_su[:], suT)
        p.dma("sync", "dC", c_mask[:], scanmask)
        p.dma("sync", "dC", c_id[:], identity)
        p.dma("sync", "dC", c_ones[:], ones_row)
        p.dma("sync", "dC", c_rmask[:], rankmask)
        ld_sdb = p.dma("sync", "dC", sdb[:],
                       sd_row.broadcast_to([128, D]))
        ldC = ld_sdb
        p.vop("memset", z16[:], 0.0)
        p.vop("tensor_scalar_mul", c_nbias[:], c_bias[:], -1.0, wait=[ldC])

        def stats_full(src_ap, dump_ap, eps, mean_too, wait=(), st=None):
            if st is None:
                st = stat[:]
            if mean_too:
                p.sop("activation", dump_ap, src_ap, AF.Identity,
                      accum_out=st[:, 0:1], wait=list(wait))
                wait = ()
            p.sop("activation", dump_ap, src_ap, AF.Square,
                  accum_out=st[:, 1:2], wait=list(wait))
            sw = [("s", p.cnt["s"])]
            if mean_too:
                p.vop("tensor_scalar_mul", st[:, 2:3], st[:, 0:1], 1.0 / D,
                      wait=sw)
                p.vop("tensor_mul", st[:, 3:4], st[:, 2:3], st[:, 2:3])
                p.vop("tensor_scalar", st[:, 6:7], st[:, 1:2], 1.0 / D,
                      st[:, 3:4], ALU.mult, ALU.subtract)
                vv = p.vop("tensor_scalar_add", st[:, 6:7], st[:, 6:7], eps)
            else:
                vv = p.vop("tensor_scalar", st[:, 6:7], st[:, 1:2], 1.0 / D,
                           eps, ALU.mult, ALU.add, wait=sw)
            p.sop("activation", st[:, 5:6], st[:, 6:7], AF.Ln,
                  wait=[("v", vv)])
            return p.sop("activation", st[:, 4:5], st[:, 5:6], AF.Exp,
                         scale=-0.5)

        with ExitStack() as ph1:
            ps_tr = [ph1.enter_context(nc.psum_tensor("ps_tr%d" % i, [128, 512],
                                                      BF16)) for i in range(2)]
            statesP1 = states[:].bitcast(F32)
            scr1 = [scr_big[:], statesP1[:, 0:2048]]
            stat1 = [stat[:], statesP1[:, 2048:2056]]
            hb1 = [hb[:], states[:][:, 4112:6160]]
            st1 = {}

            def P1_0(m):
                par = m % 2
                prev = st1.get(m - 2)
                w = []
                if prev:
                    w = [("s", prev["s_sq"]), ("v", prev["v_ln1"])]
                st1[m] = dict(ldx=p.dma("sync", "dx", x0_t[par][:],
                                        x_in[m * TB:(m + 1) * TB, :], wait=w))

            def P1_1(m):
                par = m % 2
                prev = st1.get(m - 2)
                w = [st1[m]["ldx"]]
                if prev:
                    w.append(("v", prev["v_hb"]))
                st1[m]["s_r"] = stats_full(x0_t[par][:], scr1[par], eps_ln,
                                           True, wait=w, st=stat1[par])

            def P1_2(m):
                par = m % 2
                st1[m]["v_ln1"] = p.vop("tensor_scalar", scr1[par],
                                        x0_t[par][:], stat1[par][:, 2:3],
                                        stat1[par][:, 4:5], ALU.subtract,
                                        ALU.mult, wait=[("s", st1[m]["s_r"])])

            def P1_3(m):
                par = m % 2
                st1[m]["s_sq"] = p.sop("activation", x0_t[par][:], scr1[par],
                                       AF.Square, accum_out=stat1[par][:, 1:2],
                                       wait=[("v", st1[m]["v_ln1"])])
                vv = p.vop("tensor_scalar", stat1[par][:, 6:7],
                           stat1[par][:, 1:2], 1.0 / D, eps_rms, ALU.mult,
                           ALU.add, wait=[("s", p.cnt["s"])])
                p.sop("activation", stat1[par][:, 5:6], stat1[par][:, 6:7],
                      AF.Ln, wait=[("v", vv)])
                st1[m]["s_r2"] = p.sop("activation", stat1[par][:, 4:5],
                                       stat1[par][:, 5:6], AF.Exp, scale=-0.5)

            def P1_4(m):
                par = m % 2
                prev = st1.get(m - 2)
                w = [("s", st1[m]["s_r2"])]
                if prev:
                    w.append(("t", prev["t_tr"]))
                st1[m]["v_hb"] = p.vop("tensor_scalar_mul", hb1[par],
                                       scr1[par], stat1[par][:, 4:5], wait=w)

            def P1_5(m):
                par = m % 2
                copy_cnt = {}
                for g in range(KB // 4):
                    pp = ps_tr[g % 2]
                    for j in range(4):
                        k = g * 4 + j
                        w = []
                        if j == 0:
                            if g < 2:
                                w = [("v", st1[m]["v_hb"])]
                                if m > 0:
                                    w.append(("v", st1[m - 1]["tr_copies"]))
                            else:
                                w = [("v", copy_cnt[g - 2])]
                        p.top("transpose", pp[:, j * 128:(j + 1) * 128],
                              hb1[par][:, k * 128:(k + 1) * 128], c_id[:],
                              wait=w, inc=(j == 3))
                    wsl = c_wpre[:, g * 4:g * 4 + 4].unsqueeze(2)                         .broadcast_to([128, 4, TB])
                    copy_cnt[g] = p.vop(
                        "tensor_tensor",
                        hT[:].rearrange("p (k t) -> p k t", t=SEG)
                        [:, g * 4:g * 4 + 4, m * TB:(m + 1) * TB],
                        pp[:].rearrange("p (k t) -> p k t", t=TB),
                        wsl, ALU.mult, wait=[("t", p.cnt["t"])])
                st1[m]["tr_copies"] = copy_cnt[KB // 4 - 1]
                st1[m]["t_tr"] = p.cnt["t"]

            stages1 = [P1_0, P1_1, P1_2, P1_3, P1_4, P1_5]
            for m0 in range(0, NTB, 2):
                for sfn in stages1:
                    sfn(m0)
                    sfn(m0 + 1)
            s_ph1_end = p.cnt["s"]
            t_ph1_end = p.cnt["t"]

        with ExitStack() as ph2:
            E2 = ph2.enter_context
            Fb = arenaB[:, 0:KB * SEG]
            _wqo = {"a": 0, "i": 1, "v": 2}
            Wq = {w: [arenaB[:, 16384 + 2048 * (2 * _wqo[w] + i):
                             16384 + 2048 * (2 * _wqo[w] + i) + 2048]
                      for i in range(2)]
                  for w in ("a", "i", "v")}
            slA = [arenaF[:, 14336:15360], arenaF[:, 15360:16384]]
            slC = [arenaF[:, 16384:17408], arenaF[:, 17408:18432]]
            slE = [scr_big[:, 0:SEG], scr_big[:, SEG:2 * SEG]]
            slB = [hb[:].bitcast(F32)[:, 0:SEG], arenaF[:, 18432:19456]]
            slD = [arenaF[:, 19456:20480], scan_x[:, 0:SEG]]
            slF = [xa_t[:, 0:SEG], xa_t[:, SEG:2 * SEG]]
            slcd = [x0_t[0][:, 0:SEG], x0_t[0][:, SEG:2 * SEG]]
            slcs = [x0_t[1][:, 0:SEG], x0_t[1][:, SEG:2 * SEG]]
            t16s = [[t16[i][:] for i in range(6)],
                    [t16b[:, i * NCH:(i + 1) * NCH] for i in range(6)]]
            ps_a = E2(nc.psum_tensor("ps_a", [128, SEG], F32))
            ps_i = E2(nc.psum_tensor("ps_i", [128, SEG], F32))
            ps_v = E2(nc.psum_tensor("ps_v", [128, SEG], F32))

            v_hT_done = p.cnt["v"]
            t_w_end = {}
            wload = {}

            def load_aiv(o):
                for w in ("a", "i", "v"):
                    tw = [("t", t_w_end.get((w, o - 2), 0))]
                    wload[(w, o)] = p.dma("gpsimd", "w" + w, Wq[w][o % 2],
                                          w_aiv_l[w][:, o, :], wait=tw)

            load_aiv(0)
            psum_free = {"a": ("s", 0), "i": ("s", 0), "v": ("v", 0)}
            v_lp_at = {}
            v_state_w = {}
            v_fb_w = {}
            chain_v_end = {}
            for o in range(KB):
                if o + 1 < KB:
                    load_aiv(o + 1)
                for (w, ps) in (("a", ps_a), ("i", ps_i), ("v", ps_v)):
                    for k in range(KB):
                        for n in range(NMM):
                            waits = []
                            if n == 0 and k == 0:
                                waits = [wload[(w, o)], psum_free[w]]
                                if o == 0 and w == "a":
                                    waits.append(("v", v_hT_done))
                            p.top("matmul", ps[:, n * MMW:(n + 1) * MMW],
                                  Wq[w][o % 2][:, k * 128:(k + 1) * 128],
                                  hT[:, k * SEG + n * MMW:k * SEG + (n + 1) * MMW],
                                  start=(k == 0), stop=(k == KB - 1),
                                  wait=waits, inc=(k == KB - 1 and n == NMM - 1))
                    t_w_end[(w, o)] = p.cnt["t"]
                par = o % 2
                A, B, C = slA[par], slB[par], slC[par]
                Dg, Eg, F = slD[par], slE[par], slF[par]
                tcd, tcs = slcd[par], slcs[par]
                t6 = t16s[par]
                wv_prev = [("v", chain_v_end.get(o - 2, 0))]
                if o < 2:
                    wv_prev += [("v", v_hT_done), ("t", t_ph1_end),
                                ("s", s_ph1_end)]
                s1 = p.sop("activation", A, ps_a[:], AF.Exp, scale=-1.0,
                           bias=c_nbias[:, o:o + 1],
                           wait=[("t", t_w_end[("a", o)])] + wv_prev)
                psum_free["a"] = ("s", s1)
                s2 = p.sop("activation", F, ps_i[:], AF.Exp, scale=-1.0,
                           wait=[("t", t_w_end[("i", o)])])
                psum_free["i"] = ("s", s2)
                p.sop("activation", B, A, AF.Ln, bias=1.0)
                s4 = p.sop("activation", C, B, AF.Exp, scale=-2.0)
                p.sop("activation", A, F, AF.Ln, bias=1.0)
                s6 = p.sop("activation", Dg, A, AF.Exp, scale=-1.0)
                p.vop("tensor_scalar", Eg, C, -1.0, 1.0, ALU.mult,
                      ALU.add, wait=[("s", s4)])
                v_y = p.vop("tensor_scalar_max", Eg, Eg, 1e-8)
                p.sop("activation", C, Eg, AF.Ln, wait=[("v", v_y)])
                s_sq = p.sop("activation", A, C, AF.Exp, scale=0.5)
                p.vop("tensor_mul", Eg, A, Dg,
                      wait=[("s", max(s_sq, s6))])
                v_v = p.vop("tensor_mul", Eg, Eg, ps_v[:],
                            wait=[("t", t_w_end[("v", o)])])
                if o == KB - 1:
                    v_ps_free, s_ps_free = v_v, s2
                psum_free["v"] = ("v", v_v)
                p.vop("tensor_scalar_min", F, B, NLA_CLIP)
                v_scan = p.vop("tensor_tensor_scan", C, c_mask[:],
                               F, 0.0, ALU.mult, ALU.add,
                               wait=[("s", s_sq)])
                s_cd = p.sop("activation", tcd, C, AF.Exp,
                             scale=-1.0, wait=[("v", v_scan)])
                p.vop("tensor_scalar_min", F, C, NLA_CLIP)
                s_rinv = p.sop("activation", A, F, AF.Exp,
                               wait=[("v", p.cnt["v"])])
                p.vop("tensor_mul", Eg, Eg, A,
                      wait=[("s", s_rinv)])
                p.vop("tensor_tensor_scan", tcs, c_mask[:],
                      Eg, 0.0, ALU.mult, ALU.add)
                p.vop("tensor_copy", t6[0], tcd[:, CH - 1:SEG:CH],
                      wait=[("s", s_cd)])
                p.vop("tensor_copy", t6[1], tcs[:, CH - 1:SEG:CH])
                p.vop("tensor_mul", t6[2], t6[0], t6[1])
                p.vop("tensor_tensor_scan", t6[3], t6[0], t6[2], 0.0,
                      ALU.mult, ALU.add)
                p.vop("tensor_tensor_scan", t6[4], t6[0], z16[:], 1.0,
                      ALU.mult, ALU.add)
                p.vop("tensor_copy", LP[:, o:o + 1], t6[3][:, NCH - 1:NCH])
                v_lp_at[o] = p.vop("tensor_copy", LP[:, KB + o:KB + o + 1],
                                   t6[4][:, NCH - 1:NCH])
                p.vop("memset", t6[5][:, 0:1], 0.0)
                if NCH > 1:
                    p.vop("tensor_copy", t6[5][:, 1:NCH], t6[3][:, 0:NCH - 1])
                sin_b = t6[5].unsqueeze(2).broadcast_to([128, NCH, CH])
                p.vop("tensor_tensor",
                      Eg.rearrange("p (c k) -> p c k", k=CH),
                      tcs.rearrange("p (c k) -> p c k", k=CH), sin_b,
                      ALU.add)
                v_state_w[o] = p.vop(
                    "tensor_mul", states[:, o * SEG:(o + 1) * SEG], Eg,
                    tcd,
                    wait=[("s", s_ph1_end), ("t", t_ph1_end)])
                p.vop("memset", t6[5][:, 0:1], 1.0)
                if NCH > 1:
                    p.vop("tensor_copy", t6[5][:, 1:NCH], t6[4][:, 0:NCH - 1])
                pin_b = t6[5].unsqueeze(2).broadcast_to([128, NCH, CH])
                v_fb_w[o] = p.vop(
                    "tensor_tensor",
                    Fb[:, o * SEG:(o + 1) * SEG].rearrange(
                        "p (c k) -> p c k", k=CH),
                    tcd.rearrange("p (c k) -> p c k", k=CH), pin_b,
                    ALU.mult)
                chain_v_end[o] = p.cnt["v"]
            v_scan_end = p.cnt["v"]
            s_scan_end = p.cnt["s"]
            LPg = LP[:].rearrange("p (h o) -> p h o", h=2)
            wo_chunk = []
            for k in range(KB):
                if k < 6:
                    off = 16384 + 2048 * k
                elif k < 10:
                    off = 28672 + 2048 * (k - 6)
                elif k < 12:
                    off = 8192 + 2048 * (k - 10)
                else:
                    off = 2048 * (k - 12)
                wo_chunk.append(arenaB[:, off:off + 2048])
            ldcc_a = p.dma("gpsimd", "dcc", cc_in_a.ap(), LPg[:, :, 0:KSPL],
                           wait=[("v", v_lp_at[KSPL - 1])])
            p.wait("gpsimd", ldcc_a)
            p.op("gpsimd", "collective_compute", "AllGather", ALU.bypass,
                 [[r for r in range(NCORES)]],
                 ins=[cc_in_a.ap().opt()], outs=[cc_out_a.ap().opt()],
                 inc="cc", inc_k=1)
            ldwo = {}
            for k in range(6):
                ldwo[k] = p.dma("gpsimd", "dwo", wo_chunk[k],
                                w_o_r[k * 128:(k + 1) * 128, :],
                                wait=[("t", t_w_end[("v", KB - 1)])])
            ldcc_b = p.dma("gpsimd", "dcc", cc_in_b.ap(), LPg[:, :, KSPL:KB],
                           wait=[("v", v_lp_at[KB - 1])])
            p.wait("gpsimd", ldcc_b)
            p.op("gpsimd", "collective_compute", "AllGather", ALU.bypass,
                 [[r for r in range(NCORES)]],
                 ins=[cc_in_b.ap().opt()], outs=[cc_out_b.ap().opt()],
                 inc="cc", inc_k=1)
            ldg_a = p.dma("gpsimd", "dcc", gath[:, :, 0:2 * KSPL],
                          cc_out_a.ap().rearrange("(r p) f -> p r f",
                                                  r=NCORES), wait=[("cc", 1)])
            fix_w = {}

            def carry_chain(o0, o1, g_states, g_prods, wld):
                nn = o1 - o0
                rb = c_rmask[:].unsqueeze(2).broadcast_to([128, NCORES, nn])
                p.gop("tensor_tensor", cr_t1[:, :, o0:o1], g_states, rb,
                      ALU.mult, wait=[wld])
                p.gop("tensor_scalar_add", cr_t2[:, :, o0:o1], g_prods, -1.0)
                p.gop("tensor_tensor", cr_t2[:, :, o0:o1], cr_t2[:, :, o0:o1],
                      rb, ALU.mult)
                p.gop("tensor_scalar_add", cr_t2[:, :, o0:o1],
                      cr_t2[:, :, o0:o1], 1.0)
                p.gop("memset", carry[:, o0:o1], 0.0)
                for r in range(NCORES):
                    p.gop("tensor_mul", cr_tmp[:, o0:o1], cr_t2[:, r, o0:o1],
                          carry[:, o0:o1])
                    p.gop("tensor_add", carry[:, o0:o1], cr_tmp[:, o0:o1],
                          cr_t1[:, r, o0:o1])
                return p.cnt["g"]

            def fix_gp(o):
                cb = carry[:, o:o + 1].broadcast_to([128, SEG])
                p.gop("tensor_tensor", fix_tmp[:],
                      Fb[:, o * SEG:(o + 1) * SEG], cb, ALU.mult,
                      wait=[("v", v_fb_w[o])])
                fix_w[o] = ("g", p.gop(
                    "tensor_add", states[:, o * SEG:(o + 1) * SEG],
                    states[:, o * SEG:(o + 1) * SEG], fix_tmp[:],
                    wait=[("v", v_state_w[o])]))

            def fix_vec(o, g_carry):
                fix_w[o] = ("v", p.vop(
                    "scalar_tensor_tensor",
                    states[:, o * SEG:(o + 1) * SEG],
                    Fb[:, o * SEG:(o + 1) * SEG], carry[:, o:o + 1],
                    states[:, o * SEG:(o + 1) * SEG], ALU.mult, ALU.add,
                    wait=[("g", g_carry)]))

            g_carry_a = carry_chain(0, KSPL, gath[:, :, 0:KSPL],
                                    gath[:, :, KSPL:2 * KSPL], ldg_a)
            for o in range(6):
                fix_gp(o)
            for o in range(6, KSPL):
                fix_vec(o, g_carry_a)
            for k in range(6, 10):
                ldwo[k] = p.dma("gpsimd", "dwo", wo_chunk[k],
                                w_o_r[k * 128:(k + 1) * 128, :],
                                wait=[("v", v_scan_end), ("s", s_scan_end)])
            for k in range(10, 12):
                ldwo[k] = p.dma("gpsimd", "dwo", wo_chunk[k],
                                w_o_r[k * 128:(k + 1) * 128, :],
                                wait=[fix_w[8 + 2 * (k - 10)],
                                      fix_w[9 + 2 * (k - 10)]])
            for k in range(12, KB):
                ldwo[k] = p.dma("gpsimd", "dwo", wo_chunk[k],
                                w_o_r[k * 128:(k + 1) * 128, :],
                                wait=[fix_w[2 * (k - 12)],
                                      fix_w[2 * (k - 12) + 1]])
            ldg_b = p.dma("gpsimd", "dcc", gath[:, :, 2 * KSPL:2 * KB],
                          cc_out_b.ap().rearrange("(r p) f -> p r f",
                                                  r=NCORES), wait=[("cc", 2)])
            g_carry_b = carry_chain(KSPL, KB, gath[:, :, 2 * KSPL:KSPL + KB],
                                    gath[:, :, KSPL + KB:2 * KB], ldg_b)
            for o in range(KSPL, KB):
                fix_vec(o, g_carry_b)
            g_fix_end = p.cnt["g"]
            v_fix_end = p.cnt["v"]

        h2T = hT
        with ExitStack() as ph4:
            E4 = ph4.enter_context
            ps_o = [E4(nc.psum_tensor("ps_o%d" % i, [128, 512], F32))
                    for i in range(NB)]
            ps_tr = [E4(nc.psum_tensor("ps_tr4%d" % i, [128, 512], BF16))
                     for i in range(2)]
            scr4 = [scr_big[:], arenaF[:, 18432:20480]]
            xa4 = [xa_t[:], arenaF[:, 6144:8192]]
            hb4 = [hb[:], scan_x[:].bitcast(BF16)]
            stat4 = [stat[:], stat4b[:]]
            t_scan_mm = t_w_end[("v", KB - 1)]
            st4 = {}

            def mm4(m):
                par = m % 2
                tmm = {}
                for k in range(KB):
                    for n in range(NB):
                        waits = []
                        if n == 0 and m == 0:
                            waits = [ldwo[k], fix_w[k]]
                            if k == 0:
                                waits += [("s", s_ps_free), ("v", v_ps_free)]
                        elif k == 0 and m > 0:
                            waits = [("v", st4[m - 1]["evn"][n])]
                        p.top("matmul", ps_o[n][:],
                              states[:, k * SEG + m * TB:k * SEG + (m + 1) * TB],
                              wo_chunk[k][:, n * 512:(n + 1) * 512],
                              start=(k == 0), stop=(k == KB - 1),
                              wait=waits,
                              inc=(k == KB - 1))
                        if k == KB - 1:
                            tmm[n] = p.cnt["t"]
                st4.setdefault(m, {})["tmm"] = tmm

            def lh4(m):
                par = m % 2
                prev = st4.get(m - 2)
                w_ldx = [("v", v_scan_end), ("s", s_scan_end)]
                if prev:
                    w_ldx = [("v", prev["v_ln1"]), ("s", prev["s_sq"])]
                ldx = p.dma("sync", "dx", x0_t[par][:],
                            x_in[m * TB:(m + 1) * TB, :], wait=w_ldx)
                w_st = [ldx]
                if prev:
                    w_st += [("v", prev["v_hb"]), ("t", prev["t_tr"])]
                s_r = stats_full(x0_t[par][:], hb4[par], eps_ln, True,
                                 wait=w_st, st=stat4[par])
                st4.setdefault(m, {})["v_ln1"] = p.vop(
                    "tensor_scalar", scr4[par], x0_t[par][:],
                    stat4[par][:, 2:3], stat4[par][:, 4:5],
                    ALU.subtract, ALU.mult, wait=[("s", s_r)])

            def ev4(m):
                par = m % 2
                prev = st4.get(m - 2)
                evn = {}
                for n in range(NB):
                    w_ev = [("t", st4[m]["tmm"][n])]
                    if m == 0:
                        w_ev.append(("s", s_scan_end))
                    if m == 1:
                        w_ev += [("g", g_fix_end), ("v", v_fix_end)]
                    if prev:
                        w_ev += [prev["sp_xa"], ("s", prev["s_sq"])]
                    p.vop("tensor_copy", xa4[par][:, n * 512:(n + 1) * 512],
                          ps_o[n][:], wait=w_ev)
                    evn[n] = p.cnt["v"]
                st4[m]["evn"] = evn
                p.vop("tensor_add", xa4[par], xa4[par], scr4[par])

            def tail4(m):
                par = m % 2
                s_sq = p.sop("activation", x0_t[par][:], xa4[par], AF.Square,
                             accum_out=stat4[par][:, 1:2],
                             wait=[("v", p.cnt["v"])])
                st4[m]["s_sq"] = s_sq
                vv = p.vop("tensor_scalar", stat4[par][:, 6:7],
                           stat4[par][:, 1:2], 1.0 / D, eps_rms, ALU.mult,
                           ALU.add, wait=[("s", s_sq)])
                p.sop("activation", stat4[par][:, 5:6], stat4[par][:, 6:7],
                      AF.Ln, wait=[("v", vv)])
                s_r2 = p.sop("activation", stat4[par][:, 4:5],
                             stat4[par][:, 5:6], AF.Exp, scale=-0.5)
                v_hb = p.vop("tensor_scalar_mul", hb4[par], xa4[par],
                             stat4[par][:, 4:5], wait=[("s", s_r2)])
                st4[m]["v_hb"] = v_hb
                copy_cnt = {}
                for g in range(KB // 4):
                    pp = ps_tr[g % 2]
                    for j in range(4):
                        k = g * 4 + j
                        w = []
                        if j == 0:
                            if g < 2:
                                w = [("v", v_hb)]
                                if m > 0:
                                    w.append(("v", st4[m - 1]["tr_copies"]))
                            else:
                                w = [("v", copy_cnt[g - 2])]
                        p.top("transpose", pp[:, j * 128:(j + 1) * 128],
                              hb4[par][:, k * 128:(k + 1) * 128], c_id[:],
                              wait=w, inc=(j == 3))
                    wsl = c_wff[:, g * 4:g * 4 + 4].unsqueeze(2)                         .broadcast_to([128, 4, TB])
                    copy_cnt[g] = p.vop(
                        "tensor_tensor",
                        h2T[:].rearrange("p (k t) -> p k t", t=SEG)
                        [:, g * 4:g * 4 + 4, m * TB:(m + 1) * TB],
                        pp[:].rearrange("p (k t) -> p k t", t=TB),
                        wsl, ALU.mult,
                        wait=[("t", p.cnt["t"]), ("t", t_scan_mm)])
                st4[m]["tr_copies"] = copy_cnt[KB // 4 - 1]
                st4[m]["t_tr"] = p.cnt["t"]
                st4[m]["sp_xa"] = p.dma("gpsimd", "dxa",
                                        xa_sp.ap()[m * TB:(m + 1) * TB, :],
                                        xa4[par], wait=[("v", v_hb)])

            mm4(0)
            lh4(0)
            for m in range(NTB):
                ev4(m)
                if m + 1 < NTB:
                    lh4(m + 1)
                    mm4(m + 1)
                tail4(m)

        def gu_down_phase(tag, srcT, acc_ap, gate_src, up_src, dn_rhs_dma,
                          fuse_buf, dn_ring, t_silu, wg_t, wu_t,
                          sg_col=None, su_col=None):
            v_src_done = p.cnt["v"]
            t_entry = p.cnt["t"]
            s_entry = p.cnt["s"]
            dx1_entry = [("dx1s0", p.cnt["dx1s0"]),
                         ("dx1s1", p.cnt["dx1s1"])]
            ev_bank = {m: 0 for m in range(NTB)}
            ev_all = 0
            evac_final = [0] * NTB
            t_gu_last = t_entry
            lastpar = {0: t_entry, 1: t_entry}
            lastring = {}
            for q in range(NQ):
                with ExitStack() as pq:
                    ps_g = [pq.enter_context(
                        nc.psum_tensor("ps_g%s_%d_%d" % (tag, q, i),
                                       [128, SEG], F32)) for i in range(2)]
                    ps_u = [pq.enter_context(
                        nc.psum_tensor("ps_u%s_%d_%d" % (tag, q, i),
                                       [128, SEG], F32)) for i in range(2)]
                    sgv = {}
                    fu = {}
                    wl = {}

                    def load_pair(oi, q=q, wl=wl):
                        o = q * OHQ + oi
                        par = oi % 2
                        tw = [("t", lastpar[par])]
                        if q == 0 and oi == 0:
                            tw = [("t", t_entry), ("v", v_src_done),
                                  ("s", s_entry)]
                        a_ = p.dma("gpsimd", "wg", wg_t[par],
                                   gate_src[:, o, :], wait=tw)
                        b_ = p.dma("gpsimd", "wu", wu_t[par], up_src[:, o, :])
                        wl[oi] = (a_, b_)

                    load_pair(0)
                    for oi in range(OHQ):
                        o = q * OHQ + oi
                        par = oi % 2
                        if oi + 1 < OHQ:
                            load_pair(oi + 1)
                        for (wt, ps, wld) in ((wg_t[par], ps_g[par], wl[oi][0]),
                                              (wu_t[par], ps_u[par], wl[oi][1])):
                            for k in range(KB):
                                for n in range(NMM):
                                    waits = []
                                    if n == 0 and k == 0:
                                        waits = [wld, ("v", v_src_done),
                                                 ("s", sgv.get(par, 0)),
                                                 ("v", max(fu.get(par, 0),
                                                           ev_all))]
                                    p.top("matmul",
                                          ps[:, n * MMW:(n + 1) * MMW],
                                          wt[:, k * 128:(k + 1) * 128],
                                          srcT[:, k * SEG + n * MMW:
                                               k * SEG + (n + 1) * MMW],
                                          start=(k == 0), stop=(k == KB - 1),
                                          wait=waits,
                                          inc=(k == KB - 1 and n == NMM - 1))
                        wl[("t", oi)] = p.cnt["t"]
                        lastpar[par] = p.cnt["t"]
                        silu_sl = t_silu[:, par * SEG:(par + 1) * SEG]
                        guard = ([("v", v_src_done)] + dx1_entry
                                 if (q == 0 and oi == 0) else [])
                        if sg_col is None:
                            s_g = p.sop("activation", silu_sl, ps_g[par][:],
                                        AF.Silu, wait=[("t", wl[("t", oi)]),
                                                       ("v", fu.get(par, 0))]
                                        + guard)
                            f_ = p.vop("tensor_mul",
                                       fuse_buf[:, oi * SEG:(oi + 1) * SEG],
                                       silu_sl, ps_u[par][:],
                                       wait=[("s", s_g)])
                        else:
                            s_g = p.sop("activation", silu_sl, ps_g[par][:],
                                        AF.Silu, scale=sg_col[:, o:o + 1],
                                        wait=[("t", wl[("t", oi)]),
                                              ("v", fu.get(par, 0))] + guard)
                            f_ = p.vop("scalar_tensor_tensor",
                                       fuse_buf[:, oi * SEG:(oi + 1) * SEG],
                                       ps_u[par][:], su_col[:, o:o + 1],
                                       silu_sl, ALU.mult, ALU.mult,
                                       wait=[("s", s_g)])
                        sgv[par] = s_g
                        fu[par] = f_
                    v_fuse_done = p.cnt["v"]
                    s_last = p.cnt["s"]
                    if q == NQ - 1:
                        t_gu_last = p.cnt["t"]
                with ExitStack() as pd:
                    ps_d = [pd.enter_context(
                        nc.psum_tensor("ps_d%s_%d_%d" % (tag, q, i),
                                       [128, 512], F32))
                        for i in range(min(8, NTB))]
                    for n in range(NB):
                        tm = {}
                        for k in range(KHQ):
                            r = k % len(dn_ring)
                            ring = dn_ring[r]
                            tw = [("t", lastring.get(r, t_entry))]
                            ldd = dn_rhs_dma(q, n, k, ring, tw)
                            for m in range(NTB):
                                waits = []
                                if m == 0:
                                    waits = [ldd]
                                if k == 0:
                                    waits = list(waits) + [("v", ev_bank[m])]
                                    if n == 0:
                                        waits += [("v", v_fuse_done),
                                                  ("s", s_last)]
                                p.top("matmul", ps_d[m % 8][:],
                                      fuse_buf[:, k * SEG + m * TB:
                                               k * SEG + (m + 1) * TB],
                                      ring,
                                      start=(k == 0), stop=(k == KHQ - 1),
                                      wait=waits,
                                      inc=(m == NTB - 1 or k == KHQ - 1))
                                if k == KHQ - 1:
                                    tm[m] = p.cnt["t"]
                            lastring[r] = p.cnt["t"]
                        for m in range(NTB):
                            dst = acc_ap[:, m * D + n * 512:m * D + (n + 1) * 512]
                            if q == 0:
                                p.vop("tensor_copy", dst, ps_d[m % 8][:],
                                      wait=[("t", tm[m]),
                                            ("dxa0", 16 * (NTB // 2)),
                                            ("dxa1", 16 * (NTB // 2))])
                            else:
                                p.vop("tensor_add", dst, dst, ps_d[m % 8][:],
                                      wait=[("t", tm[m])])
                            ev_bank[m] = p.cnt["v"]
                            if q == NQ - 1 and n == NB - 1:
                                evac_final[m] = p.cnt["v"]
                    ev_all = max(ev_bank.values())
            return dict(ev_all=ev_all, evac_final=evac_final,
                        t_gu_last=t_gu_last)

        with ExitStack() as ph5:
            E5 = ph5.enter_context
            assert OHQ * SEG == KB * SEG
            fuse5 = states[:, 0:OHQ * SEG]
            fuse7 = hT[:, 0:OHQ * SEG]
            acc = arenaF[:, 0:NTB * D]
            t_silu = scr_big[:]
            KW = KB * 128
            x0b = [x0_t[i][:].bitcast(BF16) for i in range(2)]
            wg_t = [x0b[0][:, 0:KW], x0b[0][:, KW:2 * KW]]
            wu_t = [x0b[1][:, 0:KW], x0b[1][:, KW:2 * KW]]
            nring = max(1, min(4, D // 512))
            dn_ring = [hb[:, r * 512:(r + 1) * 512] for r in range(nring)]

            def dn_dma(qq, n, k, ring, tw):
                row0 = qq * (FFN // NQ) + k * 128
                return p.dma("gpsimd", "dn", ring,
                             w_dn_r[row0:row0 + 128, n * 512:(n + 1) * 512],
                             wait=tw)

            info5 = gu_down_phase('5', h2T, acc, w_gu_l[:, 0:OBF, :],
                                  w_gu_l[:, OBF:2 * OBF, :], dn_dma,
                                  fuse5, dn_ring, t_silu, wg_t, wu_t)

            ln2T = states
            t_p5_end = p.cnt["t"]
            s_p5_end = p.cnt["s"]
            t_gu5 = info5["t_gu_last"]
            ev5 = info5["evac_final"]
            ev5_all = info5["ev_all"]
            hTf = hT[:].bitcast(F32)
            scrb16 = scr_big[:].bitcast(BF16)
            scr6 = [hTf[:, 0:2048], hTf[:, 2048:4096]]
            xat6 = [hTf[:, 4096:6144], xa_t[:]]
            stat6 = [stat[:], hTf[:, 6144:6152]]
            hb6 = [scrb16[:, 0:2048], scrb16[:, 2048:4096]]
            dump6 = [arenaF[:, 16384:18432], arenaF[:, 18432:20480]]
            with ExitStack() as ph6:
                ps_tr6 = [ph6.enter_context(
                    nc.psum_tensor("ps_tr6%d" % i, [128, 512], BF16))
                    for i in range(2)]
                st6 = {}

                def S6_0(m):
                    par = m % 2
                    buf = x0_t[par]
                    prev = st6.get(m - 2)
                    w = [("t", t_gu5),
                         ("dxa%d" % (m % 2), 16 * (m // 2 + 1))]
                    if prev:
                        w += [("v", prev["v_x1"])]
                    ldxa = p.dma("sync", "dx", buf[:],
                                 xa_sp.ap()[m * TB:(m + 1) * TB, :], wait=w)
                    w2 = [ldxa, ("v", ev5[m]), ("t", t_gu5)]
                    if prev:
                        w2 += [("s", prev["s_end"]), prev["sp_x1"]]
                    v_ao = p.vop("tensor_add", scr6[par], buf[:],
                                 acc[:, m * D:(m + 1) * D], wait=w2)
                    st6[m] = dict(v_ao=v_ao)

                def S6_1(m):
                    par = m % 2
                    prev = st6.get(m - 2)
                    w = [("v", st6[m]["v_ao"])]
                    if prev:
                        w.append(("v", prev["v_hb"]))
                    st6[m]["s_r"] = stats_full(scr6[par], dump6[par],
                                               eps_ln, True, wait=w,
                                               st=stat6[par])
                    st6[m]["ldx0"] = p.dma("sync", "dx", x0_t[par][:],
                                           x_in[m * TB:(m + 1) * TB, :],
                                           wait=[("v", st6[m]["v_ao"])])

                def S6_2(m):
                    par = m % 2
                    s_r = st6[m]["s_r"]
                    p.vop("tensor_scalar", xat6[par], scr6[par],
                          stat6[par][:, 2:3], stat6[par][:, 4:5],
                          ALU.subtract, ALU.mult,
                          wait=[("s", s_r), ("dxa0", 16 * (NTB // 2)),
                                ("dxa1", 16 * (NTB // 2))])

                def S6_3(m):
                    par = m % 2
                    v_x1 = p.vop("tensor_add", scr6[par], xat6[par],
                                 x0_t[par][:], wait=[st6[m]["ldx0"]])
                    st6[m]["v_x1"] = v_x1
                    st6[m]["sp_x1"] = p.dma("gpsimd", "dx1s",
                                            x1_sp.ap()[m * TB:(m + 1) * TB, :],
                                            scr6[par], wait=[("v", v_x1)])
                    st6[m]["s_r2"] = stats_full(scr6[par], xat6[par], eps_ln,
                                                True, wait=[("v", v_x1)],
                                                st=stat6[par])
                    st6[m]["s_end"] = p.cnt["s"]

                def S6_4(m):
                    par = m % 2
                    prev = st6.get(m - 2)
                    w = [("s", st6[m]["s_r2"]), ("s", s_p5_end)]
                    if prev:
                        w.append(("t", prev["t_tr"]))
                    st6[m]["v_hb"] = p.vop(
                        "tensor_scalar", hb6[par], scr6[par],
                        stat6[par][:, 2:3], stat6[par][:, 4:5],
                        ALU.subtract, ALU.mult, wait=w)

                def S6_5(m):
                    par = m % 2
                    copy_cnt = {}
                    for g in range(KB // 4):
                        pp = ps_tr6[g % 2]
                        for j in range(4):
                            k = g * 4 + j
                            w = []
                            if j == 0:
                                if g < 2:
                                    w = [("v", st6[m]["v_hb"])]
                                    if m == 0:
                                        w.append(("v", ev5_all))
                                    else:
                                        w.append(("v",
                                                  st6[m - 1]["tr_copies"]))
                                else:
                                    w = [("v", copy_cnt[g - 2])]
                            p.top("transpose", pp[:, j * 128:(j + 1) * 128],
                                  hb6[par][:, k * 128:(k + 1) * 128],
                                  c_id[:], wait=w, inc=(j == 3))
                        copy_cnt[g] = p.vop(
                            "tensor_copy",
                            ln2T[:].rearrange("p (k t) -> p k t", t=SEG)
                            [:, g * 4:g * 4 + 4, m * TB:(m + 1) * TB],
                            pp[:].rearrange("p (k t) -> p k t", t=TB),
                            wait=[("t", p.cnt["t"]), ("t", t_p5_end)])
                    st6[m]["tr_copies"] = copy_cnt[KB // 4 - 1]
                    st6[m]["t_tr"] = p.cnt["t"]

                stages6 = [S6_0, S6_1, S6_2, S6_3, S6_4, S6_5]
                for m0 in range(0, NTB, 2):
                    for sfn in stages6:
                        sfn(m0)
                        sfn(m0 + 1)

            def dn8_dma(qq, n, k, ring, tw):
                row0 = qq * (FFN // NQ) + k * 128
                return p.dma("gpsimd", "dn", ring,
                             wd8_r[n, row0:row0 + 128, :], wait=tw)

            info7 = gu_down_phase('7', ln2T, acc, wg8_l, wu8_l, dn8_dma,
                                  fuse7, dn_ring, t_silu, wg_t, wu_t,
                                  sg_col=c_sg[:], su_col=c_su[:])

            t_p7_end = p.cnt["t"]
            t_gu7 = info7["t_gu_last"]
            ev7 = info7["evac_final"]
            statesf = states[:].bitcast(F32)
            scr8 = [hTf[:, 0:2048], hTf[:, 2048:4096]]
            xat8 = [hTf[:, 4096:6144], xa_t[:]]
            stat8 = [stat[:], hTf[:, 6144:6152]]
            dump8 = [statesf[:, 0:2048], statesf[:, 2048:4096]]
            st8 = {}

            def S8_0(m):
                par = m % 2
                prev = st8.get(m - 2)
                w = [("t", t_gu7),
                     ("dx1s%d" % (m % 2), 16 * (m // 2 + 1))]
                if prev:
                    w.append(("v", prev["v_o"]))
                ldx1 = p.dma("sync", "dx", x0_t[par][:],
                             x1_sp.ap()[m * TB:(m + 1) * TB, :], wait=w)
                w2 = [("v", ev7[m]), ld_sdb, ("t", t_p7_end)]
                if prev:
                    w2 += [("s", prev["s_end"]), prev["dy"]]
                v_y3 = p.vop("tensor_mul", scr8[par],
                             acc[:, m * D:(m + 1) * D], sdb[:], wait=w2)
                st8[m] = dict(ldx1=ldx1, v_y3=v_y3)

            def S8_1(m):
                par = m % 2
                st8[m]["s_r"] = stats_full(scr8[par], dump8[par], eps_ln,
                                           True, wait=[("v", st8[m]["v_y3"])],
                                           st=stat8[par])
                st8[m]["s_end"] = p.cnt["s"]

            def S8_2(m):
                par = m % 2
                p.vop("tensor_scalar", xat8[par], scr8[par],
                      stat8[par][:, 2:3], stat8[par][:, 4:5],
                      ALU.subtract, ALU.mult, wait=[("s", st8[m]["s_r"])])

            def S8_3(m):
                par = m % 2
                v_o = p.vop("tensor_add", scr8[par], xat8[par], x0_t[par][:],
                            wait=[st8[m]["ldx1"]])
                st8[m]["v_o"] = v_o
                st8[m]["dy"] = p.dma("sync", "dy",
                                     y_out[m * TB:(m + 1) * TB, :],
                                     scr8[par], wait=[("v", v_o)])

            stages8 = [S8_0, S8_1, S8_2, S8_3]
            for m0 in range(0, NTB, 2):
                for sfn in stages8:
                    sfn(m0)
                    sfn(m0 + 1)
            p.wait("sync", st8[NTB - 2]["dy"], st8[NTB - 1]["dy"])

        with nc.Block() as block:
            p.finalize(block)

    return nc


def _lhsT_layout(W, dtype):
    Din, Dout = W.shape
    KBl, OBl = Din // 128, Dout // 128
    return np.ascontiguousarray(
        W.reshape(KBl, 128, OBl, 128).transpose(1, 2, 0, 3)
        .reshape(128, OBl, KBl * 128)).astype(dtype, copy=False)


def _colT(vec):
    Dl = vec.shape[0]
    return np.ascontiguousarray(vec.reshape(Dl // 128, 128).T)


def make_inputs(cfg, fi):
    D, FFN, SEG, NSEG, Bb = (cfg[k] for k in ("D", "FFN", "SEG", "NSEG", "B"))
    NCORES = Bb * NSEG
    NB = D // 512
    x = np.asarray(fi["L0_attn_x"], np.float32)

    shared = {
        "w_a_l": _lhsT_layout(np.asarray(fi["L0_attn_w_a"], np.float32), np.float32),
        "w_i_l": _lhsT_layout(np.asarray(fi["L0_attn_w_i"], np.float32), np.float32),
        "w_v_l": _lhsT_layout(np.asarray(fi["L0_attn_w_v"], np.float32), np.float32),
        "w_o_r": np.ascontiguousarray(np.asarray(fi["L0_attn_out_proj_w"], np.float32)),
        "w_gu_l": _lhsT_layout(np.asarray(fi["L0_attn_w_gate_up"], np.float32),
                               np.float32),
        "w_dn_r": np.ascontiguousarray(np.asarray(fi["L0_attn_w_down"], np.float32)),
        "wg8_l": _lhsT_layout(np.asarray(fi["L0_ffn_w_gate"], np.int8), np.int8),
        "wu8_l": _lhsT_layout(np.asarray(fi["L0_ffn_w_up"], np.int8), np.int8),
        "wd8_r": np.ascontiguousarray(
            np.asarray(fi["L0_ffn_w_down"], np.int8).reshape(FFN, NB, 512)
            .transpose(1, 0, 2)),
        "biasT": _colT(np.asarray(fi["L0_attn_decay_bias"], np.float32)),
        "wpreT": _colT(np.asarray(fi["L0_attn_pre_norm_w"], np.float32)),
        "wffT": _colT(np.asarray(fi["L0_attn_ffn_norm_w"], np.float32)),
        "sgT": _colT(np.asarray(fi["L0_ffn_s_gate"], np.float32)),
        "suT": _colT(np.asarray(fi["L0_ffn_s_up"], np.float32)),
        "sd_row": np.ascontiguousarray(
            np.asarray(fi["L0_ffn_s_down"], np.float32)[None, :]),
        "identity": np.eye(128, dtype=BF),
        "ones_row": np.ones((1, 128), BF),
    }
    scanmask = np.ones((128, SEG), np.float32)
    scanmask[:, ::CH] = 0.0
    shared["scanmask"] = scanmask

    in_maps = []
    for c in range(NCORES):
        b, s0 = divmod(c, NSEG)
        rmask = np.zeros((128, NCORES), np.float32)
        for r in range(NCORES):
            if r // NSEG == b and (r % NSEG) < s0:
                rmask[:, r] = 1.0
        m = dict(shared)
        m["x"] = np.ascontiguousarray(x[b, s0 * SEG:(s0 + 1) * SEG, :])
        m["rankmask"] = rmask
        in_maps.append(m)
    return in_maps


def kernel(**inputs):
    cfg = FULL
    NCORES = cfg["B"] * cfg["NSEG"]
    nc = build_nc(cfg)
    in_maps = make_inputs(cfg, inputs)
    res = run_bass_kernel_spmd(nc, in_maps, core_ids=list(range(NCORES)))
    SEG, D = cfg["SEG"], cfg["D"]
    out = np.empty((cfg["B"], cfg["NSEG"] * SEG, D), np.float32)
    for c in range(NCORES):
        b, s0 = divmod(c, cfg["NSEG"])
        out[b, s0 * SEG:(s0 + 1) * SEG, :] = res.results[c]["y"]
    return out


# version 19
# speedup vs baseline: 1.0689x; 1.0030x over previous
from contextlib import ExitStack

import numpy as np
import ml_dtypes

import concourse.bass as bass
import concourse.mybir as mybir
from concourse.bass_utils import run_bass_kernel_spmd

F32 = mybir.dt.float32
BF16 = mybir.dt.bfloat16
I8 = mybir.dt.int8
AF = mybir.ActivationFunctionType
ALU = mybir.AluOpType
BF = ml_dtypes.bfloat16

NLA_CLIP = 23.025851
CH = 64
TB = 128

FULL = dict(D=2048, FFN=8192, SEG=1024, NSEG=4, B=2)
MINI = dict(D=512, FFN=2048, SEG=256, NSEG=2, B=2)

DMA_STREAMS = {"dC": 1, "dx": 4, "wa": 2, "wi": 2, "wv": 2, "dwo": 16,
               "wg": 2, "wu": 2, "dn": 4, "dxa": 2, "dx1s": 2, "dcc": 1,
               "dy": 2}


class Prog:

    ENGINES = ("sync", "scalar", "vector", "tensor", "gpsimd")

    def __init__(self, nc):
        self.nc = nc
        self.ops = {e: [] for e in self.ENGINES}
        self.cnt = {}
        self.sems = {}
        self.duse = {}
        self._waited = {e: {} for e in self.ENGINES}

    def make_sems(self, ctx, names):
        for n in names:
            self.sems[n] = ctx.enter_context(self.nc.semaphore("sem_" + n))
            self.cnt[n] = 0

    def make_dma_streams(self, ctx, streams):
        for s, n in streams.items():
            for i in range(n):
                name = s + str(i)
                self.sems[name] = ctx.enter_context(
                    self.nc.semaphore("sem_" + name))
                self.cnt[name] = 0
            self.duse[s] = 0

    def _dedup(self, engine, waits):
        wl = []
        for w in waits:
            if w is None:
                continue
            sname, val = w
            if val is None or val <= 0:
                continue
            if self._waited[engine].get(sname, 0) >= val:
                continue
            self._waited[engine][sname] = val
            wl.append((self.sems[sname], val))
        return wl

    def op(self, engine, method, *args, wait=(), inc=None, inc_k=1, **kwargs):
        wl = self._dedup(engine, list(wait))
        ret = None
        sem = None
        if inc is not None:
            self.cnt[inc] += inc_k
            ret = self.cnt[inc]
            sem = self.sems[inc]

        def go(eng, wl=wl, method=method, args=args, kwargs=kwargs, sem=sem,
               inc_k=inc_k):
            for (s, val) in wl:
                eng.wait_ge(s, val)
            inst = getattr(eng, method)(*args, **kwargs)
            if sem is not None:
                inst.then_inc(sem, inc_k)

        self.ops[engine].append(go)
        return ret

    def wait(self, engine, *waits):
        wl = self._dedup(engine, list(waits))
        if wl:
            def go(eng, wl=wl):
                for (s, val) in wl:
                    eng.wait_ge(s, val)
            self.ops[engine].append(go)

    def vop(self, method, *args, wait=(), **kwargs):
        return self.op("vector", method, *args,
                       wait=[("v", self.cnt["v"])] + list(wait), inc="v", **kwargs)

    def sop(self, method, *args, wait=(), **kwargs):
        return self.op("scalar", method, *args,
                       wait=[("s", self.cnt["s"])] + list(wait), inc="s", **kwargs)

    def gop(self, method, *args, wait=(), **kwargs):
        return self.op("gpsimd", method, *args,
                       wait=[("g", self.cnt["g"])] + list(wait), inc="g", **kwargs)

    def top(self, method, *args, wait=(), inc=False, **kwargs):
        return self.op("tensor", method, *args, wait=wait,
                       inc=("t" if inc else None), **kwargs)

    def dma(self, queue, stream, dst, src, wait=()):
        n = DMA_STREAMS[stream]
        use = self.duse[stream]
        self.duse[stream] = use + 1
        semname = stream + str(use % n)
        self.op(queue, "dma_start", dst, src, wait=wait, inc=semname, inc_k=16)
        return (semname, self.cnt[semname])

    def finalize(self, block):
        for e in self.ENGINES:
            fns = self.ops[e]
            if not fns:
                continue

            def emit(eng, fns=fns):
                for f in fns:
                    f(eng)

            getattr(block, e)(emit)


def build_nc(cfg):
    D, FFN, SEG = cfg["D"], cfg["FFN"], cfg["SEG"]
    KB = D // 128
    NTB = SEG // TB
    NCH = SEG // CH
    OBF = FFN // 128
    NB = D // 512
    NCORES = cfg["B"] * cfg["NSEG"]
    NQ = 4 if OBF % 8 == 0 else 2
    OHQ = OBF // NQ
    KHQ = OHQ
    MMW = min(SEG, 512)
    NMM = SEG // MMW
    eps_ln, eps_rms = 1e-5, 1e-6

    nc = bass.Bass("TRN2", target_bir_lowering=False)

    def din(name, shape, dt=F32):
        return nc.dram_tensor(name, list(shape), dt, kind="ExternalInput").ap()

    x_in = din("x", (SEG, D))
    w_aiv_l = {w: din("w_%s_l" % w, (128, KB, KB * 128)) for w in ("a", "i", "v")}
    w_o_r = din("w_o_r", (D, D))
    w_gu_l = din("w_gu_l", (128, 2 * OBF, KB * 128))
    w_dn_r = din("w_dn_r", (FFN, D))
    wg8_l = din("wg8_l", (128, OBF, KB * 128), I8)
    wu8_l = din("wu8_l", (128, OBF, KB * 128), I8)
    wd8_r = din("wd8_r", (NB, FFN, 512), I8)
    biasT = din("biasT", (128, KB))
    wpreT = din("wpreT", (128, KB))
    wffT = din("wffT", (128, KB))
    sgT = din("sgT", (128, OBF))
    suT = din("suT", (128, OBF))
    sd_row = din("sd_row", (1, D))
    scanmask = din("scanmask", (128, SEG))
    identity = din("identity", (128, 128), BF16)
    ones_row = din("ones_row", (1, 128), BF16)
    rankmask = din("rankmask", (128, NCORES))
    y_out = nc.dram_tensor("y", [SEG, D], F32, kind="ExternalOutput").ap()

    KSPL = 12
    cc_in_a = nc.dram_tensor("cc_in_a", [128, 2 * KSPL], F32)
    cc_out_a = nc.dram_tensor("cc_out_a", [NCORES * 128, 2 * KSPL], F32)
    cc_in_b = nc.dram_tensor("cc_in_b", [128, 2 * (KB - KSPL)], F32)
    cc_out_b = nc.dram_tensor("cc_out_b", [NCORES * 128, 2 * (KB - KSPL)], F32)
    xa_sp = nc.dram_tensor("xa_sp", [SEG, D], F32)
    x1_sp = nc.dram_tensor("x1_sp", [SEG, D], F32)

    with ExitStack() as ctx:
        E = ctx.enter_context
        p = Prog(nc)
        p.make_sems(ctx, ["v", "s", "g", "t", "cc"])
        p.make_dma_streams(ctx, DMA_STREAMS)

        hT = E(nc.sbuf_tensor("hT", [128, KB * SEG], BF16))
        states = E(nc.sbuf_tensor("states", [128, KB * SEG], BF16))
        arena = E(nc.sbuf_tensor("arena", [128, 20480], F32))
        arenaF = arena[:]
        arenaB = arena[:].bitcast(BF16)
        c_bias = E(nc.sbuf_tensor("c_bias", [128, KB], F32))
        c_nbias = E(nc.sbuf_tensor("c_nbias", [128, KB], F32))
        c_wpre = E(nc.sbuf_tensor("c_wpre", [128, KB], F32))
        c_wff = E(nc.sbuf_tensor("c_wff", [128, KB], F32))
        c_sg = E(nc.sbuf_tensor("c_sg", [128, OBF], F32))
        c_su = E(nc.sbuf_tensor("c_su", [128, OBF], F32))
        c_mask = E(nc.sbuf_tensor("c_mask", [128, SEG], F32))
        c_id = E(nc.sbuf_tensor("c_id", [128, 128], BF16))
        c_ones = E(nc.sbuf_tensor("c_ones", [1, 128], BF16))
        c_rmask = E(nc.sbuf_tensor("c_rmask", [128, NCORES], F32))
        sdb = E(nc.sbuf_tensor("sdb", [128, D], F32))
        LP = E(nc.sbuf_tensor("LP", [128, 2 * KB], F32))
        gath = E(nc.sbuf_tensor("gath", [128, NCORES, 2 * KB], F32))
        carry = E(nc.sbuf_tensor("carry", [128, KB], F32))
        cr_t1 = E(nc.sbuf_tensor("cr_t1", [128, NCORES, KB], F32))
        cr_t2 = E(nc.sbuf_tensor("cr_t2", [128, NCORES, KB], F32))
        cr_tmp = E(nc.sbuf_tensor("cr_tmp", [128, KB], F32))
        fix_tmp = E(nc.sbuf_tensor("fix_tmp", [128, SEG], BF16))
        scan_x = E(nc.sbuf_tensor("scan_x", [128, SEG], F32))
        t16b = E(nc.sbuf_tensor("t16b", [128, 6 * (SEG // CH)], F32))
        xa_t = E(nc.sbuf_tensor("xa_t", [128, D], F32))
        stat4b = E(nc.sbuf_tensor("stat4b", [128, 32], F32))
        x0_t = [E(nc.sbuf_tensor("x0_t%d" % i, [128, D], F32)) for i in range(2)]
        stat = E(nc.sbuf_tensor("stat", [128, 32], F32))
        scr_big = E(nc.sbuf_tensor("scr_big", [128, D], F32))
        hb = E(nc.sbuf_tensor("hb", [128, D], BF16))
        t16 = [E(nc.sbuf_tensor("t16_%d" % i, [128, NCH], F32)) for i in range(6)]
        z16 = E(nc.sbuf_tensor("z16", [128, NCH], F32))

        p.dma("sync", "dC", c_bias[:], biasT)
        p.dma("sync", "dC", c_wpre[:], wpreT)
        p.dma("sync", "dC", c_wff[:], wffT)
        p.dma("sync", "dC", c_sg[:], sgT)
        p.dma("sync", "dC", c_su[:], suT)
        p.dma("sync", "dC", c_mask[:], scanmask)
        p.dma("sync", "dC", c_id[:], identity)
        p.dma("sync", "dC", c_ones[:], ones_row)
        p.dma("sync", "dC", c_rmask[:], rankmask)
        ld_sdb = p.dma("sync", "dC", sdb[:],
                       sd_row.broadcast_to([128, D]))
        ldC = ld_sdb
        p.vop("memset", z16[:], 0.0)
        p.vop("tensor_scalar_mul", c_nbias[:], c_bias[:], -1.0, wait=[ldC])

        def stats_full(src_ap, dump_ap, eps, mean_too, wait=(), st=None):
            if st is None:
                st = stat[:]
            nchk = D // 512
            for c in range(nchk):
                p.vop("bn_stats", st[:, 8 + 6 * c:8 + 6 * (c + 1)],
                      src_ap[:, c * 512:(c + 1) * 512],
                      wait=(list(wait) if c == 0 else ()))
            p.vop("bn_aggr", st[:, 2:4], st[:, 8:8 + 6 * nchk])
            if mean_too:
                vv = p.vop("tensor_scalar_add", st[:, 6:7], st[:, 3:4], eps)
            else:
                p.vop("tensor_scalar", st[:, 6:7], st[:, 2:3], st[:, 2:3],
                      eps, ALU.mult, ALU.add)
                vv = p.vop("tensor_add", st[:, 6:7], st[:, 6:7], st[:, 3:4])
            p.sop("activation", st[:, 5:6], st[:, 6:7], AF.Ln,
                  wait=[("v", vv)])
            return p.sop("activation", st[:, 4:5], st[:, 5:6], AF.Exp,
                         scale=-0.5)

        with ExitStack() as ph1:
            ps_tr = [ph1.enter_context(nc.psum_tensor("ps_tr%d" % i, [128, 512],
                                                      BF16)) for i in range(2)]
            statesP1 = states[:].bitcast(F32)
            scr1 = [scr_big[:], statesP1[:, 0:2048]]
            stat1 = [stat[:], statesP1[:, 2048:2080]]
            hb1 = [hb[:], states[:][:, 4160:6208]]
            st1 = {}

            def P1_0(m):
                par = m % 2
                prev = st1.get(m - 2)
                w = []
                if prev:
                    w = [("s", prev["s_sq"]), ("v", prev["v_ln1"])]
                st1[m] = dict(ldx=p.dma("sync", "dx", x0_t[par][:],
                                        x_in[m * TB:(m + 1) * TB, :], wait=w))

            def P1_1(m):
                par = m % 2
                prev = st1.get(m - 2)
                w = [st1[m]["ldx"]]
                if prev:
                    w.append(("v", prev["v_hb"]))
                st1[m]["s_r"] = stats_full(x0_t[par][:], scr1[par], eps_ln,
                                           True, wait=w, st=stat1[par])

            def P1_2(m):
                par = m % 2
                st1[m]["v_ln1"] = p.vop("tensor_scalar", scr1[par],
                                        x0_t[par][:], stat1[par][:, 2:3],
                                        stat1[par][:, 4:5], ALU.subtract,
                                        ALU.mult, wait=[("s", st1[m]["s_r"])])

            def P1_3(m):
                par = m % 2
                s_r2 = stats_full(scr1[par], None, eps_rms, False,
                                  wait=[("v", st1[m]["v_ln1"])],
                                  st=stat1[par])
                st1[m]["s_sq"] = s_r2
                st1[m]["s_r2"] = s_r2

            def P1_4(m):
                par = m % 2
                prev = st1.get(m - 2)
                w = [("s", st1[m]["s_r2"])]
                if prev:
                    w.append(("t", prev["t_tr"]))
                st1[m]["v_hb"] = p.vop("tensor_scalar_mul", hb1[par],
                                       scr1[par], stat1[par][:, 4:5], wait=w)

            def P1_5(m):
                par = m % 2
                copy_cnt = {}
                for g in range(KB // 4):
                    pp = ps_tr[g % 2]
                    for j in range(4):
                        k = g * 4 + j
                        w = []
                        if j == 0:
                            if g < 2:
                                w = [("v", st1[m]["v_hb"])]
                                if m > 0:
                                    w.append(("v", st1[m - 1]["tr_copies"]))
                            else:
                                w = [("v", copy_cnt[g - 2])]
                        p.top("transpose", pp[:, j * 128:(j + 1) * 128],
                              hb1[par][:, k * 128:(k + 1) * 128], c_id[:],
                              wait=w, inc=(j == 3))
                    wsl = c_wpre[:, g * 4:g * 4 + 4].unsqueeze(2)                         .broadcast_to([128, 4, TB])
                    copy_cnt[g] = p.vop(
                        "tensor_tensor",
                        hT[:].rearrange("p (k t) -> p k t", t=SEG)
                        [:, g * 4:g * 4 + 4, m * TB:(m + 1) * TB],
                        pp[:].rearrange("p (k t) -> p k t", t=TB),
                        wsl, ALU.mult, wait=[("t", p.cnt["t"])])
                st1[m]["tr_copies"] = copy_cnt[KB // 4 - 1]
                st1[m]["t_tr"] = p.cnt["t"]

            stages1 = [P1_0, P1_1, P1_2, P1_3, P1_4, P1_5]
            for m0 in range(0, NTB, 2):
                for sfn in stages1:
                    sfn(m0)
                    sfn(m0 + 1)
            s_ph1_end = p.cnt["s"]
            t_ph1_end = p.cnt["t"]

        with ExitStack() as ph2:
            E2 = ph2.enter_context
            Fb = arenaB[:, 0:KB * SEG]
            _wqo = {"a": 0, "i": 1, "v": 2}
            Wq = {w: [arenaB[:, 16384 + 2048 * (2 * _wqo[w] + i):
                             16384 + 2048 * (2 * _wqo[w] + i) + 2048]
                      for i in range(2)]
                  for w in ("a", "i", "v")}
            slA = [arenaF[:, 14336:15360], arenaF[:, 15360:16384]]
            slC = [arenaF[:, 16384:17408], arenaF[:, 17408:18432]]
            slE = [scr_big[:, 0:SEG], scr_big[:, SEG:2 * SEG]]
            slB = [hb[:].bitcast(F32)[:, 0:SEG], arenaF[:, 18432:19456]]
            slD = [arenaF[:, 19456:20480], scan_x[:, 0:SEG]]
            slF = [xa_t[:, 0:SEG], xa_t[:, SEG:2 * SEG]]
            slcd = [x0_t[0][:, 0:SEG], x0_t[0][:, SEG:2 * SEG]]
            slcs = [x0_t[1][:, 0:SEG], x0_t[1][:, SEG:2 * SEG]]
            t16s = [[t16[i][:] for i in range(6)],
                    [t16b[:, i * NCH:(i + 1) * NCH] for i in range(6)]]
            ps_a = E2(nc.psum_tensor("ps_a", [128, SEG], F32))
            ps_i = E2(nc.psum_tensor("ps_i", [128, SEG], F32))
            ps_v = E2(nc.psum_tensor("ps_v", [128, SEG], F32))

            v_hT_done = p.cnt["v"]
            t_w_end = {}
            wload = {}

            def load_aiv(o):
                for w in ("a", "i", "v"):
                    tw = [("t", t_w_end.get((w, o - 2), 0))]
                    wload[(w, o)] = p.dma("gpsimd", "w" + w, Wq[w][o % 2],
                                          w_aiv_l[w][:, o, :], wait=tw)

            load_aiv(0)
            psum_free = {"a": ("s", 0), "i": ("s", 0), "v": ("v", 0)}
            v_lp_at = {}
            v_state_w = {}
            v_fb_w = {}
            chain_v_end = {}
            for o in range(KB):
                if o + 1 < KB:
                    load_aiv(o + 1)
                for (w, ps) in (("a", ps_a), ("i", ps_i), ("v", ps_v)):
                    for k in range(KB):
                        for n in range(NMM):
                            waits = []
                            if n == 0 and k == 0:
                                waits = [wload[(w, o)], psum_free[w]]
                                if o == 0 and w == "a":
                                    waits.append(("v", v_hT_done))
                            p.top("matmul", ps[:, n * MMW:(n + 1) * MMW],
                                  Wq[w][o % 2][:, k * 128:(k + 1) * 128],
                                  hT[:, k * SEG + n * MMW:k * SEG + (n + 1) * MMW],
                                  start=(k == 0), stop=(k == KB - 1),
                                  wait=waits, inc=(k == KB - 1 and n == NMM - 1))
                    t_w_end[(w, o)] = p.cnt["t"]
                par = o % 2
                A, B, C = slA[par], slB[par], slC[par]
                Dg, Eg, F = slD[par], slE[par], slF[par]
                tcd, tcs = slcd[par], slcs[par]
                t6 = t16s[par]
                wv_prev = [("v", chain_v_end.get(o - 2, 0))]
                if o < 2:
                    wv_prev += [("v", v_hT_done), ("t", t_ph1_end),
                                ("s", s_ph1_end)]
                s1 = p.sop("activation", A, ps_a[:], AF.Exp, scale=-1.0,
                           bias=c_nbias[:, o:o + 1],
                           wait=[("t", t_w_end[("a", o)])] + wv_prev)
                psum_free["a"] = ("s", s1)
                s2 = p.sop("activation", F, ps_i[:], AF.Exp, scale=-1.0,
                           wait=[("t", t_w_end[("i", o)])])
                psum_free["i"] = ("s", s2)
                p.sop("activation", B, A, AF.Ln, bias=1.0)
                s4 = p.sop("activation", C, B, AF.Exp, scale=-2.0)
                p.sop("activation", A, F, AF.Ln, bias=1.0)
                s6 = p.sop("activation", Dg, A, AF.Exp, scale=-1.0)
                p.vop("tensor_scalar", Eg, C, -1.0, 1.0, ALU.mult,
                      ALU.add, wait=[("s", s4)])
                v_y = p.vop("tensor_scalar_max", Eg, Eg, 1e-8)
                p.sop("activation", C, Eg, AF.Ln, wait=[("v", v_y)])
                s_sq = p.sop("activation", A, C, AF.Exp, scale=0.5)
                p.vop("tensor_mul", Eg, A, Dg,
                      wait=[("s", max(s_sq, s6))])
                v_v = p.vop("tensor_mul", Eg, Eg, ps_v[:],
                            wait=[("t", t_w_end[("v", o)])])
                if o == KB - 1:
                    v_ps_free, s_ps_free = v_v, s2
                psum_free["v"] = ("v", v_v)
                p.vop("tensor_scalar_min", F, B, NLA_CLIP)
                v_scan = p.vop("tensor_tensor_scan", C, c_mask[:],
                               F, 0.0, ALU.mult, ALU.add,
                               wait=[("s", s_sq)])
                s_cd = p.sop("activation", tcd, C, AF.Exp,
                             scale=-1.0, wait=[("v", v_scan)])
                p.vop("tensor_scalar_min", F, C, NLA_CLIP)
                s_rinv = p.sop("activation", A, F, AF.Exp,
                               wait=[("v", p.cnt["v"])])
                p.vop("tensor_mul", Eg, Eg, A,
                      wait=[("s", s_rinv)])
                p.vop("tensor_tensor_scan", tcs, c_mask[:],
                      Eg, 0.0, ALU.mult, ALU.add)
                p.vop("tensor_copy", t6[0], tcd[:, CH - 1:SEG:CH],
                      wait=[("s", s_cd)])
                p.vop("tensor_copy", t6[1], tcs[:, CH - 1:SEG:CH])
                p.vop("tensor_mul", t6[2], t6[0], t6[1])
                p.vop("tensor_tensor_scan", t6[3], t6[0], t6[2], 0.0,
                      ALU.mult, ALU.add)
                p.vop("tensor_tensor_scan", t6[4], t6[0], z16[:], 1.0,
                      ALU.mult, ALU.add)
                p.vop("tensor_copy", LP[:, o:o + 1], t6[3][:, NCH - 1:NCH])
                v_lp_at[o] = p.vop("tensor_copy", LP[:, KB + o:KB + o + 1],
                                   t6[4][:, NCH - 1:NCH])
                p.vop("memset", t6[5][:, 0:1], 0.0)
                if NCH > 1:
                    p.vop("tensor_copy", t6[5][:, 1:NCH], t6[3][:, 0:NCH - 1])
                sin_b = t6[5].unsqueeze(2).broadcast_to([128, NCH, CH])
                p.vop("tensor_tensor",
                      Eg.rearrange("p (c k) -> p c k", k=CH),
                      tcs.rearrange("p (c k) -> p c k", k=CH), sin_b,
                      ALU.add)
                v_state_w[o] = p.vop(
                    "tensor_mul", states[:, o * SEG:(o + 1) * SEG], Eg,
                    tcd,
                    wait=[("s", s_ph1_end), ("t", t_ph1_end)])
                p.vop("memset", t6[5][:, 0:1], 1.0)
                if NCH > 1:
                    p.vop("tensor_copy", t6[5][:, 1:NCH], t6[4][:, 0:NCH - 1])
                pin_b = t6[5].unsqueeze(2).broadcast_to([128, NCH, CH])
                v_fb_w[o] = p.vop(
                    "tensor_tensor",
                    Fb[:, o * SEG:(o + 1) * SEG].rearrange(
                        "p (c k) -> p c k", k=CH),
                    tcd.rearrange("p (c k) -> p c k", k=CH), pin_b,
                    ALU.mult)
                chain_v_end[o] = p.cnt["v"]
            v_scan_end = p.cnt["v"]
            s_scan_end = p.cnt["s"]
            LPg = LP[:].rearrange("p (h o) -> p h o", h=2)
            wo_chunk = []
            for k in range(KB):
                if k < 6:
                    off = 16384 + 2048 * k
                elif k < 10:
                    off = 28672 + 2048 * (k - 6)
                elif k < 12:
                    off = 8192 + 2048 * (k - 10)
                else:
                    off = 2048 * (k - 12)
                wo_chunk.append(arenaB[:, off:off + 2048])
            ldcc_a = p.dma("gpsimd", "dcc", cc_in_a.ap(), LPg[:, :, 0:KSPL],
                           wait=[("v", v_lp_at[KSPL - 1])])
            p.wait("gpsimd", ldcc_a)
            p.op("gpsimd", "collective_compute", "AllGather", ALU.bypass,
                 [[r for r in range(NCORES)]],
                 ins=[cc_in_a.ap().opt()], outs=[cc_out_a.ap().opt()],
                 inc="cc", inc_k=1)
            ldwo = {}
            for k in range(6):
                ldwo[k] = p.dma("gpsimd", "dwo", wo_chunk[k],
                                w_o_r[k * 128:(k + 1) * 128, :],
                                wait=[("t", t_w_end[("v", KB - 1)])])
            ldcc_b = p.dma("gpsimd", "dcc", cc_in_b.ap(), LPg[:, :, KSPL:KB],
                           wait=[("v", v_lp_at[KB - 1])])
            p.wait("gpsimd", ldcc_b)
            p.op("gpsimd", "collective_compute", "AllGather", ALU.bypass,
                 [[r for r in range(NCORES)]],
                 ins=[cc_in_b.ap().opt()], outs=[cc_out_b.ap().opt()],
                 inc="cc", inc_k=1)
            ldg_a = p.dma("gpsimd", "dcc", gath[:, :, 0:2 * KSPL],
                          cc_out_a.ap().rearrange("(r p) f -> p r f",
                                                  r=NCORES), wait=[("cc", 1)])
            fix_w = {}

            def carry_chain(o0, o1, g_states, g_prods, wld):
                nn = o1 - o0
                rb = c_rmask[:].unsqueeze(2).broadcast_to([128, NCORES, nn])
                p.gop("tensor_tensor", cr_t1[:, :, o0:o1], g_states, rb,
                      ALU.mult, wait=[wld])
                p.gop("tensor_scalar_add", cr_t2[:, :, o0:o1], g_prods, -1.0)
                p.gop("tensor_tensor", cr_t2[:, :, o0:o1], cr_t2[:, :, o0:o1],
                      rb, ALU.mult)
                p.gop("tensor_scalar_add", cr_t2[:, :, o0:o1],
                      cr_t2[:, :, o0:o1], 1.0)
                p.gop("memset", carry[:, o0:o1], 0.0)
                for r in range(NCORES):
                    p.gop("tensor_mul", cr_tmp[:, o0:o1], cr_t2[:, r, o0:o1],
                          carry[:, o0:o1])
                    p.gop("tensor_add", carry[:, o0:o1], cr_tmp[:, o0:o1],
                          cr_t1[:, r, o0:o1])
                return p.cnt["g"]

            def fix_gp(o):
                cb = carry[:, o:o + 1].broadcast_to([128, SEG])
                p.gop("tensor_tensor", fix_tmp[:],
                      Fb[:, o * SEG:(o + 1) * SEG], cb, ALU.mult,
                      wait=[("v", v_fb_w[o])])
                fix_w[o] = ("g", p.gop(
                    "tensor_add", states[:, o * SEG:(o + 1) * SEG],
                    states[:, o * SEG:(o + 1) * SEG], fix_tmp[:],
                    wait=[("v", v_state_w[o])]))

            def fix_vec(o, g_carry):
                fix_w[o] = ("v", p.vop(
                    "scalar_tensor_tensor",
                    states[:, o * SEG:(o + 1) * SEG],
                    Fb[:, o * SEG:(o + 1) * SEG], carry[:, o:o + 1],
                    states[:, o * SEG:(o + 1) * SEG], ALU.mult, ALU.add,
                    wait=[("g", g_carry)]))

            g_carry_a = carry_chain(0, KSPL, gath[:, :, 0:KSPL],
                                    gath[:, :, KSPL:2 * KSPL], ldg_a)
            for o in range(6):
                fix_gp(o)
            for o in range(6, KSPL):
                fix_vec(o, g_carry_a)
            for k in range(6, 10):
                ldwo[k] = p.dma("gpsimd", "dwo", wo_chunk[k],
                                w_o_r[k * 128:(k + 1) * 128, :],
                                wait=[("v", v_scan_end), ("s", s_scan_end)])
            for k in range(10, 12):
                ldwo[k] = p.dma("gpsimd", "dwo", wo_chunk[k],
                                w_o_r[k * 128:(k + 1) * 128, :],
                                wait=[fix_w[8 + 2 * (k - 10)],
                                      fix_w[9 + 2 * (k - 10)]])
            for k in range(12, KB):
                ldwo[k] = p.dma("gpsimd", "dwo", wo_chunk[k],
                                w_o_r[k * 128:(k + 1) * 128, :],
                                wait=[fix_w[2 * (k - 12)],
                                      fix_w[2 * (k - 12) + 1]])
            ldg_b = p.dma("gpsimd", "dcc", gath[:, :, 2 * KSPL:2 * KB],
                          cc_out_b.ap().rearrange("(r p) f -> p r f",
                                                  r=NCORES), wait=[("cc", 2)])
            g_carry_b = carry_chain(KSPL, KB, gath[:, :, 2 * KSPL:KSPL + KB],
                                    gath[:, :, KSPL + KB:2 * KB], ldg_b)
            for o in range(KSPL, KB):
                fix_vec(o, g_carry_b)
            g_fix_end = p.cnt["g"]
            v_fix_end = p.cnt["v"]

        h2T = hT
        with ExitStack() as ph4:
            E4 = ph4.enter_context
            ps_o = [E4(nc.psum_tensor("ps_o%d" % i, [128, 512], F32))
                    for i in range(NB)]
            ps_tr = [E4(nc.psum_tensor("ps_tr4%d" % i, [128, 512], BF16))
                     for i in range(2)]
            scr4 = [scr_big[:], arenaF[:, 18432:20480]]
            xa4 = [xa_t[:], arenaF[:, 6144:8192]]
            hb4 = [hb[:], scan_x[:].bitcast(BF16)]
            stat4 = [stat[:], stat4b[:]]
            t_scan_mm = t_w_end[("v", KB - 1)]
            st4 = {}

            def mm4(m):
                par = m % 2
                tmm = {}
                for k in range(KB):
                    for n in range(NB):
                        waits = []
                        if n == 0 and m == 0:
                            waits = [ldwo[k], fix_w[k]]
                            if k == 0:
                                waits += [("s", s_ps_free), ("v", v_ps_free)]
                        elif k == 0 and m > 0:
                            waits = [("v", st4[m - 1]["evn"][n])]
                        p.top("matmul", ps_o[n][:],
                              states[:, k * SEG + m * TB:k * SEG + (m + 1) * TB],
                              wo_chunk[k][:, n * 512:(n + 1) * 512],
                              start=(k == 0), stop=(k == KB - 1),
                              wait=waits,
                              inc=(k == KB - 1))
                        if k == KB - 1:
                            tmm[n] = p.cnt["t"]
                st4.setdefault(m, {})["tmm"] = tmm

            def lh4(m):
                par = m % 2
                prev = st4.get(m - 2)
                w_ldx = [("v", v_scan_end), ("s", s_scan_end)]
                if prev:
                    w_ldx = [("v", prev["v_ln1"]), ("s", prev["s_sq"])]
                ldx = p.dma("sync", "dx", x0_t[par][:],
                            x_in[m * TB:(m + 1) * TB, :], wait=w_ldx)
                w_st = [ldx]
                if prev:
                    w_st += [("v", prev["v_hb"]), ("t", prev["t_tr"])]
                s_r = stats_full(x0_t[par][:], hb4[par], eps_ln, True,
                                 wait=w_st, st=stat4[par])
                st4.setdefault(m, {})["v_ln1"] = p.vop(
                    "tensor_scalar", scr4[par], x0_t[par][:],
                    stat4[par][:, 2:3], stat4[par][:, 4:5],
                    ALU.subtract, ALU.mult, wait=[("s", s_r)])

            def ev4(m):
                par = m % 2
                prev = st4.get(m - 2)
                evn = {}
                for n in range(NB):
                    w_ev = [("t", st4[m]["tmm"][n])]
                    if m == 0:
                        w_ev.append(("s", s_scan_end))
                    if m == 1:
                        w_ev += [("g", g_fix_end), ("v", v_fix_end)]
                    if prev:
                        w_ev += [prev["sp_xa"], ("s", prev["s_sq"])]
                    p.vop("tensor_copy", xa4[par][:, n * 512:(n + 1) * 512],
                          ps_o[n][:], wait=w_ev)
                    evn[n] = p.cnt["v"]
                st4[m]["evn"] = evn
                p.vop("tensor_add", xa4[par], xa4[par], scr4[par])

            def tail4(m):
                par = m % 2
                s_r2 = stats_full(xa4[par], None, eps_rms, False,
                                  wait=[("v", p.cnt["v"])], st=stat4[par])
                st4[m]["s_sq"] = s_r2
                v_hb = p.vop("tensor_scalar_mul", hb4[par], xa4[par],
                             stat4[par][:, 4:5], wait=[("s", s_r2)])
                st4[m]["v_hb"] = v_hb
                copy_cnt = {}
                for g in range(KB // 4):
                    pp = ps_tr[g % 2]
                    for j in range(4):
                        k = g * 4 + j
                        w = []
                        if j == 0:
                            if g < 2:
                                w = [("v", v_hb)]
                                if m > 0:
                                    w.append(("v", st4[m - 1]["tr_copies"]))
                            else:
                                w = [("v", copy_cnt[g - 2])]
                        p.top("transpose", pp[:, j * 128:(j + 1) * 128],
                              hb4[par][:, k * 128:(k + 1) * 128], c_id[:],
                              wait=w, inc=(j == 3))
                    wsl = c_wff[:, g * 4:g * 4 + 4].unsqueeze(2)                         .broadcast_to([128, 4, TB])
                    copy_cnt[g] = p.vop(
                        "tensor_tensor",
                        h2T[:].rearrange("p (k t) -> p k t", t=SEG)
                        [:, g * 4:g * 4 + 4, m * TB:(m + 1) * TB],
                        pp[:].rearrange("p (k t) -> p k t", t=TB),
                        wsl, ALU.mult,
                        wait=[("t", p.cnt["t"]), ("t", t_scan_mm)])
                st4[m]["tr_copies"] = copy_cnt[KB // 4 - 1]
                st4[m]["t_tr"] = p.cnt["t"]
                st4[m]["sp_xa"] = p.dma("gpsimd", "dxa",
                                        xa_sp.ap()[m * TB:(m + 1) * TB, :],
                                        xa4[par], wait=[("v", v_hb)])

            mm4(0)
            lh4(0)
            for m in range(NTB):
                ev4(m)
                if m + 1 < NTB:
                    lh4(m + 1)
                    mm4(m + 1)
                tail4(m)

        def gu_down_phase(tag, srcT, acc_ap, gate_src, up_src, dn_rhs_dma,
                          fuse_buf, dn_ring, t_silu, wg_t, wu_t,
                          sg_col=None, su_col=None):
            v_src_done = p.cnt["v"]
            t_entry = p.cnt["t"]
            s_entry = p.cnt["s"]
            dx1_entry = [("dx1s0", p.cnt["dx1s0"]),
                         ("dx1s1", p.cnt["dx1s1"])]
            ev_bank = {m: 0 for m in range(NTB)}
            ev_all = 0
            evac_final = [0] * NTB
            t_gu_last = t_entry
            lastpar = {0: t_entry, 1: t_entry}
            lastring = {}
            for q in range(NQ):
                with ExitStack() as pq:
                    ps_g = [pq.enter_context(
                        nc.psum_tensor("ps_g%s_%d_%d" % (tag, q, i),
                                       [128, SEG], F32)) for i in range(2)]
                    ps_u = [pq.enter_context(
                        nc.psum_tensor("ps_u%s_%d_%d" % (tag, q, i),
                                       [128, SEG], F32)) for i in range(2)]
                    sgv = {}
                    fu = {}
                    wl = {}

                    def load_pair(oi, q=q, wl=wl):
                        o = q * OHQ + oi
                        par = oi % 2
                        tw = [("t", lastpar[par])]
                        if q == 0 and oi == 0:
                            tw = [("t", t_entry), ("v", v_src_done),
                                  ("s", s_entry)]
                        a_ = p.dma("gpsimd", "wg", wg_t[par],
                                   gate_src[:, o, :], wait=tw)
                        b_ = p.dma("gpsimd", "wu", wu_t[par], up_src[:, o, :])
                        wl[oi] = (a_, b_)

                    load_pair(0)
                    for oi in range(OHQ):
                        o = q * OHQ + oi
                        par = oi % 2
                        if oi + 1 < OHQ:
                            load_pair(oi + 1)
                        for (wt, ps, wld) in ((wg_t[par], ps_g[par], wl[oi][0]),
                                              (wu_t[par], ps_u[par], wl[oi][1])):
                            for k in range(KB):
                                for n in range(NMM):
                                    waits = []
                                    if n == 0 and k == 0:
                                        waits = [wld, ("v", v_src_done),
                                                 ("s", sgv.get(par, 0)),
                                                 ("v", max(fu.get(par, 0),
                                                           ev_all))]
                                    p.top("matmul",
                                          ps[:, n * MMW:(n + 1) * MMW],
                                          wt[:, k * 128:(k + 1) * 128],
                                          srcT[:, k * SEG + n * MMW:
                                               k * SEG + (n + 1) * MMW],
                                          start=(k == 0), stop=(k == KB - 1),
                                          wait=waits,
                                          inc=(k == KB - 1 and n == NMM - 1))
                        wl[("t", oi)] = p.cnt["t"]
                        lastpar[par] = p.cnt["t"]
                        silu_sl = t_silu[:, par * SEG:(par + 1) * SEG]
                        guard = ([("v", v_src_done)] + dx1_entry
                                 if (q == 0 and oi == 0) else [])
                        if sg_col is None:
                            s_g = p.sop("activation", silu_sl, ps_g[par][:],
                                        AF.Silu, wait=[("t", wl[("t", oi)]),
                                                       ("v", fu.get(par, 0))]
                                        + guard)
                            f_ = p.vop("tensor_mul",
                                       fuse_buf[:, oi * SEG:(oi + 1) * SEG],
                                       silu_sl, ps_u[par][:],
                                       wait=[("s", s_g)])
                        else:
                            s_g = p.sop("activation", silu_sl, ps_g[par][:],
                                        AF.Silu, scale=sg_col[:, o:o + 1],
                                        wait=[("t", wl[("t", oi)]),
                                              ("v", fu.get(par, 0))] + guard)
                            f_ = p.vop("scalar_tensor_tensor",
                                       fuse_buf[:, oi * SEG:(oi + 1) * SEG],
                                       ps_u[par][:], su_col[:, o:o + 1],
                                       silu_sl, ALU.mult, ALU.mult,
                                       wait=[("s", s_g)])
                        sgv[par] = s_g
                        fu[par] = f_
                    v_fuse_done = p.cnt["v"]
                    s_last = p.cnt["s"]
                    if q == NQ - 1:
                        t_gu_last = p.cnt["t"]
                with ExitStack() as pd:
                    ps_d = [pd.enter_context(
                        nc.psum_tensor("ps_d%s_%d_%d" % (tag, q, i),
                                       [128, 512], F32))
                        for i in range(min(8, NTB))]
                    for n in range(NB):
                        tm = {}
                        for k in range(KHQ):
                            r = k % len(dn_ring)
                            ring = dn_ring[r]
                            tw = [("t", lastring.get(r, t_entry))]
                            ldd = dn_rhs_dma(q, n, k, ring, tw)
                            for m in range(NTB):
                                waits = []
                                if m == 0:
                                    waits = [ldd]
                                if k == 0:
                                    waits = list(waits) + [("v", ev_bank[m])]
                                    if n == 0:
                                        waits += [("v", v_fuse_done),
                                                  ("s", s_last)]
                                p.top("matmul", ps_d[m % 8][:],
                                      fuse_buf[:, k * SEG + m * TB:
                                               k * SEG + (m + 1) * TB],
                                      ring,
                                      start=(k == 0), stop=(k == KHQ - 1),
                                      wait=waits,
                                      inc=(m == NTB - 1 or k == KHQ - 1))
                                if k == KHQ - 1:
                                    tm[m] = p.cnt["t"]
                            lastring[r] = p.cnt["t"]
                        for m in range(NTB):
                            dst = acc_ap[:, m * D + n * 512:m * D + (n + 1) * 512]
                            if q == 0:
                                p.vop("tensor_copy", dst, ps_d[m % 8][:],
                                      wait=[("t", tm[m]),
                                            ("dxa0", 16 * (NTB // 2)),
                                            ("dxa1", 16 * (NTB // 2))])
                            else:
                                p.vop("tensor_add", dst, dst, ps_d[m % 8][:],
                                      wait=[("t", tm[m])])
                            ev_bank[m] = p.cnt["v"]
                            if q == NQ - 1 and n == NB - 1:
                                evac_final[m] = p.cnt["v"]
                    ev_all = max(ev_bank.values())
            return dict(ev_all=ev_all, evac_final=evac_final,
                        t_gu_last=t_gu_last)

        with ExitStack() as ph5:
            E5 = ph5.enter_context
            assert OHQ * SEG == KB * SEG
            fuse5 = states[:, 0:OHQ * SEG]
            fuse7 = hT[:, 0:OHQ * SEG]
            acc = arenaF[:, 0:NTB * D]
            t_silu = scr_big[:]
            KW = KB * 128
            x0b = [x0_t[i][:].bitcast(BF16) for i in range(2)]
            wg_t = [x0b[0][:, 0:KW], x0b[0][:, KW:2 * KW]]
            wu_t = [x0b[1][:, 0:KW], x0b[1][:, KW:2 * KW]]
            nring = max(1, min(4, D // 512))
            dn_ring = [hb[:, r * 512:(r + 1) * 512] for r in range(nring)]

            def dn_dma(qq, n, k, ring, tw):
                row0 = qq * (FFN // NQ) + k * 128
                return p.dma("gpsimd", "dn", ring,
                             w_dn_r[row0:row0 + 128, n * 512:(n + 1) * 512],
                             wait=tw)

            info5 = gu_down_phase('5', h2T, acc, w_gu_l[:, 0:OBF, :],
                                  w_gu_l[:, OBF:2 * OBF, :], dn_dma,
                                  fuse5, dn_ring, t_silu, wg_t, wu_t)

            ln2T = states
            t_p5_end = p.cnt["t"]
            s_p5_end = p.cnt["s"]
            t_gu5 = info5["t_gu_last"]
            ev5 = info5["evac_final"]
            ev5_all = info5["ev_all"]
            hTf = hT[:].bitcast(F32)
            scrb16 = scr_big[:].bitcast(BF16)
            scr6 = [hTf[:, 0:2048], hTf[:, 2048:4096]]
            xat6 = [hTf[:, 4096:6144], xa_t[:]]
            stat6 = [stat[:], hTf[:, 6144:6176]]
            hb6 = [scrb16[:, 0:2048], scrb16[:, 2048:4096]]
            dump6 = [arenaF[:, 16384:18432], arenaF[:, 18432:20480]]
            with ExitStack() as ph6:
                ps_tr6 = [ph6.enter_context(
                    nc.psum_tensor("ps_tr6%d" % i, [128, 512], BF16))
                    for i in range(2)]
                st6 = {}

                def S6_0(m):
                    par = m % 2
                    buf = x0_t[par]
                    prev = st6.get(m - 2)
                    w = [("t", t_gu5),
                         ("dxa%d" % (m % 2), 16 * (m // 2 + 1))]
                    if prev:
                        w += [("v", prev["v_x1"])]
                    ldxa = p.dma("sync", "dx", buf[:],
                                 xa_sp.ap()[m * TB:(m + 1) * TB, :], wait=w)
                    w2 = [ldxa, ("v", ev5[m]), ("t", t_gu5)]
                    if prev:
                        w2 += [("s", prev["s_end"]), prev["sp_x1"]]
                    v_ao = p.vop("tensor_add", scr6[par], buf[:],
                                 acc[:, m * D:(m + 1) * D], wait=w2)
                    st6[m] = dict(v_ao=v_ao)

                def S6_1(m):
                    par = m % 2
                    prev = st6.get(m - 2)
                    w = [("v", st6[m]["v_ao"])]
                    if prev:
                        w.append(("v", prev["v_hb"]))
                    st6[m]["s_r"] = stats_full(scr6[par], dump6[par],
                                               eps_ln, True, wait=w,
                                               st=stat6[par])
                    st6[m]["ldx0"] = p.dma("sync", "dx", x0_t[par][:],
                                           x_in[m * TB:(m + 1) * TB, :],
                                           wait=[("v", st6[m]["v_ao"])])

                def S6_2(m):
                    par = m % 2
                    s_r = st6[m]["s_r"]
                    p.vop("tensor_scalar", xat6[par], scr6[par],
                          stat6[par][:, 2:3], stat6[par][:, 4:5],
                          ALU.subtract, ALU.mult,
                          wait=[("s", s_r), ("dxa0", 16 * (NTB // 2)),
                                ("dxa1", 16 * (NTB // 2))])

                def S6_3(m):
                    par = m % 2
                    v_x1 = p.vop("tensor_add", scr6[par], xat6[par],
                                 x0_t[par][:], wait=[st6[m]["ldx0"]])
                    st6[m]["v_x1"] = v_x1
                    st6[m]["sp_x1"] = p.dma("gpsimd", "dx1s",
                                            x1_sp.ap()[m * TB:(m + 1) * TB, :],
                                            scr6[par], wait=[("v", v_x1)])
                    st6[m]["s_r2"] = stats_full(scr6[par], xat6[par], eps_ln,
                                                True, wait=[("v", v_x1)],
                                                st=stat6[par])
                    st6[m]["s_end"] = p.cnt["s"]

                def S6_4(m):
                    par = m % 2
                    prev = st6.get(m - 2)
                    w = [("s", st6[m]["s_r2"]), ("s", s_p5_end)]
                    if prev:
                        w.append(("t", prev["t_tr"]))
                    st6[m]["v_hb"] = p.vop(
                        "tensor_scalar", hb6[par], scr6[par],
                        stat6[par][:, 2:3], stat6[par][:, 4:5],
                        ALU.subtract, ALU.mult, wait=w)

                def S6_5(m):
                    par = m % 2
                    copy_cnt = {}
                    for g in range(KB // 4):
                        pp = ps_tr6[g % 2]
                        for j in range(4):
                            k = g * 4 + j
                            w = []
                            if j == 0:
                                if g < 2:
                                    w = [("v", st6[m]["v_hb"])]
                                    if m == 0:
                                        w.append(("v", ev5_all))
                                    else:
                                        w.append(("v",
                                                  st6[m - 1]["tr_copies"]))
                                else:
                                    w = [("v", copy_cnt[g - 2])]
                            p.top("transpose", pp[:, j * 128:(j + 1) * 128],
                                  hb6[par][:, k * 128:(k + 1) * 128],
                                  c_id[:], wait=w, inc=(j == 3))
                        copy_cnt[g] = p.vop(
                            "tensor_copy",
                            ln2T[:].rearrange("p (k t) -> p k t", t=SEG)
                            [:, g * 4:g * 4 + 4, m * TB:(m + 1) * TB],
                            pp[:].rearrange("p (k t) -> p k t", t=TB),
                            wait=[("t", p.cnt["t"]), ("t", t_p5_end)])
                    st6[m]["tr_copies"] = copy_cnt[KB // 4 - 1]
                    st6[m]["t_tr"] = p.cnt["t"]

                stages6 = [S6_0, S6_1, S6_2, S6_3, S6_4, S6_5]
                for m0 in range(0, NTB, 2):
                    for sfn in stages6:
                        sfn(m0)
                        sfn(m0 + 1)

            def dn8_dma(qq, n, k, ring, tw):
                row0 = qq * (FFN // NQ) + k * 128
                return p.dma("gpsimd", "dn", ring,
                             wd8_r[n, row0:row0 + 128, :], wait=tw)

            info7 = gu_down_phase('7', ln2T, acc, wg8_l, wu8_l, dn8_dma,
                                  fuse7, dn_ring, t_silu, wg_t, wu_t,
                                  sg_col=c_sg[:], su_col=c_su[:])

            t_p7_end = p.cnt["t"]
            t_gu7 = info7["t_gu_last"]
            ev7 = info7["evac_final"]
            statesf = states[:].bitcast(F32)
            scr8 = [hTf[:, 0:2048], hTf[:, 2048:4096]]
            xat8 = [hTf[:, 4096:6144], xa_t[:]]
            stat8 = [stat[:], hTf[:, 6144:6176]]
            dump8 = [statesf[:, 0:2048], statesf[:, 2048:4096]]
            st8 = {}

            def S8_0(m):
                par = m % 2
                prev = st8.get(m - 2)
                w = [("t", t_gu7),
                     ("dx1s%d" % (m % 2), 16 * (m // 2 + 1))]
                if prev:
                    w.append(("v", prev["v_o"]))
                ldx1 = p.dma("sync", "dx", x0_t[par][:],
                             x1_sp.ap()[m * TB:(m + 1) * TB, :], wait=w)
                w2 = [("v", ev7[m]), ld_sdb, ("t", t_p7_end)]
                if prev:
                    w2 += [("s", prev["s_end"]), prev["dy"]]
                v_y3 = p.vop("tensor_mul", scr8[par],
                             acc[:, m * D:(m + 1) * D], sdb[:], wait=w2)
                st8[m] = dict(ldx1=ldx1, v_y3=v_y3)

            def S8_1(m):
                par = m % 2
                st8[m]["s_r"] = stats_full(scr8[par], dump8[par], eps_ln,
                                           True, wait=[("v", st8[m]["v_y3"])],
                                           st=stat8[par])
                st8[m]["s_end"] = p.cnt["s"]

            def S8_2(m):
                par = m % 2
                p.vop("tensor_scalar", xat8[par], scr8[par],
                      stat8[par][:, 2:3], stat8[par][:, 4:5],
                      ALU.subtract, ALU.mult, wait=[("s", st8[m]["s_r"])])

            def S8_3(m):
                par = m % 2
                v_o = p.vop("tensor_add", scr8[par], xat8[par], x0_t[par][:],
                            wait=[st8[m]["ldx1"]])
                st8[m]["v_o"] = v_o
                st8[m]["dy"] = p.dma("sync", "dy",
                                     y_out[m * TB:(m + 1) * TB, :],
                                     scr8[par], wait=[("v", v_o)])

            stages8 = [S8_0, S8_1, S8_2, S8_3]
            for m0 in range(0, NTB, 2):
                for sfn in stages8:
                    sfn(m0)
                    sfn(m0 + 1)
            p.wait("sync", st8[NTB - 2]["dy"], st8[NTB - 1]["dy"])

        with nc.Block() as block:
            p.finalize(block)

    return nc


def _lhsT_layout(W, dtype):
    Din, Dout = W.shape
    KBl, OBl = Din // 128, Dout // 128
    return np.ascontiguousarray(
        W.reshape(KBl, 128, OBl, 128).transpose(1, 2, 0, 3)
        .reshape(128, OBl, KBl * 128)).astype(dtype, copy=False)


def _colT(vec):
    Dl = vec.shape[0]
    return np.ascontiguousarray(vec.reshape(Dl // 128, 128).T)


def make_inputs(cfg, fi):
    D, FFN, SEG, NSEG, Bb = (cfg[k] for k in ("D", "FFN", "SEG", "NSEG", "B"))
    NCORES = Bb * NSEG
    NB = D // 512
    x = np.asarray(fi["L0_attn_x"], np.float32)

    shared = {
        "w_a_l": _lhsT_layout(np.asarray(fi["L0_attn_w_a"], np.float32), np.float32),
        "w_i_l": _lhsT_layout(np.asarray(fi["L0_attn_w_i"], np.float32), np.float32),
        "w_v_l": _lhsT_layout(np.asarray(fi["L0_attn_w_v"], np.float32), np.float32),
        "w_o_r": np.ascontiguousarray(np.asarray(fi["L0_attn_out_proj_w"], np.float32)),
        "w_gu_l": _lhsT_layout(np.asarray(fi["L0_attn_w_gate_up"], np.float32),
                               np.float32),
        "w_dn_r": np.ascontiguousarray(np.asarray(fi["L0_attn_w_down"], np.float32)),
        "wg8_l": _lhsT_layout(np.asarray(fi["L0_ffn_w_gate"], np.int8), np.int8),
        "wu8_l": _lhsT_layout(np.asarray(fi["L0_ffn_w_up"], np.int8), np.int8),
        "wd8_r": np.ascontiguousarray(
            np.asarray(fi["L0_ffn_w_down"], np.int8).reshape(FFN, NB, 512)
            .transpose(1, 0, 2)),
        "biasT": _colT(np.asarray(fi["L0_attn_decay_bias"], np.float32)),
        "wpreT": _colT(np.asarray(fi["L0_attn_pre_norm_w"], np.float32)),
        "wffT": _colT(np.asarray(fi["L0_attn_ffn_norm_w"], np.float32)),
        "sgT": _colT(np.asarray(fi["L0_ffn_s_gate"], np.float32)),
        "suT": _colT(np.asarray(fi["L0_ffn_s_up"], np.float32)),
        "sd_row": np.ascontiguousarray(
            np.asarray(fi["L0_ffn_s_down"], np.float32)[None, :]),
        "identity": np.eye(128, dtype=BF),
        "ones_row": np.ones((1, 128), BF),
    }
    scanmask = np.ones((128, SEG), np.float32)
    scanmask[:, ::CH] = 0.0
    shared["scanmask"] = scanmask

    in_maps = []
    for c in range(NCORES):
        b, s0 = divmod(c, NSEG)
        rmask = np.zeros((128, NCORES), np.float32)
        for r in range(NCORES):
            if r // NSEG == b and (r % NSEG) < s0:
                rmask[:, r] = 1.0
        m = dict(shared)
        m["x"] = np.ascontiguousarray(x[b, s0 * SEG:(s0 + 1) * SEG, :])
        m["rankmask"] = rmask
        in_maps.append(m)
    return in_maps


def kernel(**inputs):
    cfg = FULL
    NCORES = cfg["B"] * cfg["NSEG"]
    nc = build_nc(cfg)
    in_maps = make_inputs(cfg, inputs)
    res = run_bass_kernel_spmd(nc, in_maps, core_ids=list(range(NCORES)))
    SEG, D = cfg["SEG"], cfg["D"]
    out = np.empty((cfg["B"], cfg["NSEG"] * SEG, D), np.float32)
    for c in range(NCORES):
        b, s0 = divmod(c, cfg["NSEG"])
        out[b, s0 * SEG:(s0 + 1) * SEG, :] = res.results[c]["y"]
    return out


# version 20
# speedup vs baseline: 1.0726x; 1.0035x over previous
from contextlib import ExitStack

import numpy as np
import ml_dtypes

import concourse.bass as bass
import concourse.mybir as mybir
from concourse.bass_utils import run_bass_kernel_spmd

F32 = mybir.dt.float32
BF16 = mybir.dt.bfloat16
I8 = mybir.dt.int8
AF = mybir.ActivationFunctionType
ALU = mybir.AluOpType
BF = ml_dtypes.bfloat16

NLA_CLIP = 23.025851
CH = 64
TB = 128

FULL = dict(D=2048, FFN=8192, SEG=1024, NSEG=4, B=2)
MINI = dict(D=512, FFN=2048, SEG=256, NSEG=2, B=2)

DMA_STREAMS = {"dC": 1, "dx": 4, "wa": 2, "wi": 2, "wv": 2, "dwo": 16,
               "wg": 2, "wu": 2, "dn": 4, "dxa": 2, "dx1s": 2, "dcc": 1,
               "dy": 2}


class Prog:

    ENGINES = ("sync", "scalar", "vector", "tensor", "gpsimd")

    def __init__(self, nc):
        self.nc = nc
        self.ops = {e: [] for e in self.ENGINES}
        self.cnt = {}
        self.sems = {}
        self.duse = {}
        self._waited = {e: {} for e in self.ENGINES}

    def make_sems(self, ctx, names):
        for n in names:
            self.sems[n] = ctx.enter_context(self.nc.semaphore("sem_" + n))
            self.cnt[n] = 0

    def make_dma_streams(self, ctx, streams):
        for s, n in streams.items():
            for i in range(n):
                name = s + str(i)
                self.sems[name] = ctx.enter_context(
                    self.nc.semaphore("sem_" + name))
                self.cnt[name] = 0
            self.duse[s] = 0

    def _dedup(self, engine, waits):
        wl = []
        for w in waits:
            if w is None:
                continue
            sname, val = w
            if val is None or val <= 0:
                continue
            if self._waited[engine].get(sname, 0) >= val:
                continue
            self._waited[engine][sname] = val
            wl.append((self.sems[sname], val))
        return wl

    def op(self, engine, method, *args, wait=(), inc=None, inc_k=1, **kwargs):
        wl = self._dedup(engine, list(wait))
        ret = None
        sem = None
        if inc is not None:
            self.cnt[inc] += inc_k
            ret = self.cnt[inc]
            sem = self.sems[inc]

        def go(eng, wl=wl, method=method, args=args, kwargs=kwargs, sem=sem,
               inc_k=inc_k):
            for (s, val) in wl:
                eng.wait_ge(s, val)
            inst = getattr(eng, method)(*args, **kwargs)
            if sem is not None:
                inst.then_inc(sem, inc_k)

        self.ops[engine].append(go)
        return ret

    def wait(self, engine, *waits):
        wl = self._dedup(engine, list(waits))
        if wl:
            def go(eng, wl=wl):
                for (s, val) in wl:
                    eng.wait_ge(s, val)
            self.ops[engine].append(go)

    def vop(self, method, *args, wait=(), **kwargs):
        return self.op("vector", method, *args,
                       wait=[("v", self.cnt["v"])] + list(wait), inc="v", **kwargs)

    def sop(self, method, *args, wait=(), **kwargs):
        return self.op("scalar", method, *args,
                       wait=[("s", self.cnt["s"])] + list(wait), inc="s", **kwargs)

    def gop(self, method, *args, wait=(), **kwargs):
        return self.op("gpsimd", method, *args,
                       wait=[("g", self.cnt["g"])] + list(wait), inc="g", **kwargs)

    def top(self, method, *args, wait=(), inc=False, **kwargs):
        return self.op("tensor", method, *args, wait=wait,
                       inc=("t" if inc else None), **kwargs)

    def dma(self, queue, stream, dst, src, wait=()):
        n = DMA_STREAMS[stream]
        use = self.duse[stream]
        self.duse[stream] = use + 1
        semname = stream + str(use % n)
        self.op(queue, "dma_start", dst, src, wait=wait, inc=semname, inc_k=16)
        return (semname, self.cnt[semname])

    def finalize(self, block):
        for e in self.ENGINES:
            fns = self.ops[e]
            if not fns:
                continue

            def emit(eng, fns=fns):
                for f in fns:
                    f(eng)

            getattr(block, e)(emit)


def build_nc(cfg):
    D, FFN, SEG = cfg["D"], cfg["FFN"], cfg["SEG"]
    KB = D // 128
    NTB = SEG // TB
    NCH = SEG // CH
    OBF = FFN // 128
    NB = D // 512
    NCORES = cfg["B"] * cfg["NSEG"]
    NQ = 4 if OBF % 8 == 0 else 2
    OHQ = OBF // NQ
    KHQ = OHQ
    MMW = min(SEG, 512)
    NMM = SEG // MMW
    eps_ln, eps_rms = 1e-5, 1e-6

    nc = bass.Bass("TRN2", target_bir_lowering=False)

    def din(name, shape, dt=F32):
        return nc.dram_tensor(name, list(shape), dt, kind="ExternalInput").ap()

    x_in = din("x", (SEG, D))
    w_aiv_l = {w: din("w_%s_l" % w, (128, KB, KB * 128)) for w in ("a", "i", "v")}
    w_o_r = din("w_o_r", (D, D))
    w_gu_l = din("w_gu_l", (128, 2 * OBF, KB * 128))
    w_dn_r = din("w_dn_r", (FFN, D))
    wg8_l = din("wg8_l", (128, OBF, KB * 128), I8)
    wu8_l = din("wu8_l", (128, OBF, KB * 128), I8)
    wd8_r = din("wd8_r", (NB, FFN, 512), I8)
    biasT = din("biasT", (128, KB))
    wpreT = din("wpreT", (128, KB))
    wffT = din("wffT", (128, KB))
    sgT = din("sgT", (128, OBF))
    suT = din("suT", (128, OBF))
    sd_row = din("sd_row", (1, D))
    scanmask = din("scanmask", (128, SEG))
    identity = din("identity", (128, 128), BF16)
    ones_row = din("ones_row", (1, 128), BF16)
    rankmask = din("rankmask", (128, NCORES))
    y_out = nc.dram_tensor("y", [SEG, D], F32, kind="ExternalOutput").ap()

    KSPL = 12
    cc_in_a = nc.dram_tensor("cc_in_a", [128, 2 * KSPL], F32)
    cc_out_a = nc.dram_tensor("cc_out_a", [NCORES * 128, 2 * KSPL], F32)
    cc_in_b = nc.dram_tensor("cc_in_b", [128, 2 * (KB - KSPL)], F32)
    cc_out_b = nc.dram_tensor("cc_out_b", [NCORES * 128, 2 * (KB - KSPL)], F32)
    xa_sp = nc.dram_tensor("xa_sp", [SEG, D], F32)
    x1_sp = nc.dram_tensor("x1_sp", [SEG, D], F32)

    with ExitStack() as ctx:
        E = ctx.enter_context
        p = Prog(nc)
        p.make_sems(ctx, ["v", "s", "g", "t", "cc"])
        p.make_dma_streams(ctx, DMA_STREAMS)

        hT = E(nc.sbuf_tensor("hT", [128, KB * SEG], BF16))
        states = E(nc.sbuf_tensor("states", [128, KB * SEG], BF16))
        arena = E(nc.sbuf_tensor("arena", [128, 20480], F32))
        arenaF = arena[:]
        arenaB = arena[:].bitcast(BF16)
        c_bias = E(nc.sbuf_tensor("c_bias", [128, KB], F32))
        c_nbias = E(nc.sbuf_tensor("c_nbias", [128, KB], F32))
        c_wpre = E(nc.sbuf_tensor("c_wpre", [128, KB], F32))
        c_wff = E(nc.sbuf_tensor("c_wff", [128, KB], F32))
        c_sg = E(nc.sbuf_tensor("c_sg", [128, OBF], F32))
        c_su = E(nc.sbuf_tensor("c_su", [128, OBF], F32))
        c_mask = E(nc.sbuf_tensor("c_mask", [128, SEG], F32))
        c_id = E(nc.sbuf_tensor("c_id", [128, 128], BF16))
        c_ones = E(nc.sbuf_tensor("c_ones", [1, 128], BF16))
        c_rmask = E(nc.sbuf_tensor("c_rmask", [128, NCORES], F32))
        sdb = E(nc.sbuf_tensor("sdb", [128, D], F32))
        LP = E(nc.sbuf_tensor("LP", [128, 2 * KB], F32))
        gath = E(nc.sbuf_tensor("gath", [128, NCORES, 2 * KB], F32))
        carry = E(nc.sbuf_tensor("carry", [128, KB], F32))
        cr_t1 = E(nc.sbuf_tensor("cr_t1", [128, NCORES, KB], F32))
        cr_t2 = E(nc.sbuf_tensor("cr_t2", [128, NCORES, KB], F32))
        cr_tmp = E(nc.sbuf_tensor("cr_tmp", [128, KB], F32))
        fix_tmp = E(nc.sbuf_tensor("fix_tmp", [128, SEG], BF16))
        scan_x = E(nc.sbuf_tensor("scan_x", [128, SEG], F32))
        t16b = E(nc.sbuf_tensor("t16b", [128, 6 * (SEG // CH)], F32))
        xa_t = E(nc.sbuf_tensor("xa_t", [128, D], F32))
        stat4b = E(nc.sbuf_tensor("stat4b", [128, 8], F32))
        x0_t = [E(nc.sbuf_tensor("x0_t%d" % i, [128, D], F32)) for i in range(2)]
        stat = E(nc.sbuf_tensor("stat", [128, 8], F32))
        scr_big = E(nc.sbuf_tensor("scr_big", [128, D], F32))
        hb = E(nc.sbuf_tensor("hb", [128, D], BF16))
        t16 = [E(nc.sbuf_tensor("t16_%d" % i, [128, NCH], F32)) for i in range(6)]
        z16 = E(nc.sbuf_tensor("z16", [128, NCH], F32))

        p.dma("sync", "dC", c_bias[:], biasT)
        p.dma("sync", "dC", c_wpre[:], wpreT)
        p.dma("sync", "dC", c_wff[:], wffT)
        p.dma("sync", "dC", c_sg[:], sgT)
        p.dma("sync", "dC", c_su[:], suT)
        p.dma("sync", "dC", c_mask[:], scanmask)
        p.dma("sync", "dC", c_id[:], identity)
        p.dma("sync", "dC", c_ones[:], ones_row)
        p.dma("sync", "dC", c_rmask[:], rankmask)
        ld_sdb = p.dma("sync", "dC", sdb[:],
                       sd_row.broadcast_to([128, D]))
        ldC = ld_sdb
        p.vop("memset", z16[:], 0.0)
        p.vop("tensor_scalar_mul", c_nbias[:], c_bias[:], -1.0, wait=[ldC])

        def stats_full(src_ap, dump_ap, eps, mean_too, wait=(), st=None):
            if st is None:
                st = stat[:]
            if mean_too:
                p.sop("activation", dump_ap, src_ap, AF.Identity,
                      accum_out=st[:, 0:1], wait=list(wait))
                wait = ()
            p.sop("activation", dump_ap, src_ap, AF.Square,
                  accum_out=st[:, 1:2], wait=list(wait))
            sw = [("s", p.cnt["s"])]
            if mean_too:
                p.vop("tensor_scalar_mul", st[:, 2:3], st[:, 0:1], 1.0 / D,
                      wait=sw)
                p.vop("tensor_mul", st[:, 3:4], st[:, 2:3], st[:, 2:3])
                p.vop("tensor_scalar", st[:, 6:7], st[:, 1:2], 1.0 / D,
                      st[:, 3:4], ALU.mult, ALU.subtract)
                vv = p.vop("tensor_scalar_add", st[:, 6:7], st[:, 6:7], eps)
            else:
                vv = p.vop("tensor_scalar", st[:, 6:7], st[:, 1:2], 1.0 / D,
                           eps, ALU.mult, ALU.add, wait=sw)
            p.sop("activation", st[:, 5:6], st[:, 6:7], AF.Ln,
                  wait=[("v", vv)])
            return p.sop("activation", st[:, 4:5], st[:, 5:6], AF.Exp,
                         scale=-0.5)

        with ExitStack() as ph1:
            ps_tr = [ph1.enter_context(nc.psum_tensor("ps_tr%d" % i, [128, 512],
                                                      BF16)) for i in range(2)]
            statesP1 = states[:].bitcast(F32)
            scr1 = [scr_big[:], statesP1[:, 0:2048]]
            stat1 = [stat[:], statesP1[:, 2048:2056]]
            hb1 = [hb[:], states[:][:, 4112:6160]]
            st1 = {}

            def P1_0(m):
                par = m % 2
                prev = st1.get(m - 2)
                w = []
                if prev:
                    w = [("s", prev["s_sq"]), ("v", prev["v_ln1"])]
                st1[m] = dict(ldx=p.dma("sync", "dx", x0_t[par][:],
                                        x_in[m * TB:(m + 1) * TB, :], wait=w))

            def P1_1(m):
                par = m % 2
                prev = st1.get(m - 2)
                w = [st1[m]["ldx"]]
                if prev:
                    w.append(("v", prev["v_hb"]))
                st1[m]["s_r"] = stats_full(x0_t[par][:], scr1[par], eps_ln,
                                           True, wait=w, st=stat1[par])

            def P1_2(m):
                par = m % 2
                st1[m]["v_ln1"] = p.vop("tensor_scalar", scr1[par],
                                        x0_t[par][:], stat1[par][:, 2:3],
                                        stat1[par][:, 4:5], ALU.subtract,
                                        ALU.mult, wait=[("s", st1[m]["s_r"])])

            def P1_3(m):
                par = m % 2
                st1[m]["s_sq"] = p.sop("activation", x0_t[par][:], scr1[par],
                                       AF.Square, accum_out=stat1[par][:, 1:2],
                                       wait=[("v", st1[m]["v_ln1"])])
                vv = p.vop("tensor_scalar", stat1[par][:, 6:7],
                           stat1[par][:, 1:2], 1.0 / D, eps_rms, ALU.mult,
                           ALU.add, wait=[("s", p.cnt["s"])])
                p.sop("activation", stat1[par][:, 5:6], stat1[par][:, 6:7],
                      AF.Ln, wait=[("v", vv)])
                st1[m]["s_r2"] = p.sop("activation", stat1[par][:, 4:5],
                                       stat1[par][:, 5:6], AF.Exp, scale=-0.5)

            def P1_4(m):
                par = m % 2
                prev = st1.get(m - 2)
                w = [("s", st1[m]["s_r2"])]
                if prev:
                    w.append(("t", prev["t_tr"]))
                st1[m]["v_hb"] = p.vop("tensor_scalar_mul", hb1[par],
                                       scr1[par], stat1[par][:, 4:5], wait=w)

            def P1_5(m):
                par = m % 2
                copy_cnt = {}
                for g in range(KB // 4):
                    pp = ps_tr[g % 2]
                    for j in range(4):
                        k = g * 4 + j
                        w = []
                        if j == 0:
                            if g < 2:
                                w = [("v", st1[m]["v_hb"])]
                                if m > 0:
                                    w.append(("v", st1[m - 1]["tr_copies"]))
                            else:
                                w = [("v", copy_cnt[g - 2])]
                        p.top("transpose", pp[:, j * 128:(j + 1) * 128],
                              hb1[par][:, k * 128:(k + 1) * 128], c_id[:],
                              wait=w, inc=(j == 3))
                    wsl = c_wpre[:, g * 4:g * 4 + 4].unsqueeze(2)                         .broadcast_to([128, 4, TB])
                    copy_cnt[g] = p.vop(
                        "tensor_tensor",
                        hT[:].rearrange("p (k t) -> p k t", t=SEG)
                        [:, g * 4:g * 4 + 4, m * TB:(m + 1) * TB],
                        pp[:].rearrange("p (k t) -> p k t", t=TB),
                        wsl, ALU.mult, wait=[("t", p.cnt["t"])])
                st1[m]["tr_copies"] = copy_cnt[KB // 4 - 1]
                st1[m]["t_tr"] = p.cnt["t"]

            stages1 = [P1_0, P1_1, P1_2, P1_3, P1_4, P1_5]
            for m0 in range(0, NTB, 2):
                for sfn in stages1:
                    sfn(m0)
                    sfn(m0 + 1)
            s_ph1_end = p.cnt["s"]
            t_ph1_end = p.cnt["t"]

        with ExitStack() as ph2:
            E2 = ph2.enter_context
            Fb = arenaB[:, 0:KB * SEG]
            _wqo = {"a": 0, "i": 1, "v": 2}
            Wq = {w: [arenaB[:, 16384 + 2048 * (2 * _wqo[w] + i):
                             16384 + 2048 * (2 * _wqo[w] + i) + 2048]
                      for i in range(2)]
                  for w in ("a", "i", "v")}
            slA = [arenaF[:, 14336:15360], arenaF[:, 15360:16384]]
            slC = [arenaF[:, 16384:17408], arenaF[:, 17408:18432]]
            slE = [scr_big[:, 0:SEG], scr_big[:, SEG:2 * SEG]]
            slB = [hb[:].bitcast(F32)[:, 0:SEG], arenaF[:, 18432:19456]]
            slD = [arenaF[:, 19456:20480], scan_x[:, 0:SEG]]
            slF = [xa_t[:, 0:SEG], xa_t[:, SEG:2 * SEG]]
            slcd = [x0_t[0][:, 0:SEG], x0_t[0][:, SEG:2 * SEG]]
            slcs = [x0_t[1][:, 0:SEG], x0_t[1][:, SEG:2 * SEG]]
            t16s = [[t16[i][:] for i in range(6)],
                    [t16b[:, i * NCH:(i + 1) * NCH] for i in range(6)]]
            ps_a = E2(nc.psum_tensor("ps_a", [128, SEG], F32))
            ps_i = E2(nc.psum_tensor("ps_i", [128, SEG], F32))
            ps_v = E2(nc.psum_tensor("ps_v", [128, SEG], F32))

            v_hT_done = p.cnt["v"]
            t_w_end = {}
            wload = {}

            def load_aiv(o):
                for w in ("a", "i", "v"):
                    tw = [("t", t_w_end.get((w, o - 2), 0))]
                    wload[(w, o)] = p.dma("gpsimd", "w" + w, Wq[w][o % 2],
                                          w_aiv_l[w][:, o, :], wait=tw)

            load_aiv(0)
            psum_free = {"a": ("s", 0), "i": ("s", 0), "v": ("v", 0)}
            v_lp_at = {}
            v_state_w = {}
            v_fb_w = {}
            chain_v_end = {}
            for o in range(KB):
                if o + 1 < KB:
                    load_aiv(o + 1)
                for (w, ps) in (("a", ps_a), ("i", ps_i), ("v", ps_v)):
                    for k in range(KB):
                        for n in range(NMM):
                            waits = []
                            if n == 0 and k == 0:
                                waits = [wload[(w, o)], psum_free[w]]
                                if o == 0 and w == "a":
                                    waits.append(("v", v_hT_done))
                            p.top("matmul", ps[:, n * MMW:(n + 1) * MMW],
                                  Wq[w][o % 2][:, k * 128:(k + 1) * 128],
                                  hT[:, k * SEG + n * MMW:k * SEG + (n + 1) * MMW],
                                  start=(k == 0), stop=(k == KB - 1),
                                  wait=waits, inc=(k == KB - 1 and n == NMM - 1))
                    t_w_end[(w, o)] = p.cnt["t"]
                par = o % 2
                A, B, C = slA[par], slB[par], slC[par]
                Dg, Eg, F = slD[par], slE[par], slF[par]
                tcd, tcs = slcd[par], slcs[par]
                t6 = t16s[par]
                wv_prev = [("v", chain_v_end.get(o - 2, 0))]
                if o < 2:
                    wv_prev += [("v", v_hT_done), ("t", t_ph1_end),
                                ("s", s_ph1_end)]
                s1 = p.sop("activation", A, ps_a[:], AF.Exp, scale=-1.0,
                           bias=c_nbias[:, o:o + 1],
                           wait=[("t", t_w_end[("a", o)])] + wv_prev)
                psum_free["a"] = ("s", s1)
                s2 = p.sop("activation", F, ps_i[:], AF.Exp, scale=-1.0,
                           wait=[("t", t_w_end[("i", o)])])
                psum_free["i"] = ("s", s2)
                p.sop("activation", B, A, AF.Ln, bias=1.0)
                s4 = p.sop("activation", C, B, AF.Exp, scale=-2.0)
                p.sop("activation", A, F, AF.Ln, bias=1.0)
                s6 = p.sop("activation", Dg, A, AF.Exp, scale=-1.0)
                p.vop("tensor_scalar", Eg, C, -1.0, 1.0, ALU.mult,
                      ALU.add, wait=[("s", s4)])
                v_y = p.vop("tensor_scalar_max", Eg, Eg, 1e-8)
                p.sop("activation", C, Eg, AF.Ln, wait=[("v", v_y)])
                s_sq = p.sop("activation", A, C, AF.Exp, scale=0.5)
                p.vop("tensor_mul", Eg, A, Dg,
                      wait=[("s", max(s_sq, s6))])
                v_v = p.vop("tensor_mul", Eg, Eg, ps_v[:],
                            wait=[("t", t_w_end[("v", o)])])
                if o == KB - 1:
                    v_ps_free, s_ps_free = v_v, s2
                psum_free["v"] = ("v", v_v)
                p.vop("tensor_scalar_min", F, B, NLA_CLIP)
                v_scan = p.vop("tensor_tensor_scan", C, c_mask[:],
                               F, 0.0, ALU.mult, ALU.add,
                               wait=[("s", s_sq)])
                s_cd = p.sop("activation", tcd, C, AF.Exp,
                             scale=-1.0, wait=[("v", v_scan)])
                p.vop("tensor_scalar_min", F, C, NLA_CLIP)
                s_rinv = p.sop("activation", A, F, AF.Exp,
                               wait=[("v", p.cnt["v"])])
                p.vop("tensor_mul", Eg, Eg, A,
                      wait=[("s", s_rinv)])
                p.vop("tensor_tensor_scan", tcs, c_mask[:],
                      Eg, 0.0, ALU.mult, ALU.add)
                p.vop("tensor_copy", t6[0], tcd[:, CH - 1:SEG:CH],
                      wait=[("s", s_cd)])
                p.vop("tensor_copy", t6[1], tcs[:, CH - 1:SEG:CH])
                p.vop("tensor_mul", t6[2], t6[0], t6[1])
                p.vop("tensor_tensor_scan", t6[3], t6[0], t6[2], 0.0,
                      ALU.mult, ALU.add)
                p.vop("tensor_tensor_scan", t6[4], t6[0], z16[:], 1.0,
                      ALU.mult, ALU.add)
                p.vop("tensor_copy", LP[:, o:o + 1], t6[3][:, NCH - 1:NCH])
                v_lp_at[o] = p.vop("tensor_copy", LP[:, KB + o:KB + o + 1],
                                   t6[4][:, NCH - 1:NCH])
                p.vop("memset", t6[5][:, 0:1], 0.0)
                if NCH > 1:
                    p.vop("tensor_copy", t6[5][:, 1:NCH], t6[3][:, 0:NCH - 1])
                sin_b = t6[5].unsqueeze(2).broadcast_to([128, NCH, CH])
                p.vop("tensor_tensor",
                      Eg.rearrange("p (c k) -> p c k", k=CH),
                      tcs.rearrange("p (c k) -> p c k", k=CH), sin_b,
                      ALU.add)
                v_state_w[o] = p.vop(
                    "tensor_mul", states[:, o * SEG:(o + 1) * SEG], Eg,
                    tcd,
                    wait=[("s", s_ph1_end), ("t", t_ph1_end)])
                p.vop("memset", t6[5][:, 0:1], 1.0)
                if NCH > 1:
                    p.vop("tensor_copy", t6[5][:, 1:NCH], t6[4][:, 0:NCH - 1])
                pin_b = t6[5].unsqueeze(2).broadcast_to([128, NCH, CH])
                v_fb_w[o] = p.vop(
                    "tensor_tensor",
                    Fb[:, o * SEG:(o + 1) * SEG].rearrange(
                        "p (c k) -> p c k", k=CH),
                    tcd.rearrange("p (c k) -> p c k", k=CH), pin_b,
                    ALU.mult)
                chain_v_end[o] = p.cnt["v"]
            v_scan_end = p.cnt["v"]
            s_scan_end = p.cnt["s"]
            LPg = LP[:].rearrange("p (h o) -> p h o", h=2)
            wo_chunk = []
            for k in range(KB):
                if k < 6:
                    off = 16384 + 2048 * k
                elif k < 10:
                    off = 28672 + 2048 * (k - 6)
                elif k < 12:
                    off = 8192 + 2048 * (k - 10)
                else:
                    off = 2048 * (k - 12)
                wo_chunk.append(arenaB[:, off:off + 2048])
            ldcc_a = p.dma("gpsimd", "dcc", cc_in_a.ap(), LPg[:, :, 0:KSPL],
                           wait=[("v", v_lp_at[KSPL - 1])])
            p.wait("gpsimd", ldcc_a)
            p.op("gpsimd", "collective_compute", "AllGather", ALU.bypass,
                 [[r for r in range(NCORES)]],
                 ins=[cc_in_a.ap().opt()], outs=[cc_out_a.ap().opt()],
                 inc="cc", inc_k=1)
            ldwo = {}
            for k in range(6):
                ldwo[k] = p.dma("gpsimd", "dwo", wo_chunk[k],
                                w_o_r[k * 128:(k + 1) * 128, :],
                                wait=[("t", t_w_end[("v", KB - 1)])])
            ldcc_b = p.dma("gpsimd", "dcc", cc_in_b.ap(), LPg[:, :, KSPL:KB],
                           wait=[("v", v_lp_at[KB - 1])])
            p.wait("gpsimd", ldcc_b)
            p.op("gpsimd", "collective_compute", "AllGather", ALU.bypass,
                 [[r for r in range(NCORES)]],
                 ins=[cc_in_b.ap().opt()], outs=[cc_out_b.ap().opt()],
                 inc="cc", inc_k=1)
            ldg_a = p.dma("gpsimd", "dcc", gath[:, :, 0:2 * KSPL],
                          cc_out_a.ap().rearrange("(r p) f -> p r f",
                                                  r=NCORES), wait=[("cc", 1)])
            fix_w = {}

            def carry_chain(o0, o1, g_states, g_prods, wld):
                nn = o1 - o0
                rb = c_rmask[:].unsqueeze(2).broadcast_to([128, NCORES, nn])
                p.gop("tensor_tensor", cr_t1[:, :, o0:o1], g_states, rb,
                      ALU.mult, wait=[wld])
                p.gop("tensor_scalar_add", cr_t2[:, :, o0:o1], g_prods, -1.0)
                p.gop("tensor_tensor", cr_t2[:, :, o0:o1], cr_t2[:, :, o0:o1],
                      rb, ALU.mult)
                p.gop("tensor_scalar_add", cr_t2[:, :, o0:o1],
                      cr_t2[:, :, o0:o1], 1.0)
                p.gop("memset", carry[:, o0:o1], 0.0)
                for r in range(NCORES):
                    p.gop("tensor_mul", cr_tmp[:, o0:o1], cr_t2[:, r, o0:o1],
                          carry[:, o0:o1])
                    p.gop("tensor_add", carry[:, o0:o1], cr_tmp[:, o0:o1],
                          cr_t1[:, r, o0:o1])
                return p.cnt["g"]

            def fix_gp(o):
                cb = carry[:, o:o + 1].broadcast_to([128, SEG])
                p.gop("tensor_tensor", fix_tmp[:],
                      Fb[:, o * SEG:(o + 1) * SEG], cb, ALU.mult,
                      wait=[("v", v_fb_w[o])])
                fix_w[o] = ("g", p.gop(
                    "tensor_add", states[:, o * SEG:(o + 1) * SEG],
                    states[:, o * SEG:(o + 1) * SEG], fix_tmp[:],
                    wait=[("v", v_state_w[o])]))

            def fix_vec(o, g_carry):
                fix_w[o] = ("v", p.vop(
                    "scalar_tensor_tensor",
                    states[:, o * SEG:(o + 1) * SEG],
                    Fb[:, o * SEG:(o + 1) * SEG], carry[:, o:o + 1],
                    states[:, o * SEG:(o + 1) * SEG], ALU.mult, ALU.add,
                    wait=[("g", g_carry)]))

            g_carry_a = carry_chain(0, KSPL, gath[:, :, 0:KSPL],
                                    gath[:, :, KSPL:2 * KSPL], ldg_a)
            for o in range(6):
                fix_gp(o)
            for o in range(6, KSPL):
                fix_vec(o, g_carry_a)
            for k in range(6, 10):
                ldwo[k] = p.dma("gpsimd", "dwo", wo_chunk[k],
                                w_o_r[k * 128:(k + 1) * 128, :],
                                wait=[("v", v_scan_end), ("s", s_scan_end)])
            for k in range(10, 12):
                ldwo[k] = p.dma("gpsimd", "dwo", wo_chunk[k],
                                w_o_r[k * 128:(k + 1) * 128, :],
                                wait=[fix_w[8 + 2 * (k - 10)],
                                      fix_w[9 + 2 * (k - 10)]])
            for k in range(12, KB):
                ldwo[k] = p.dma("gpsimd", "dwo", wo_chunk[k],
                                w_o_r[k * 128:(k + 1) * 128, :],
                                wait=[fix_w[2 * (k - 12)],
                                      fix_w[2 * (k - 12) + 1]])
            ldg_b = p.dma("gpsimd", "dcc", gath[:, :, 2 * KSPL:2 * KB],
                          cc_out_b.ap().rearrange("(r p) f -> p r f",
                                                  r=NCORES), wait=[("cc", 2)])
            g_carry_b = carry_chain(KSPL, KB, gath[:, :, 2 * KSPL:KSPL + KB],
                                    gath[:, :, KSPL + KB:2 * KB], ldg_b)
            for o in range(KSPL, KB):
                fix_vec(o, g_carry_b)
            g_fix_end = p.cnt["g"]
            v_fix_end = p.cnt["v"]

        h2T = hT
        with ExitStack() as ph4:
            E4 = ph4.enter_context
            ps_o = [E4(nc.psum_tensor("ps_o%d" % i, [128, 512], F32))
                    for i in range(NB)]
            ps_tr = [E4(nc.psum_tensor("ps_tr4%d" % i, [128, 512], BF16))
                     for i in range(2)]
            scr4 = [scr_big[:], arenaF[:, 18432:20480]]
            xa4 = [xa_t[:], arenaF[:, 6144:8192]]
            hb4 = [hb[:], scan_x[:].bitcast(BF16)]
            stat4 = [stat[:], stat4b[:]]
            t_scan_mm = t_w_end[("v", KB - 1)]
            st4 = {}

            def mm4(m):
                par = m % 2
                tmm = {}
                for k in range(KB):
                    for n in range(NB):
                        waits = []
                        if n == 0 and m == 0:
                            waits = [ldwo[k], fix_w[k]]
                            if k == 0:
                                waits += [("s", s_ps_free), ("v", v_ps_free)]
                        elif k == 0 and m > 0:
                            waits = [("v", st4[m - 1]["evn"][n])]
                        p.top("matmul", ps_o[n][:],
                              states[:, k * SEG + m * TB:k * SEG + (m + 1) * TB],
                              wo_chunk[k][:, n * 512:(n + 1) * 512],
                              start=(k == 0), stop=(k == KB - 1),
                              wait=waits,
                              inc=(k == KB - 1))
                        if k == KB - 1:
                            tmm[n] = p.cnt["t"]
                st4.setdefault(m, {})["tmm"] = tmm

            def lh4(m):
                par = m % 2
                prev = st4.get(m - 2)
                w_ldx = [("v", v_scan_end), ("s", s_scan_end)]
                if prev:
                    w_ldx = [("v", prev["v_ln1"]), ("s", prev["s_sq"])]
                ldx = p.dma("sync", "dx", x0_t[par][:],
                            x_in[m * TB:(m + 1) * TB, :], wait=w_ldx)
                w_st = [ldx]
                if prev:
                    w_st += [("v", prev["v_hb"]), ("t", prev["t_tr"])]
                s_r = stats_full(x0_t[par][:], hb4[par], eps_ln, True,
                                 wait=w_st, st=stat4[par])
                st4.setdefault(m, {})["v_ln1"] = p.vop(
                    "tensor_scalar", scr4[par], x0_t[par][:],
                    stat4[par][:, 2:3], stat4[par][:, 4:5],
                    ALU.subtract, ALU.mult, wait=[("s", s_r)])

            def ev4(m):
                par = m % 2
                prev = st4.get(m - 2)
                evn = {}
                for n in range(NB):
                    w_ev = [("t", st4[m]["tmm"][n])]
                    if m == 0:
                        w_ev.append(("s", s_scan_end))
                    if m == 1:
                        w_ev += [("g", g_fix_end), ("v", v_fix_end)]
                    if prev:
                        w_ev += [prev["sp_xa"], ("s", prev["s_sq"])]
                    p.vop("tensor_copy", xa4[par][:, n * 512:(n + 1) * 512],
                          ps_o[n][:], wait=w_ev)
                    evn[n] = p.cnt["v"]
                st4[m]["evn"] = evn
                p.vop("tensor_add", xa4[par], xa4[par], scr4[par])

            def tail4(m):
                par = m % 2
                s_sq = p.sop("activation", x0_t[par][:], xa4[par], AF.Square,
                             accum_out=stat4[par][:, 1:2],
                             wait=[("v", p.cnt["v"])])
                st4[m]["s_sq"] = s_sq
                vv = p.vop("tensor_scalar", stat4[par][:, 6:7],
                           stat4[par][:, 1:2], 1.0 / D, eps_rms, ALU.mult,
                           ALU.add, wait=[("s", s_sq)])
                p.sop("activation", stat4[par][:, 5:6], stat4[par][:, 6:7],
                      AF.Ln, wait=[("v", vv)])
                s_r2 = p.sop("activation", stat4[par][:, 4:5],
                             stat4[par][:, 5:6], AF.Exp, scale=-0.5)
                v_hb = p.vop("tensor_scalar_mul", hb4[par], xa4[par],
                             stat4[par][:, 4:5], wait=[("s", s_r2)])
                st4[m]["v_hb"] = v_hb
                copy_cnt = {}
                for g in range(KB // 4):
                    pp = ps_tr[g % 2]
                    for j in range(4):
                        k = g * 4 + j
                        w = []
                        if j == 0:
                            if g < 2:
                                w = [("v", v_hb)]
                                if m > 0:
                                    w.append(("v", st4[m - 1]["tr_copies"]))
                            else:
                                w = [("v", copy_cnt[g - 2])]
                        p.top("transpose", pp[:, j * 128:(j + 1) * 128],
                              hb4[par][:, k * 128:(k + 1) * 128], c_id[:],
                              wait=w, inc=(j == 3))
                    wsl = c_wff[:, g * 4:g * 4 + 4].unsqueeze(2)                         .broadcast_to([128, 4, TB])
                    copy_cnt[g] = p.vop(
                        "tensor_tensor",
                        h2T[:].rearrange("p (k t) -> p k t", t=SEG)
                        [:, g * 4:g * 4 + 4, m * TB:(m + 1) * TB],
                        pp[:].rearrange("p (k t) -> p k t", t=TB),
                        wsl, ALU.mult,
                        wait=[("t", p.cnt["t"]), ("t", t_scan_mm)])
                st4[m]["tr_copies"] = copy_cnt[KB // 4 - 1]
                st4[m]["t_tr"] = p.cnt["t"]
                st4[m]["sp_xa"] = p.dma("gpsimd", "dxa",
                                        xa_sp.ap()[m * TB:(m + 1) * TB, :],
                                        xa4[par], wait=[("v", v_hb)])

            mm4(0)
            lh4(0)
            for m in range(NTB):
                ev4(m)
                if m + 1 < NTB:
                    lh4(m + 1)
                    mm4(m + 1)
                tail4(m)

        def gu_down_phase(tag, srcT, acc_ap, gate_src, up_src, dn_rhs_dma,
                          fuse_buf, dn_ring, t_silu, wg_t, wu_t,
                          sg_col=None, su_col=None):
            v_src_done = p.cnt["v"]
            t_entry = p.cnt["t"]
            s_entry = p.cnt["s"]
            dx1_entry = [("dx1s0", p.cnt["dx1s0"]),
                         ("dx1s1", p.cnt["dx1s1"])]
            ev_bank = {m: 0 for m in range(NTB)}
            ev_all = 0
            evac_final = [0] * NTB
            t_gu_last = t_entry
            lastpar = {0: t_entry, 1: t_entry}
            lastring = {}
            for q in range(NQ):
                with ExitStack() as pq:
                    ps_g = [pq.enter_context(
                        nc.psum_tensor("ps_g%s_%d_%d" % (tag, q, i),
                                       [128, SEG], F32)) for i in range(2)]
                    ps_u = [pq.enter_context(
                        nc.psum_tensor("ps_u%s_%d_%d" % (tag, q, i),
                                       [128, SEG], F32)) for i in range(2)]
                    sgv = {}
                    fu = {}
                    wl = {}

                    def load_pair(oi, q=q, wl=wl):
                        o = q * OHQ + oi
                        par = oi % 2
                        tw = [("t", lastpar[par])]
                        if q == 0 and oi == 0:
                            tw = [("t", t_entry), ("v", v_src_done),
                                  ("s", s_entry)]
                        a_ = p.dma("gpsimd", "wg", wg_t[par],
                                   gate_src[:, o, :], wait=tw)
                        b_ = p.dma("gpsimd", "wu", wu_t[par], up_src[:, o, :])
                        wl[oi] = (a_, b_)

                    load_pair(0)
                    for oi in range(OHQ):
                        o = q * OHQ + oi
                        par = oi % 2
                        if oi + 1 < OHQ:
                            load_pair(oi + 1)
                        for (wt, ps, wld) in ((wg_t[par], ps_g[par], wl[oi][0]),
                                              (wu_t[par], ps_u[par], wl[oi][1])):
                            for k in range(KB):
                                for n in range(NMM):
                                    waits = []
                                    if n == 0 and k == 0:
                                        waits = [wld, ("v", v_src_done),
                                                 ("s", sgv.get(par, 0)),
                                                 ("v", max(fu.get(par, 0),
                                                           ev_all))]
                                    p.top("matmul",
                                          ps[:, n * MMW:(n + 1) * MMW],
                                          wt[:, k * 128:(k + 1) * 128],
                                          srcT[:, k * SEG + n * MMW:
                                               k * SEG + (n + 1) * MMW],
                                          start=(k == 0), stop=(k == KB - 1),
                                          wait=waits,
                                          inc=(k == KB - 1 and n == NMM - 1))
                        wl[("t", oi)] = p.cnt["t"]
                        lastpar[par] = p.cnt["t"]
                        silu_sl = t_silu[:, par * SEG:(par + 1) * SEG]
                        guard = ([("v", v_src_done)] + dx1_entry
                                 if (q == 0 and oi == 0) else [])
                        if sg_col is None:
                            s_g = p.sop("activation", silu_sl, ps_g[par][:],
                                        AF.Silu, wait=[("t", wl[("t", oi)]),
                                                       ("v", fu.get(par, 0))]
                                        + guard)
                            f_ = p.vop("tensor_mul",
                                       fuse_buf[:, oi * SEG:(oi + 1) * SEG],
                                       silu_sl, ps_u[par][:],
                                       wait=[("s", s_g)])
                        else:
                            s_g = p.sop("activation", silu_sl, ps_g[par][:],
                                        AF.Silu, scale=sg_col[:, o:o + 1],
                                        wait=[("t", wl[("t", oi)]),
                                              ("v", fu.get(par, 0))] + guard)
                            f_ = p.vop("scalar_tensor_tensor",
                                       fuse_buf[:, oi * SEG:(oi + 1) * SEG],
                                       ps_u[par][:], su_col[:, o:o + 1],
                                       silu_sl, ALU.mult, ALU.mult,
                                       wait=[("s", s_g)])
                        sgv[par] = s_g
                        fu[par] = f_
                    v_fuse_done = p.cnt["v"]
                    s_last = p.cnt["s"]
                    if q == NQ - 1:
                        t_gu_last = p.cnt["t"]
                with ExitStack() as pd:
                    ps_d = [pd.enter_context(
                        nc.psum_tensor("ps_d%s_%d_%d" % (tag, q, i),
                                       [128, 512], F32))
                        for i in range(min(8, NTB))]
                    for n in range(NB):
                        tm = {}
                        for k in range(KHQ):
                            r = k % len(dn_ring)
                            ring = dn_ring[r]
                            tw = [("t", lastring.get(r, t_entry))]
                            ldd = dn_rhs_dma(q, n, k, ring, tw)
                            for m in range(NTB):
                                waits = []
                                if m == 0:
                                    waits = [ldd]
                                if k == 0:
                                    waits = list(waits) + [("v", ev_bank[m])]
                                    if n == 0:
                                        waits += [("v", v_fuse_done),
                                                  ("s", s_last)]
                                p.top("matmul", ps_d[m % 8][:],
                                      fuse_buf[:, k * SEG + m * TB:
                                               k * SEG + (m + 1) * TB],
                                      ring,
                                      start=(k == 0), stop=(k == KHQ - 1),
                                      wait=waits,
                                      inc=(m == NTB - 1 or k == KHQ - 1))
                                if k == KHQ - 1:
                                    tm[m] = p.cnt["t"]
                            lastring[r] = p.cnt["t"]
                        for m in range(NTB):
                            dst = acc_ap[:, m * D + n * 512:m * D + (n + 1) * 512]
                            if q == 0:
                                p.vop("tensor_copy", dst, ps_d[m % 8][:],
                                      wait=[("t", tm[m]),
                                            ("dxa0", 16 * (NTB // 2)),
                                            ("dxa1", 16 * (NTB // 2))])
                            else:
                                p.vop("tensor_add", dst, dst, ps_d[m % 8][:],
                                      wait=[("t", tm[m])])
                            ev_bank[m] = p.cnt["v"]
                            if q == NQ - 1 and n == NB - 1:
                                evac_final[m] = p.cnt["v"]
                    ev_all = max(ev_bank.values())
            return dict(ev_all=ev_all, evac_final=evac_final,
                        t_gu_last=t_gu_last)

        with ExitStack() as ph5:
            E5 = ph5.enter_context
            assert OHQ * SEG == KB * SEG
            fuse5 = states[:, 0:OHQ * SEG]
            fuse7 = hT[:, 0:OHQ * SEG]
            acc = arenaF[:, 0:NTB * D]
            t_silu = scr_big[:]
            KW = KB * 128
            x0b = [x0_t[i][:].bitcast(BF16) for i in range(2)]
            wg_t = [x0b[0][:, 0:KW], x0b[0][:, KW:2 * KW]]
            wu_t = [x0b[1][:, 0:KW], x0b[1][:, KW:2 * KW]]
            nring = max(1, min(4, D // 512))
            dn_ring = [hb[:, r * 512:(r + 1) * 512] for r in range(nring)]

            def dn_dma(qq, n, k, ring, tw):
                row0 = qq * (FFN // NQ) + k * 128
                return p.dma("gpsimd", "dn", ring,
                             w_dn_r[row0:row0 + 128, n * 512:(n + 1) * 512],
                             wait=tw)

            info5 = gu_down_phase('5', h2T, acc, w_gu_l[:, 0:OBF, :],
                                  w_gu_l[:, OBF:2 * OBF, :], dn_dma,
                                  fuse5, dn_ring, t_silu, wg_t, wu_t)

            ln2T = states
            t_p5_end = p.cnt["t"]
            s_p5_end = p.cnt["s"]
            t_gu5 = info5["t_gu_last"]
            ev5 = info5["evac_final"]
            ev5_all = info5["ev_all"]
            hTf = hT[:].bitcast(F32)
            scrb16 = scr_big[:].bitcast(BF16)
            scr6 = [hTf[:, 0:2048], hTf[:, 2048:4096]]
            xat6 = [hTf[:, 4096:6144], xa_t[:]]
            stat6 = [stat[:], hTf[:, 6144:6152]]
            hb6 = [scrb16[:, 0:2048], scrb16[:, 2048:4096]]
            dump6 = [arenaF[:, 16384:18432], arenaF[:, 18432:20480]]
            with ExitStack() as ph6:
                ps_tr6 = [ph6.enter_context(
                    nc.psum_tensor("ps_tr6%d" % i, [128, 512], BF16))
                    for i in range(2)]
                st6 = {}

                def S6_0(m):
                    par = m % 2
                    buf = x0_t[par]
                    prev = st6.get(m - 2)
                    w = [("t", t_gu5),
                         ("dxa%d" % (m % 2), 16 * (m // 2 + 1))]
                    if prev:
                        w += [("g", prev["g_x1"])]
                    ldxa = p.dma("sync", "dx", buf[:],
                                 xa_sp.ap()[m * TB:(m + 1) * TB, :], wait=w)
                    w2 = [ldxa, ("v", ev5[m]), ("t", t_gu5)]
                    if prev:
                        w2 += [("s", prev["vhb_s"]), prev["sp_x1"]]
                    g_ao = p.gop("tensor_add", scr6[par], buf[:],
                                 acc[:, m * D:(m + 1) * D], wait=w2)
                    st6[m] = dict(g_ao=g_ao)

                def S6_1(m):
                    par = m % 2
                    w = [("g", st6[m]["g_ao"])]
                    st6[m]["s_r"] = stats_full(scr6[par], dump6[par],
                                               eps_ln, True, wait=w,
                                               st=stat6[par])
                    st6[m]["ldx0"] = p.dma("sync", "dx", x0_t[par][:],
                                           x_in[m * TB:(m + 1) * TB, :],
                                           wait=[("g", st6[m]["g_ao"])])

                def S6_2(m):
                    par = m % 2
                    s_r = st6[m]["s_r"]
                    nmr = p.vop("tensor_scalar", stat6[par][:, 7:8],
                                stat6[par][:, 2:3], stat6[par][:, 4:5],
                                -1.0, ALU.mult, ALU.mult, wait=[("s", s_r)])
                    st6[m]["s62"] = p.sop(
                        "activation", xat6[par], scr6[par], AF.Identity,
                        scale=stat6[par][:, 4:5], bias=stat6[par][:, 7:8],
                        wait=[("v", nmr), ("dxa0", 16 * (NTB // 2)),
                              ("dxa1", 16 * (NTB // 2))])

                def S6_3(m):
                    par = m % 2
                    g_x1 = p.gop("tensor_add", scr6[par], xat6[par],
                                 x0_t[par][:],
                                 wait=[st6[m]["ldx0"], ("s", st6[m]["s62"])])
                    st6[m]["g_x1"] = g_x1
                    st6[m]["sp_x1"] = p.dma("gpsimd", "dx1s",
                                            x1_sp.ap()[m * TB:(m + 1) * TB, :],
                                            scr6[par], wait=[("g", g_x1)])
                    st6[m]["s_r2"] = stats_full(scr6[par], xat6[par], eps_ln,
                                                True, wait=[("g", g_x1)],
                                                st=stat6[par])

                def S6_4(m):
                    par = m % 2
                    prev = st6.get(m - 2)
                    nmr2 = p.vop("tensor_scalar", stat6[par][:, 7:8],
                                 stat6[par][:, 2:3], stat6[par][:, 4:5],
                                 -1.0, ALU.mult, ALU.mult,
                                 wait=[("s", st6[m]["s_r2"])])
                    w = [("v", nmr2), ("s", s_p5_end)]
                    if prev:
                        w.append(("t", prev["t_tr"]))
                    st6[m]["vhb_s"] = p.sop(
                        "activation", hb6[par], scr6[par], AF.Identity,
                        scale=stat6[par][:, 4:5], bias=stat6[par][:, 7:8],
                        wait=w)

                def S6_5(m):
                    par = m % 2
                    copy_cnt = {}
                    for g in range(KB // 4):
                        pp = ps_tr6[g % 2]
                        for j in range(4):
                            k = g * 4 + j
                            w = []
                            if j == 0:
                                if g < 2:
                                    w = [("s", st6[m]["vhb_s"])]
                                    if m == 0:
                                        w.append(("v", ev5_all))
                                    else:
                                        w.append(("v",
                                                  st6[m - 1]["tr_copies"]))
                                else:
                                    w = [("v", copy_cnt[g - 2])]
                            p.top("transpose", pp[:, j * 128:(j + 1) * 128],
                                  hb6[par][:, k * 128:(k + 1) * 128],
                                  c_id[:], wait=w, inc=(j == 3))
                        copy_cnt[g] = p.vop(
                            "tensor_copy",
                            ln2T[:].rearrange("p (k t) -> p k t", t=SEG)
                            [:, g * 4:g * 4 + 4, m * TB:(m + 1) * TB],
                            pp[:].rearrange("p (k t) -> p k t", t=TB),
                            wait=[("t", p.cnt["t"]), ("t", t_p5_end)])
                    st6[m]["tr_copies"] = copy_cnt[KB // 4 - 1]
                    st6[m]["t_tr"] = p.cnt["t"]

                stages6 = [S6_0, S6_1, S6_2, S6_3, S6_4, S6_5]
                for m0 in range(0, NTB, 2):
                    for sfn in stages6:
                        sfn(m0)
                        sfn(m0 + 1)

            def dn8_dma(qq, n, k, ring, tw):
                row0 = qq * (FFN // NQ) + k * 128
                return p.dma("gpsimd", "dn", ring,
                             wd8_r[n, row0:row0 + 128, :], wait=tw)

            info7 = gu_down_phase('7', ln2T, acc, wg8_l, wu8_l, dn8_dma,
                                  fuse7, dn_ring, t_silu, wg_t, wu_t,
                                  sg_col=c_sg[:], su_col=c_su[:])

            t_p7_end = p.cnt["t"]
            t_gu7 = info7["t_gu_last"]
            ev7 = info7["evac_final"]
            statesf = states[:].bitcast(F32)
            scr8 = [hTf[:, 0:2048], hTf[:, 2048:4096]]
            xat8 = [hTf[:, 4096:6144], xa_t[:]]
            stat8 = [stat[:], hTf[:, 6144:6152]]
            dump8 = [statesf[:, 0:2048], statesf[:, 2048:4096]]
            st8 = {}

            def S8_0(m):
                par = m % 2
                prev = st8.get(m - 2)
                w = [("t", t_gu7),
                     ("dx1s%d" % (m % 2), 16 * (m // 2 + 1))]
                if prev:
                    w.append(("v", prev["v_o"]))
                ldx1 = p.dma("sync", "dx", x0_t[par][:],
                             x1_sp.ap()[m * TB:(m + 1) * TB, :], wait=w)
                w2 = [("v", ev7[m]), ld_sdb, ("t", t_p7_end)]
                if prev:
                    w2 += [("s", prev["s_end"]), prev["dy"]]
                v_y3 = p.vop("tensor_mul", scr8[par],
                             acc[:, m * D:(m + 1) * D], sdb[:], wait=w2)
                st8[m] = dict(ldx1=ldx1, v_y3=v_y3)

            def S8_1(m):
                par = m % 2
                st8[m]["s_r"] = stats_full(scr8[par], dump8[par], eps_ln,
                                           True, wait=[("v", st8[m]["v_y3"])],
                                           st=stat8[par])
                st8[m]["s_end"] = p.cnt["s"]

            def S8_2(m):
                par = m % 2
                p.vop("tensor_scalar", xat8[par], scr8[par],
                      stat8[par][:, 2:3], stat8[par][:, 4:5],
                      ALU.subtract, ALU.mult, wait=[("s", st8[m]["s_r"])])

            def S8_3(m):
                par = m % 2
                v_o = p.vop("tensor_add", scr8[par], xat8[par], x0_t[par][:],
                            wait=[st8[m]["ldx1"]])
                st8[m]["v_o"] = v_o
                st8[m]["dy"] = p.dma("sync", "dy",
                                     y_out[m * TB:(m + 1) * TB, :],
                                     scr8[par], wait=[("v", v_o)])

            stages8 = [S8_0, S8_1, S8_2, S8_3]
            for m0 in range(0, NTB, 2):
                for sfn in stages8:
                    sfn(m0)
                    sfn(m0 + 1)
            p.wait("sync", st8[NTB - 2]["dy"], st8[NTB - 1]["dy"])

        with nc.Block() as block:
            p.finalize(block)

    return nc


def _lhsT_layout(W, dtype):
    Din, Dout = W.shape
    KBl, OBl = Din // 128, Dout // 128
    return np.ascontiguousarray(
        W.reshape(KBl, 128, OBl, 128).transpose(1, 2, 0, 3)
        .reshape(128, OBl, KBl * 128)).astype(dtype, copy=False)


def _colT(vec):
    Dl = vec.shape[0]
    return np.ascontiguousarray(vec.reshape(Dl // 128, 128).T)


def make_inputs(cfg, fi):
    D, FFN, SEG, NSEG, Bb = (cfg[k] for k in ("D", "FFN", "SEG", "NSEG", "B"))
    NCORES = Bb * NSEG
    NB = D // 512
    x = np.asarray(fi["L0_attn_x"], np.float32)

    shared = {
        "w_a_l": _lhsT_layout(np.asarray(fi["L0_attn_w_a"], np.float32), np.float32),
        "w_i_l": _lhsT_layout(np.asarray(fi["L0_attn_w_i"], np.float32), np.float32),
        "w_v_l": _lhsT_layout(np.asarray(fi["L0_attn_w_v"], np.float32), np.float32),
        "w_o_r": np.ascontiguousarray(np.asarray(fi["L0_attn_out_proj_w"], np.float32)),
        "w_gu_l": _lhsT_layout(np.asarray(fi["L0_attn_w_gate_up"], np.float32),
                               np.float32),
        "w_dn_r": np.ascontiguousarray(np.asarray(fi["L0_attn_w_down"], np.float32)),
        "wg8_l": _lhsT_layout(np.asarray(fi["L0_ffn_w_gate"], np.int8), np.int8),
        "wu8_l": _lhsT_layout(np.asarray(fi["L0_ffn_w_up"], np.int8), np.int8),
        "wd8_r": np.ascontiguousarray(
            np.asarray(fi["L0_ffn_w_down"], np.int8).reshape(FFN, NB, 512)
            .transpose(1, 0, 2)),
        "biasT": _colT(np.asarray(fi["L0_attn_decay_bias"], np.float32)),
        "wpreT": _colT(np.asarray(fi["L0_attn_pre_norm_w"], np.float32)),
        "wffT": _colT(np.asarray(fi["L0_attn_ffn_norm_w"], np.float32)),
        "sgT": _colT(np.asarray(fi["L0_ffn_s_gate"], np.float32)),
        "suT": _colT(np.asarray(fi["L0_ffn_s_up"], np.float32)),
        "sd_row": np.ascontiguousarray(
            np.asarray(fi["L0_ffn_s_down"], np.float32)[None, :]),
        "identity": np.eye(128, dtype=BF),
        "ones_row": np.ones((1, 128), BF),
    }
    scanmask = np.ones((128, SEG), np.float32)
    scanmask[:, ::CH] = 0.0
    shared["scanmask"] = scanmask

    in_maps = []
    for c in range(NCORES):
        b, s0 = divmod(c, NSEG)
        rmask = np.zeros((128, NCORES), np.float32)
        for r in range(NCORES):
            if r // NSEG == b and (r % NSEG) < s0:
                rmask[:, r] = 1.0
        m = dict(shared)
        m["x"] = np.ascontiguousarray(x[b, s0 * SEG:(s0 + 1) * SEG, :])
        m["rankmask"] = rmask
        in_maps.append(m)
    return in_maps


def kernel(**inputs):
    cfg = FULL
    NCORES = cfg["B"] * cfg["NSEG"]
    nc = build_nc(cfg)
    in_maps = make_inputs(cfg, inputs)
    res = run_bass_kernel_spmd(nc, in_maps, core_ids=list(range(NCORES)))
    SEG, D = cfg["SEG"], cfg["D"]
    out = np.empty((cfg["B"], cfg["NSEG"] * SEG, D), np.float32)
    for c in range(NCORES):
        b, s0 = divmod(c, cfg["NSEG"])
        out[b, s0 * SEG:(s0 + 1) * SEG, :] = res.results[c]["y"]
    return out
